# version 01
# baseline (speedup 1.0000x reference)
import numpy as np
import ml_dtypes
from contextlib import ExitStack
import concourse.bass as bass
import concourse.mybir as mybir
from concourse.bass_utils import run_bass_kernel_spmd
from concourse.bass_core_rust import IndirectOffsetOnAxis

F32 = mybir.dt.float32
BF16 = mybir.dt.bfloat16
U8 = mybir.dt.uint8
I32 = mybir.dt.int32
AF = mybir.ActivationFunctionType
ALU = mybir.AluOpType
AX = mybir.AxisListType

NCORES = 8
T = 2048
D = 1024
NT = D // 128


class Prog:

    ENG = ("sync", "gpsimd", "tensor", "vector", "scalar")
    DMA_Q = ("sync", "gpsimd")
    NSLOT = 8

    def __init__(self, nc):
        self.nc = nc
        self.es = ExitStack()
        self.ops = []
        self.counts = {e: 0 for e in self.ENG}
        self.sems = {e: self.es.enter_context(nc.semaphore(f"sem_{e}")) for e in self.ENG}
        for q in self.DMA_Q:
            for s in range(self.NSLOT):
                self.sems[f"{q}_s{s}"] = self.es.enter_context(nc.semaphore(f"sem_{q}_s{s}"))
                self.counts[f"{q}_s{s}"] = 0
        self.dma_idx = {q: 0 for q in self.DMA_Q}
        self._names = {}

    def sbuf(self, base, shape, dtype):
        return self.es.enter_context(self.nc.sbuf_tensor(self.name(base), shape, dtype))

    def psum(self, base, shape, dtype=F32):
        return self.es.enter_context(self.nc.psum_tensor(self.name(base), shape, dtype))

    def name(self, base):
        i = self._names.get(base, 0)
        self._names[base] = i + 1
        return f"{base}_{i}"

    def op(self, engine, fn, waits=(), dma=False):
        ws = [w for w in waits if w is not None]
        if dma:
            assert engine in self.DMA_Q
            q = engine
            slot = f"{q}_s{self.dma_idx[q] % self.NSLOT}"
            self.dma_idx[q] += 1
            if self.counts[slot] > 0:
                ws.append((slot, self.counts[slot]))
            self.counts[slot] += 16
            tok = (slot, self.counts[slot])
            self.ops.append((engine, fn, tuple(ws), 16, slot))
            return tok
        self.counts[engine] += 1
        tok = (engine, self.counts[engine])
        self.ops.append((engine, fn, tuple(ws), 1, engine))
        return tok

    def finalize(self):
        nc = self.nc
        with nc.Block() as block:
            for eng_name in self.ENG:
                ops = [(f, w, a, s) for (e, f, w, a, s) in self.ops if e == eng_name]
                if not ops:
                    continue

                def make(eng_name_, _ops):
                    def sec(eng):
                        seen = {}
                        for f, waits, amount, sem in _ops:
                            for (we, wv) in waits:
                                if seen.get(we, 0) >= wv:
                                    continue
                                seen[we] = wv
                                eng.wait_ge(self.sems[we], wv)
                            last = f(eng)
                            last.then_inc(self.sems[sem], amount)
                    return sec

                getattr(block, eng_name)(make(eng_name, ops))
        self.es.close()


def emit_ln_ttile(p, bufs, x_tiles, tsl, out_tiles, osl, dep, gain_col=None,
                  bias_col=None, eps=1e-5, tag="ln", ps_override=None):
    W = tsl.stop - tsl.start
    onescol_f = bufs["ones_col_f"]
    onesrow_f = bufs["ones_row_f"]
    if ps_override is None:
        sxp = bufs["ps_row0"][:, 0:W]
        sxxp = bufs["ps_row1"][:, 0:W]
        rb_f = bufs["ps_rb"][:, 0:W]
        mrb_f = bufs["ps_mrb"][:, 0:W]
    else:
        sxp, sxxp, rb_f, mrb_f = ps_override
    sh = bufs["ln_shared"]
    x2c = sh["x2"]
    mrow, rrow, mrrow, tmp = sh["m"], sh["r"], sh["mr"], sh["t"]

    def mm_sx(t):
        for g in range(NT):
            last = t.matmul(sxp, onescol_f[:, :], x_tiles[g][:, tsl],
                            start=(g == 0), stop=(g == NT - 1))
        return last
    t_sx = p.op("tensor", mm_sx, waits=dep)

    sq_toks = []
    mm_toks = [t_sx]
    for g in range(NT):
        tq = p.op("scalar", lambda s, g=g: s.activation(x2c[:, 0:W], x_tiles[g][:, tsl], AF.Square),
                  waits=list(dep) + mm_toks[-1:])
        tm = p.op("tensor", lambda t, g=g: t.matmul(sxxp, onescol_f[:, :], x2c[:, 0:W],
                                                    start=(g == 0), stop=(g == NT - 1)), waits=[tq])
        mm_toks.append(tm)
        sq_toks.append(tq)

    tkm = p.op("vector", lambda v: v.tensor_scalar_mul(mrow[:, 0:W], sxp, 1.0 / 1024.0), waits=[t_sx])
    tke = p.op("vector", lambda v: v.tensor_scalar_mul(tmp[:, 0:W], sxxp, 1.0 / 1024.0), waits=[mm_toks[-1]])
    t1 = p.op("vector", lambda v: v.tensor_mul(rrow[:, 0:W], mrow[:, 0:W], mrow[:, 0:W]), waits=[tkm])
    t2 = p.op("vector", lambda v: v.tensor_sub(tmp[:, 0:W], tmp[:, 0:W], rrow[:, 0:W]), waits=[tke, t1])
    t2b = p.op("vector", lambda v: v.tensor_scalar_add(tmp[:, 0:W], tmp[:, 0:W], float(eps)), waits=[t2])
    t3 = p.op("scalar", lambda s: s.activation(tmp[:, 0:W], tmp[:, 0:W], AF.Ln), waits=[t2b])
    t4 = p.op("scalar", lambda s: s.activation(rrow[:, 0:W], tmp[:, 0:W], AF.Exp, scale=-0.5), waits=[t3])
    t5 = p.op("vector", lambda v: v.tensor_mul(mrrow[:, 0:W], mrow[:, 0:W], rrow[:, 0:W]), waits=[t4])

    rb = rb_f
    mrb = mrb_f
    scratch = sh["scr"]
    tb1 = p.op("tensor", lambda t: t.matmul(rb, onesrow_f[:, :], rrow[:, 0:W], start=True, stop=True), waits=[t4])
    tb2 = p.op("tensor", lambda t: t.matmul(mrb, onesrow_f[:, :], mrrow[:, 0:W], start=True, stop=True), waits=[t5])
    out_toks = []
    for g in range(NT):
        tn1 = p.op("vector", lambda v, g=g: v.tensor_mul(scratch[:, 0:W], x_tiles[g][:, tsl], rb),
                   waits=[tb1] + out_toks[-1:])
        if gain_col is None:
            tn2 = p.op("vector", lambda v, g=g: v.tensor_sub(out_tiles[g][:, osl], scratch[:, 0:W], mrb),
                       waits=[tb2, tn1])
        else:
            tn2a = p.op("vector", lambda v, g=g: v.tensor_sub(scratch[:, 0:W], scratch[:, 0:W], mrb),
                        waits=[tb2, tn1])
            tn2 = p.op("vector", lambda v, g=g: v.tensor_scalar(
                out_tiles[g][:, osl], scratch[:, 0:W], gain_col[g], bias_col[g], ALU.mult, ALU.add),
                waits=[tn2a])
        out_toks.append(tn2)
    return out_toks


def emit_signnet_convert(p, bufs, tag="cvt"):
    k0a, k0b = bufs["k0a_hbm"], bufs["k0b_hbm"]
    k1, k2 = bufs["k1_hbm"], bufs["k2_hbm"]
    d0, d1, d2 = bufs["d0p_dram"], bufs["d1p_dram"], bufs["d2p_dram"]

    W = 2048
    sa = p.sbuf(f"{tag}_a", [128, W], BF16)
    sb = p.sbuf(f"{tag}_b", [128, W], BF16)
    sc = p.sbuf(f"{tag}_c", [128, W], BF16)
    last = []
    prev = []
    for r in range(8):
        for h in range(2):
            rs = slice(128 * r, 128 * (r + 1))
            cs = slice(W * h, W * (h + 1))
            l1 = p.op("gpsimd", lambda g, rs=rs, cs=cs: g.dma_start(out=sa[:, :], in_=k0a[rs, cs]), waits=prev, dma=True)
            l2 = p.op("gpsimd", lambda g, rs=rs, cs=cs: g.dma_start(out=sb[:, :], in_=k0b[rs, cs]), waits=prev, dma=True)
            c1 = p.op("vector", lambda v: v.tensor_sub(sc[:, :], sa[:, :], sb[:, :]), waits=[l1, l2])
            s1 = p.op("sync", lambda s, rs=rs, cs=cs: s.dma_start(out=d0[rs, cs], in_=sc[:, :]), waits=[c1], dma=True)
            prev = [s1]
    d0_tok = prev[0]
    for r in range(32):
        for h in range(2):
            rs = slice(128 * r, 128 * (r + 1))
            cs = slice(W * h, W * (h + 1))
            l1 = p.op("gpsimd", lambda g, rs=rs, cs=cs: g.dma_start(out=sa[:, :], in_=k1[rs, cs]), waits=prev, dma=True)
            c1 = p.op("vector", lambda v: v.tensor_scalar(sc[:, :], sa[:, :], 2.0, -1.0, ALU.mult, ALU.add), waits=[l1])
            s1 = p.op("sync", lambda s, rs=rs, cs=cs: s.dma_start(out=d1[rs, cs], in_=sc[:, :]), waits=[c1], dma=True)
            prev = [s1]
    d1_tok = prev[0]
    for r in range(32):
        rs = slice(128 * r, 128 * (r + 1))
        l1 = p.op("gpsimd", lambda g, rs=rs: g.dma_start(out=sa[:, 0:1024], in_=k2[rs, :]), waits=prev, dma=True)
        c1 = p.op("vector", lambda v: v.tensor_scalar(sc[:, 0:1024], sa[:, 0:1024], 2.0, -1.0, ALU.mult, ALU.add), waits=[l1])
        s1 = p.op("sync", lambda s, rs=rs: s.dma_start(out=d2[rs, :], in_=sc[:, 0:1024]), waits=[c1], dma=True)
        prev = [s1]
    d2_tok = prev[0]
    return d0_tok, d1_tok, d2_tok


def emit_l0ffn(p, bufs, x_tiles, dep, tag="ffn0"):
    d0 = bufs["d0p_dram"]
    d1 = bufs["d1p_dram"]
    d2 = bufs["d2p_dram"]
    s0c, s1c, s2c = bufs["s0_col"], bufs["s1_col"], bufs["s2_col"]

    lnb = [p.sbuf(f"{tag}_ln{g}", [128, 512], BF16) for g in range(NT)]
    h1 = p.sbuf(f"{tag}_h1", [128, 32 * 512], BF16)
    h2 = p.sbuf(f"{tag}_h2", [128, 32 * 512], BF16)
    h1t = [h1[:, 512 * i:512 * (i + 1)] for i in range(32)]
    h2t = [h2[:, 512 * i:512 * (i + 1)] for i in range(32)]
    w0 = [p.sbuf(f"{tag}_w0", [128, 8 * 128], BF16) for _ in range(2)]
    w1 = [p.sbuf(f"{tag}_w1", [128, 32 * 128], BF16) for _ in range(2)]
    w2 = [p.sbuf(f"{tag}_w2", [128, 32 * 128], BF16) for _ in range(2)]
    ps = bufs["ps_mm0"]
    ps2 = bufs["ps_mm1"]

    d0r = d0.rearrange("(kt k) o -> k kt o", k=128)
    d1r = d1.rearrange("(kt k) o -> k kt o", k=128)
    d2r = d2.rearrange("(kt k) o -> k kt o", k=128)

    x_toks = list(dep)
    for tt in range(4):
        tsl = slice(512 * tt, 512 * (tt + 1))
        ln_toks = emit_ln_ttile(p, bufs, x_tiles, tsl, lnb, slice(0, 512),
                                x_toks[-NT:] if tt else list(dep), tag=f"{tag}_ln{tt}")
        h1_toks = []
        mm_hist = bufs.setdefault("_mm_hist", {})
        prev_ps_free = x_toks[-1:]
        for ot in range(32):
            wb = w0[ot % 2]
            lt = p.op("sync", lambda s, ot=ot, wb=wb: s.dma_start(
                out=wb[:, :].rearrange("k (kt o) -> k kt o", o=128),
                in_=d0r[:, :, 128 * ot:128 * (ot + 1)]),
                waits=[bufs["d0_ready"], mm_hist.get(("w0", ot % 2))], dma=True)
            def mm(t, wb=wb, tsl=tsl):
                for kt in range(8):
                    last = t.matmul(ps, wb[:, 128 * kt:128 * (kt + 1)], lnb[kt][:, :],
                                    start=(kt == 0), stop=(kt == 7))
                return last
            mt = p.op("tensor", mm, waits=[lt] + ln_toks + h1_toks[-1:] + prev_ps_free)
            prev_ps_free = []
            mm_hist[("w0", ot % 2)] = mt
            et = p.op("vector", lambda v, ot=ot: v.tensor_scalar(
                h1t[ot], ps, s0c, 0.0, ALU.mult, ALU.max), waits=[mt])
            h1_toks.append(et)
        if tt == 0 and "dbg" in bufs:
            for g in range(NT):
                p.op("gpsimd", lambda gg, g=g: gg.dma_start(
                    out=bufs["dbg"]["ln"][128 * g:128 * (g + 1), :], in_=lnb[g][:, :]),
                    waits=ln_toks, dma=True)
            for i in range(32):
                p.op("gpsimd", lambda gg, i=i: gg.dma_start(
                    out=bufs["dbg"]["h1"][128 * i:128 * (i + 1), :], in_=h1t[i]),
                    waits=h1_toks, dma=True)
        h2_toks = []
        for ot in range(32):
            wb = w1[ot % 2]
            lt = p.op("sync", lambda s, ot=ot, wb=wb: s.dma_start(
                out=wb[:, :].rearrange("k (kt o) -> k kt o", o=128),
                in_=d1r[:, :, 128 * ot:128 * (ot + 1)]),
                waits=[bufs["d1_ready"], mm_hist.get(("w1", ot % 2))], dma=True)
            def mm(t, wb=wb):
                for kt in range(32):
                    last = t.matmul(ps2, wb[:, 128 * kt:128 * (kt + 1)], h1t[kt],
                                    start=(kt == 0), stop=(kt == 31))
                return last
            mt = p.op("tensor", mm, waits=[lt] + h1_toks + h2_toks[-1:])
            mm_hist[("w1", ot % 2)] = mt
            et = p.op("vector", lambda v, ot=ot: v.tensor_scalar(
                h2t[ot], ps2, s1c, 0.0, ALU.mult, ALU.max), waits=[mt])
            h2_toks.append(et)
        if tt == 0 and "dbg" in bufs:
            for i in range(32):
                p.op("gpsimd", lambda gg, i=i: gg.dma_start(
                    out=bufs["dbg"]["h2"][128 * i:128 * (i + 1), :], in_=h2t[i]),
                    waits=h2_toks, dma=True)
        res_toks = []
        for g in range(NT):
            wb = w2[g % 2]
            lt = p.op("sync", lambda s, g=g, wb=wb: s.dma_start(
                out=wb[:, :].rearrange("k (kt o) -> k kt o", o=128),
                in_=d2r[:, :, 128 * g:128 * (g + 1)]),
                waits=[bufs["d2_ready"], mm_hist.get(("w2", g % 2))], dma=True)
            def mm(t, wb=wb):
                for kt in range(32):
                    last = t.matmul(ps, wb[:, 128 * kt:128 * (kt + 1)], h2t[kt],
                                    start=(kt == 0), stop=(kt == 31))
                return last
            mt = p.op("tensor", mm, waits=[lt] + h2_toks + res_toks[-1:])
            mm_hist[("w2", g % 2)] = mt
            et = p.op("vector", lambda v, g=g, tsl=tsl: v.scalar_tensor_tensor(
                x_tiles[g][:, tsl], ps, s2c, x_tiles[g][:, tsl], ALU.mult, ALU.add),
                waits=[mt])
            res_toks.append(et)
        x_toks = res_toks
    return x_toks


def make_ln_shared(p, bufs):
    bufs["ln_shared"] = {
        "x2": p.sbuf("ln_x2", [128, 512], F32),
        "scr": p.sbuf("ln_scr", [128, 512], F32),
        "m": p.sbuf("ln_m", [1, 512], F32),
        "r": p.sbuf("ln_r", [1, 512], F32),
        "mr": p.sbuf("ln_mr", [1, 512], F32),
        "t": p.sbuf("ln_t", [1, 512], F32),
    }


def make_psum_pool(p, bufs):
    bufs["ps_mm0"] = p.psum("ps_mm0", [128, 512])[:, :]
    bufs["ps_mm1"] = p.psum("ps_mm1", [128, 512])[:, :]
    bufs["ps_rb"] = p.psum("ps_rb", [128, 512])[:, :]
    bufs["ps_mrb"] = p.psum("ps_mrb", [128, 512])[:, :]
    bufs["ps_row0"] = p.psum("ps_row0", [1, 512])[:, :]
    bufs["ps_row1"] = p.psum("ps_row1", [1, 512])[:, :]
    bufs["ps_x0"] = p.psum("ps_x0", [128, 512])[:, :]


def setup_common(p, nc, bufs):
    consts_r = nc.dram_tensor("ones_row", [1, 128], F32, kind="ExternalInput").ap()
    consts_c = nc.dram_tensor("ones_col", [128, 1], F32, kind="ExternalInput").ap()
    cr = p.sbuf("cr", [1, 128], F32)
    cc = p.sbuf("cc", [128, 1], F32)
    ccb = p.sbuf("ccb", [128, 1], BF16)
    t1 = p.op("sync", lambda s: s.dma_start(out=cr[:, :], in_=consts_r), dma=True)
    t2 = p.op("sync", lambda s: s.dma_start(out=cc[:, :], in_=consts_c), dma=True)
    t3 = p.op("gpsimd", lambda g: g.dma_start(out=ccb[:, :], in_=consts_c), dma=True)
    crb = p.sbuf("crb", [1, 128], BF16)
    t4 = p.op("gpsimd", lambda g: g.dma_start(out=crb[:, :], in_=consts_r), dma=True)
    bufs["ones_row_f"] = cr[:, :]
    bufs["ones_col_f"] = cc[:, :]
    bufs["ones_col_bf"] = ccb[:, :]
    bufs["ones_row_bf"] = crb[:, :]
    bufs["consts_ready"] = [t1, t2, t3, t4]
    return [t1, t2, t3, t4]


def emit_scalar_bcast(p, bufs, scalar_dram_list, scales, tag="sb"):
    n = len(scalar_dram_list)
    s_sb = p.sbuf(f"{tag}_row", [1, n], F32)
    scol = p.sbuf(f"{tag}_col", [128, n], F32)
    pscol = bufs["ps_rb"][:, 0:n]
    toks = [p.op("sync", lambda s, i=i, a=a: s.dma_start(out=s_sb[:, i:i + 1], in_=a), dma=True)
            for i, a in enumerate(scalar_dram_list)]
    tb = p.op("tensor", lambda t: t.matmul(pscol, bufs["ones_row_f"], s_sb[:, :],
                                           start=True, stop=True), waits=toks + bufs["consts_ready"])
    tv = []
    for i, sc in enumerate(scales):
        tv.append(p.op("vector", lambda v, i=i, sc=sc: v.tensor_scalar_mul(
            scol[:, i:i + 1], pscol[:, i:i + 1], float(sc)), waits=[tb]))
    return scol, tv


def build_stage_b():
    nc = bass.Bass("TRN2", target_bir_lowering=False)
    x_in = nc.dram_tensor("x_in", [D, T], F32, kind="ExternalInput").ap()
    k0a = nc.dram_tensor("k0a", [1024, 4096], U8, kind="ExternalInput").ap()
    k0b = nc.dram_tensor("k0b", [1024, 4096], U8, kind="ExternalInput").ap()
    k1 = nc.dram_tensor("k1", [4096, 4096], U8, kind="ExternalInput").ap()
    k2 = nc.dram_tensor("k2", [4096, 1024], U8, kind="ExternalInput").ap()
    s0 = nc.dram_tensor("s0", [1, 1], F32, kind="ExternalInput").ap()
    s1 = nc.dram_tensor("s1", [1, 1], F32, kind="ExternalInput").ap()
    s2 = nc.dram_tensor("s2", [1, 1], F32, kind="ExternalInput").ap()
    x_out = nc.dram_tensor("x_out", [D, T], F32, kind="ExternalOutput").ap()
    dbg_ln = nc.dram_tensor("dbg_ln", [D, 512], F32, kind="ExternalOutput").ap()
    dbg_h1 = nc.dram_tensor("dbg_h1", [4096, 512], F32, kind="ExternalOutput").ap()
    dbg_h2 = nc.dram_tensor("dbg_h2", [4096, 512], F32, kind="ExternalOutput").ap()
    dbg_d0 = nc.dram_tensor("dbg_d0", [1024, 4096], BF16, kind="ExternalOutput").ap()
    dbg_d1 = nc.dram_tensor("dbg_d1", [256, 4096], F32, kind="ExternalOutput").ap()

    bufs = {}
    bufs["d0p_dram"] = nc.dram_tensor("d0p", [1024, 4096], BF16).ap()
    bufs["d1p_dram"] = nc.dram_tensor("d1p", [4096, 4096], BF16).ap()
    bufs["d2p_dram"] = nc.dram_tensor("d2p", [4096, 1024], BF16).ap()
    bufs["k0a_hbm"], bufs["k0b_hbm"] = k0a, k0b
    bufs["k1_hbm"], bufs["k2_hbm"] = k1, k2

    p = Prog(nc)
    make_psum_pool(p, bufs)
    make_ln_shared(p, bufs)
    setup_common(p, nc, bufs)
    scol, stoks = emit_scalar_bcast(p, bufs, [s0, s1, s2], [4.0, 2.0, 1.0])
    bufs["s0_col"], bufs["s1_col"], bufs["s2_col"] = scol[:, 0:1], scol[:, 1:2], scol[:, 2:3]

    xs = p.sbuf("x_res", [128, NT * T], F32)
    x_tiles = [xs[:, g * T:(g + 1) * T] for g in range(NT)]
    xloads = [p.op("sync", lambda s, g=g: s.dma_start(out=x_tiles[g], in_=x_in[128 * g:128 * (g + 1), :]), dma=True)
              for g in range(NT)]

    td0, td1, td2 = emit_signnet_convert(p, bufs)
    bufs["d0_ready"], bufs["d1_ready"], bufs["d2_ready"] = td0, td1, td2
    bufs["dbg"] = {"ln": dbg_ln, "h1": dbg_h1, "h2": dbg_h2}
    p.op("gpsimd", lambda g: g.dma_start(out=dbg_d0, in_=bufs["d0p_dram"][:, :]), waits=[td0], dma=True)
    p.op("gpsimd", lambda g: g.dma_start(out=dbg_d1, in_=bufs["d1p_dram"][0:256, :]), waits=[td1], dma=True)

    x_toks = emit_l0ffn(p, bufs, x_tiles, xloads + stoks)

    st = [p.op("sync", lambda s, g=g: s.dma_start(out=x_out[128 * g:128 * (g + 1), :], in_=x_tiles[g]),
               waits=x_toks, dma=True) for g in range(NT)]
    p.op("gpsimd", lambda g: g.engine_nop(), waits=st)
    p.finalize()
    return nc



def emit_rwkv(p, bufs, x_tiles, dep, tag="rwkv"):
    Q = 512
    QC = Q + 1
    ln3 = p.sbuf(f"{tag}_ln3", [128, NT * QC], BF16)
    ln3t = [ln3[:, g * QC:(g + 1) * QC] for g in range(NT)]
    cols = p.sbuf(f"{tag}_cols", [128, 48], F32)
    ecols = p.sbuf(f"{tag}_ecols", [128, 16], F32)
    tcol = p.op("sync", lambda s: s.dma_start(out=cols[:, :], in_=bufs["rwkv_cols"]), dma=True)
    te1 = p.op("scalar", lambda s: s.activation(ecols[:, 0:8], cols[:, 32:40], AF.Exp), waits=[tcol])
    te2 = p.op("scalar", lambda s: s.activation(ecols[:, 8:16], cols[:, 40:48], AF.Exp), waits=[tcol])
    te3 = p.op("scalar", lambda s: s.activation(ecols[:, 8:16], ecols[:, 8:16], AF.Exp, scale=-1.0), waits=[te2])
    col_toks = [te1, te3]

    kin = p.sbuf(f"{tag}_kin", [128, NT * Q], BF16)
    vin = p.sbuf(f"{tag}_vin", [128, NT * Q], BF16)
    rin = p.sbuf(f"{tag}_rin", [128, NT * Q], BF16)
    dQ = p.sbuf(f"{tag}_d", [128, Q], F32)
    ek = p.sbuf(f"{tag}_ek", [128, NT * Q], F32)
    ekv = p.sbuf(f"{tag}_ekv", [128, NT * Q], F32)
    rqs = p.sbuf(f"{tag}_rq", [128, Q], F32)
    rw = p.sbuf(f"{tag}_rw", [128, NT * Q], BF16)
    NUM = p.sbuf(f"{tag}_num", [128, NT * QC], F32)
    DEN = p.sbuf(f"{tag}_den", [128, NT * QC], F32)
    carry = p.sbuf(f"{tag}_carry", [128, 16], F32)
    _w = p.sbuf(f"{tag}_w", [128, NT * D], BF16)
    wbuf = [_w, _w]

    kin_t = [kin[:, g * Q:(g + 1) * Q] for g in range(NT)]
    vin_t = [vin[:, g * Q:(g + 1) * Q] for g in range(NT)]
    rin_t = [rin[:, g * Q:(g + 1) * Q] for g in range(NT)]
    ek_t = [ek[:, g * Q:(g + 1) * Q] for g in range(NT)]
    ek_t_pre = [ek[:, g * Q:g * Q + 257] for g in range(NT)]
    ekv_t = [ekv[:, g * Q:(g + 1) * Q] for g in range(NT)]
    rw_t = [rw[:, g * Q:(g + 1) * Q] for g in range(NT)]
    NUM_t = [NUM[:, g * QC:(g + 1) * QC] for g in range(NT)]
    DEN_t = [DEN[:, g * QC:(g + 1) * QC] for g in range(NT)]
    ps = bufs["ps_mm0"]

    def wslice(b, kt, g):
        return wbuf[b][:, kt * D + 128 * g: kt * D + 128 * (g + 1)]

    tz = p.op("vector", lambda v: v.memset(carry[:, :], 0.0))
    mmh = {}
    scan_toks = {}

    wk = bufs["wkT_dram"].rearrange("(kt k) o -> k kt o", k=128)
    wv = bufs["wvT_dram"].rearrange("(kt k) o -> k kt o", k=128)
    wr = bufs["wrT_dram"].rearrange("(kt k) o -> k kt o", k=128)
    wo = bufs["woT_dram"].rearrange("(kt k) o -> k kt o", k=128)
    wre = lambda b: wbuf[b][:, :].rearrange("k (kt o) -> k kt o", o=D)

    HW = 256
    hx = [ek_t_pre[g] for g in range(NT)]
    hloads = [p.op("sync", lambda s, g=g: s.dma_start(out=hx[g], in_=bufs["halo_x_dram"][128 * g:128 * (g + 1), :]), dma=True)
              for g in range(NT)]
    bufs["halo_x_ready"] = hloads
    hln = [rw[:, g * Q:g * Q + HW + 1] for g in range(NT)]
    ln_toksH = emit_ln_ttile(p, bufs, hx, slice(0, HW + 1), hln, slice(0, HW + 1),
                             bufs["halo_x_ready"] + list(dep), tag=f"{tag}_lnw")
    warm_mix = []
    for g in range(NT):
        t1 = p.op("vector", lambda v, g=g: v.tensor_sub(dQ[:, 0:HW], hln[g][:, 1:HW + 1], hln[g][:, 0:HW]),
                  waits=ln_toksH + warm_mix[-2:])
        t2 = p.op("vector", lambda v, g=g: v.scalar_tensor_tensor(
            kin_t[g][:, 0:HW], dQ[:, 0:HW], cols[:, 0 + g:1 + g], hln[g][:, 0:HW], ALU.mult, ALU.add),
            waits=[t1, tcol])
        t3 = p.op("vector", lambda v, g=g: v.scalar_tensor_tensor(
            vin_t[g][:, 0:HW], dQ[:, 0:HW], cols[:, 8 + g:9 + g], hln[g][:, 0:HW], ALU.mult, ALU.add),
            waits=[t1])
        warm_mix += [t2, t3]
    ln_toks0 = [p.op("vector", lambda v, g=g: v.tensor_copy(ln3t[g][:, 0:1], hln[g][:, HW:HW + 1]),
                     waits=ln_toksH) for g in range(NT)]
    lwk0 = p.op("gpsimd", lambda g_: g_.dma_start(out=wre(0), in_=wk[:, :, :]), dma=True)
    warm_ek = []
    for g in range(NT):
        def mmk(t, g=g):
            for kt in range(NT):
                last = t.matmul(ps[:, 0:HW], wslice(0, kt, g), kin_t[kt][:, 0:HW],
                                start=(kt == 0), stop=(kt == NT - 1))
            return last
        mt = p.op("tensor", mmk, waits=[lwk0] + warm_mix + warm_ek[-1:])
        mmh["w"] = mt
        et = p.op("scalar", lambda s, g=g: s.activation(ek_t[g][:, 0:HW], ps[:, 0:HW], AF.Exp), waits=[mt])
        warm_ek.append(et)
    lwv0 = p.op("gpsimd", lambda g_: g_.dma_start(out=wre(1), in_=wv[:, :, :]),
                waits=[mmh.get("w")], dma=True)
    warm_v = []
    for g in range(NT):
        def mmv(t, g=g):
            for kt in range(NT):
                last = t.matmul(ps[:, 0:HW], wslice(1, kt, g), vin_t[kt][:, 0:HW],
                                start=(kt == 0), stop=(kt == NT - 1))
            return last
        mt = p.op("tensor", mmv, waits=[lwv0] + warm_mix + (warm_v[-1:] if warm_v else warm_ek) + [warm_ek[g]])
        mmh["w1"] = mt
        et = p.op("vector", lambda v, g=g: v.tensor_copy(ekv_t[g][:, 0:HW], ps[:, 0:HW]), waits=[mt])
        et2 = p.op("vector", lambda v, g=g: v.tensor_mul(ekv_t[g][:, 0:HW], ekv_t[g][:, 0:HW], ek_t[g][:, 0:HW]),
                   waits=[et, warm_ek[g]])
        warm_v.append(et2)
    for g in range(NT):
        sN = p.op("vector", lambda v, g=g: v.tensor_tensor_scan(
            NUM_t[g][:, 0:HW], ecols[:, 8 + g:9 + g].broadcast_to((128, HW)), ekv_t[g][:, 0:HW],
            0.0, ALU.mult, ALU.add), waits=[warm_v[g]] + col_toks)
        ccN = p.op("vector", lambda v, g=g: v.tensor_scalar_mul(
            carry[:, g:g + 1], NUM_t[g][:, HW - 1:HW], bufs["carry_mask"]),
            waits=[sN, tz] + bufs["cmask_ready"])
        sD = p.op("vector", lambda v, g=g: v.tensor_tensor_scan(
            DEN_t[g][:, 0:HW], ecols[:, 8 + g:9 + g].broadcast_to((128, HW)), ek_t[g][:, 0:HW],
            0.0, ALU.mult, ALU.add), waits=[warm_ek[g], ccN] + col_toks)
        ccD = p.op("vector", lambda v, g=g: v.tensor_scalar_mul(
            carry[:, 8 + g:9 + g], DEN_t[g][:, HW - 1:HW], bufs["carry_mask"]), waits=[sD])
        scan_toks[("n", g)] = ccD
        scan_toks[("d", g)] = ccD

    warm_done = [scan_toks[("n", g)] for g in range(NT)]
    x_toks = list(dep)
    halo_toks = ln_toks0
    prev_mix = []
    for q in range(4):
        tsl = slice(Q * q, Q * (q + 1))
        ln_toks = emit_ln_ttile(p, bufs, x_tiles, tsl, ln3t, slice(1, QC),
                                (x_toks[-NT:] if q else list(dep)) + prev_mix,
                                tag=f"{tag}_ln{q}")
        dep_ln = ln_toks + halo_toks + (warm_done if q == 0 else [])
        mix_toks = []
        for g in range(NT):
            t1 = p.op("vector", lambda v, g=g: v.tensor_sub(dQ[:, :], ln3t[g][:, 1:QC], ln3t[g][:, 0:Q]),
                      waits=dep_ln + mix_toks[-3:])
            t2 = p.op("vector", lambda v, g=g: v.scalar_tensor_tensor(
                kin_t[g], dQ[:, :], cols[:, 0 + g:1 + g], ln3t[g][:, 0:Q], ALU.mult, ALU.add),
                waits=[t1, tcol])
            t3 = p.op("vector", lambda v, g=g: v.scalar_tensor_tensor(
                vin_t[g], dQ[:, :], cols[:, 8 + g:9 + g], ln3t[g][:, 0:Q], ALU.mult, ALU.add),
                waits=[t1])
            t4 = p.op("vector", lambda v, g=g: v.scalar_tensor_tensor(
                rin_t[g], dQ[:, :], cols[:, 16 + g:17 + g], ln3t[g][:, 0:Q], ALU.mult, ALU.add),
                waits=[t1])
            mix_toks += [t2, t3, t4]
        halo_toks = []
        for g in range(NT):
            hc = p.op("vector", lambda v, g=g: v.tensor_copy(ln3t[g][:, 0:1], ln3t[g][:, Q:QC]),
                      waits=mix_toks)
            halo_toks.append(hc)
        prev_mix = halo_toks

        lw = p.op("gpsimd", lambda g_: g_.dma_start(out=wre(0), in_=wk[:, :, :]),
                  waits=[mmh.get("w"), mmh.get("w1")], dma=True)
        ev_toks = []
        for g in range(NT):
            def mmk(t, g=g):
                for kt in range(NT):
                    last = t.matmul(ps, wslice(0, kt, g), kin_t[kt],
                                    start=(kt == 0), stop=(kt == NT - 1))
                return last
            mt = p.op("tensor", mmk, waits=[lw] + mix_toks + ev_toks[-1:])
            mmh["w"] = mt
            et = p.op("scalar", lambda s, g=g: s.activation(ek_t[g], ps, AF.Exp), waits=[mt])
            ev_toks.append(et)
        lwv = p.op("gpsimd", lambda g_: g_.dma_start(out=wre(1), in_=wv[:, :, :]),
                   waits=[mmh.get("w"), mmh.get("w1")], dma=True)
        vv_toks = []
        for g in range(NT):
            def mmv(t, g=g):
                for kt in range(NT):
                    last = t.matmul(ps, wslice(1, kt, g), vin_t[kt],
                                    start=(kt == 0), stop=(kt == NT - 1))
                return last
            mt = p.op("tensor", mmv, waits=[lwv] + mix_toks + (vv_toks[-1:] if vv_toks else ev_toks) + [ev_toks[g]])
            mmh["w"] = mt
            et = p.op("vector", lambda v, g=g: v.tensor_copy(ekv_t[g], ps), waits=[mt])
            et2 = p.op("vector", lambda v, g=g: v.tensor_mul(ekv_t[g], ekv_t[g], ek_t[g]),
                       waits=[et, ev_toks[g]])
            vv_toks.append(et2)
        new_scan = {}
        for g in range(NT):
            prevN = scan_toks.get(("n", g))
            cN = p.op("vector", lambda v, g=g: v.tensor_copy(NUM_t[g][:, 0:1], carry[:, g:g + 1]),
                      waits=[tz, prevN])
            sN = p.op("vector", lambda v, g=g: v.tensor_tensor_scan(
                NUM_t[g][:, 1:QC], ecols[:, 8 + g:9 + g].broadcast_to((128, Q)), ekv_t[g],
                NUM_t[g][:, 0:1], ALU.mult, ALU.add), waits=[cN, vv_toks[g]] + col_toks)
            ccN = p.op("vector", lambda v, g=g: v.tensor_copy(carry[:, g:g + 1], NUM_t[g][:, Q:QC]),
                       waits=[sN])
            prevD = scan_toks.get(("d", g))
            cD = p.op("vector", lambda v, g=g: v.tensor_copy(DEN_t[g][:, 0:1], carry[:, 8 + g:9 + g]),
                      waits=[tz, prevD])
            sD = p.op("vector", lambda v, g=g: v.tensor_tensor_scan(
                DEN_t[g][:, 1:QC], ecols[:, 8 + g:9 + g].broadcast_to((128, Q)), ek_t[g],
                DEN_t[g][:, 0:1], ALU.mult, ALU.add), waits=[cD, ev_toks[g]] + col_toks)
            ccD = p.op("vector", lambda v, g=g: v.tensor_copy(carry[:, 8 + g:9 + g], DEN_t[g][:, Q:QC]),
                       waits=[sD])
            new_scan[("n", g)] = ccN
            new_scan[("d", g)] = ccD
        scan_toks = new_scan
        ekvdump = []
        if q == 0 and "dbg_rwkv" in bufs and "ekvraw" in bufs["dbg_rwkv"]:
            dd = bufs["dbg_rwkv"]
            ekvdump = [p.op("gpsimd", lambda s, g=g: s.dma_start(
                out=dd["ekvraw"][128 * g:128 * (g + 1), :], in_=ekv_t[g]), waits=vv_toks, dma=True)
                for g in range(NT)]
        wkv_toks = []
        for g in range(NT):
            e1 = p.op("vector", lambda v, g=g: v.tensor_scalar_mul(ekv_t[g], ekv_t[g], ecols[:, 0 + g:1 + g]),
                      waits=[scan_toks[("n", g)]] + ekvdump)
            e2 = p.op("vector", lambda v, g=g: v.tensor_add(ekv_t[g], ekv_t[g], NUM_t[g][:, 0:Q]), waits=[e1])
            e3 = p.op("vector", lambda v, g=g: v.tensor_scalar_mul(ek_t[g], ek_t[g], ecols[:, 0 + g:1 + g]),
                      waits=[scan_toks[("d", g)]])
            e4 = p.op("vector", lambda v, g=g: v.tensor_add(ek_t[g], ek_t[g], DEN_t[g][:, 0:Q]), waits=[e3])
            e5 = p.op("vector", lambda v, g=g: v.reciprocal(ek_t[g], ek_t[g]), waits=[e4])
            e6 = p.op("vector", lambda v, g=g: v.tensor_mul(ekv_t[g], ekv_t[g], ek_t[g]), waits=[e2, e5])
            wkv_toks.append(e6)
        if q == 0 and "dbg_rwkv" in bufs:
            dd = bufs["dbg_rwkv"]
            for g in range(NT):
                p.op("gpsimd", lambda s, g=g: s.dma_start(out=dd["ln3"][128 * g:128 * (g + 1), :], in_=ln3t[g][:, 0:QC]), waits=mix_toks + halo_toks, dma=True)
                p.op("gpsimd", lambda s, g=g: s.dma_start(out=dd["kin"][128 * g:128 * (g + 1), :], in_=kin_t[g]), waits=mix_toks, dma=True)
                p.op("gpsimd", lambda s, g=g: s.dma_start(out=dd["ek"][128 * g:128 * (g + 1), :], in_=ek_t[g]), waits=[e for e in wkv_toks], dma=True)
                p.op("gpsimd", lambda s, g=g: s.dma_start(out=dd["wkv"][128 * g:128 * (g + 1), :], in_=ekv_t[g]), waits=wkv_toks, dma=True)
                p.op("gpsimd", lambda s, g=g: s.dma_start(out=dd["num"][128 * g:128 * (g + 1), :], in_=NUM_t[g]), waits=wkv_toks, dma=True)
        lwr = p.op("gpsimd", lambda g_: g_.dma_start(out=wre(0), in_=wr[:, :, :]),
                   waits=[mmh.get("w")], dma=True)
        rw_toks = []
        for g in range(NT):
            def mmr(t, g=g):
                for kt in range(NT):
                    last = t.matmul(ps, wslice(0, kt, g), rin_t[kt],
                                    start=(kt == 0), stop=(kt == NT - 1))
                return last
            mt = p.op("tensor", mmr, waits=[lwr] + mix_toks + (rw_toks[-1:] if rw_toks else vv_toks))
            mmh["w"] = mt
            st1 = p.op("scalar", lambda s, g=g: s.activation(rqs[:, :], ps, AF.Sigmoid),
                       waits=[mt] + rw_toks[-1:])
            st2 = p.op("vector", lambda v, g=g: v.tensor_mul(rw_t[g], rqs[:, :], ekv_t[g]),
                       waits=[st1, wkv_toks[g]])
            rw_toks.append(st2)
        lwo = p.op("gpsimd", lambda g_: g_.dma_start(out=wre(1), in_=wo[:, :, :]),
                   waits=[mmh.get("w")], dma=True)
        res_toks = []
        for g in range(NT):
            def mmo(t, g=g):
                for kt in range(NT):
                    last = t.matmul(ps, wslice(1, kt, g), rw_t[kt],
                                    start=(kt == 0), stop=(kt == NT - 1))
                return last
            mt = p.op("tensor", mmo, waits=[lwo] + rw_toks + res_toks[-1:])
            mmh["w"] = mt
            et = p.op("vector", lambda v, g=g, tsl=tsl: v.scalar_tensor_tensor(
                x_tiles[g][:, tsl], ps, cols[:, 24 + g:25 + g], x_tiles[g][:, tsl], ALU.add, ALU.add),
                waits=[mt])
            res_toks.append(et)
        x_toks = res_toks
    return x_toks


def build_stage_c():
    nc = bass.Bass("TRN2", target_bir_lowering=False)
    x_in = nc.dram_tensor("x_in", [D, T], F32, kind="ExternalInput").ap()
    halo_x = nc.dram_tensor("halo_x", [D, 257], F32, kind="ExternalInput").ap()
    cmask = nc.dram_tensor("cmask", [128, 1], F32, kind="ExternalInput").ap()
    wkT = nc.dram_tensor("wkT", [D, D], F32, kind="ExternalInput").ap()
    wvT = nc.dram_tensor("wvT", [D, D], F32, kind="ExternalInput").ap()
    wrT = nc.dram_tensor("wrT", [D, D], F32, kind="ExternalInput").ap()
    woT = nc.dram_tensor("woT", [D, D], F32, kind="ExternalInput").ap()
    rcols = nc.dram_tensor("rwkv_cols", [128, 48], F32, kind="ExternalInput").ap()
    x_out = nc.dram_tensor("x_out", [D, T], F32, kind="ExternalOutput").ap()
    dbg = {
        "ln3": nc.dram_tensor("dbg_ln3", [D, 513], BF16, kind="ExternalOutput").ap(),
        "kin": nc.dram_tensor("dbg_kin", [D, 512], BF16, kind="ExternalOutput").ap(),
        "ek": nc.dram_tensor("dbg_ek", [D, 512], F32, kind="ExternalOutput").ap(),
        "wkv": nc.dram_tensor("dbg_wkv", [D, 512], F32, kind="ExternalOutput").ap(),
        "num": nc.dram_tensor("dbg_num", [D, 513], F32, kind="ExternalOutput").ap(),
        "ekvraw": nc.dram_tensor("dbg_ekvraw", [D, 512], F32, kind="ExternalOutput").ap(),
    }

    bufs = {}
    bufs["dbg_rwkv"] = dbg
    bufs["wkT_dram"], bufs["wvT_dram"] = wkT, wvT
    bufs["wrT_dram"], bufs["woT_dram"] = wrT, woT
    bufs["rwkv_cols"] = rcols

    p = Prog(nc)
    make_psum_pool(p, bufs)
    make_ln_shared(p, bufs)
    setup_common(p, nc, bufs)

    xs = p.sbuf("x_res", [128, NT * T], F32)
    x_tiles = [xs[:, g * T:(g + 1) * T] for g in range(NT)]
    xloads = [p.op("sync", lambda s, g=g: s.dma_start(out=x_tiles[g], in_=x_in[128 * g:128 * (g + 1), :]), dma=True)
              for g in range(NT)]
    hx_tiles = None
    bufs["halo_x_dram"] = halo_x
    cm_sb = p.sbuf("cmask", [128, 1], F32)
    cml = p.op("sync", lambda s: s.dma_start(out=cm_sb[:, :], in_=cmask), dma=True)
    bufs["carry_mask"] = cm_sb[:, :]
    bufs["cmask_ready"] = [cml]

    x_toks = emit_rwkv(p, bufs, x_tiles, xloads + bufs["consts_ready"])

    st = [p.op("sync", lambda s, g=g: s.dma_start(out=x_out[128 * g:128 * (g + 1), :], in_=x_tiles[g]),
               waits=x_toks, dma=True) for g in range(NT)]
    p.op("gpsimd", lambda g: g.engine_nop(), waits=st)
    p.finalize()
    return nc



def emit_perf_part1(p, bufs, x_tiles, dep, out, tag="pf1", phases=("qkv", "v", "kstab", "qstab")):
    lnb = [p.sbuf(f"{tag}_ln{g}", [128, T], BF16) for g in range(NT)]
    wq = p.sbuf(f"{tag}_wq", [128, NT * 1024], BF16)
    q2c = p.sbuf(f"{tag}_q2", [128, 512], F32)
    sk4 = p.sbuf(f"{tag}_sk4", [128, 4], F32)
    sk = p.sbuf(f"{tag}_sk", [128, 16], F32)
    seld = bufs["seld_sb"]
    pmTb = bufs["pmT_sb"]
    pmPb = bufs["pmTpair_sb"]
    ident = bufs["ident_sb"]
    q_t, k_t, vtm = out["q"], out["k"], out["vtm"]
    ps0 = bufs["ps_mm0"]
    dbank = [bufs["ps_mm1"], bufs["ps_rb"], bufs["ps_mrb"], bufs["ps_x0"]]
    wqr = bufs["wqkv_dram"].rearrange("(kt k) o -> k kt o", k=128)

    def load_wpart(part, waits):
        return p.op("gpsimd", lambda g, part=part: g.dma_start(
            out=wq[:, :].rearrange("k (kt o) -> k kt o", o=1024),
            in_=wqr[:, :, 1024 * part:1024 * (part + 1)]), waits=waits, dma=True)

    ln_toks = []
    for tt in range(4):
        prev = ln_toks[-NT:]
        xl = [p.op("sync", lambda s, g=g, tt=tt: s.dma_start(
            out=x_tiles[g], in_=bufs["x_in_dram"][128 * g:128 * (g + 1), 512 * tt:512 * (tt + 1)]),
            waits=list(dep) + prev, dma=True) for g in range(NT)]
        ln_toks += emit_ln_ttile(p, bufs, x_tiles, slice(0, 512),
                                 lnb, slice(512 * tt, 512 * (tt + 1)),
                                 xl + prev, tag=f"{tag}_ln{tt}")

    vre = vtm.rearrange("p (tc h c) -> p tc h c", h=16, c=65)
    tones = p.op("vector", lambda v: v.memset(vre[:, :, :, 64:65], 1.0))
    tz1 = p.op("vector", lambda v: v.memset(out["sq"], 0.0))
    tz2 = p.op("vector", lambda v: v.memset(sk[:, :], 0.0))

    toks = []
    diag_toks = []
    prev_read = list(ln_toks)
    lwq = None
    for part in (range(2) if "qkv" in phases else []):
        lwq = load_wpart(part, [lwq] + prev_read if part else [lwq])
        dst = q_t if part == 0 else k_t
        for cc in range(4):
            csl = slice(512 * cc, 512 * (cc + 1))
            for ot in range(8):
                def mm(t, ot=ot, csl=csl):
                    for kt in range(NT):
                        base = 1024 * kt + 128 * ot
                        last = t.matmul(ps0, wq[:, base:base + 128], lnb[kt][:, csl],
                                        start=(kt == 0), stop=(kt == NT - 1))
                    return last
                mt = p.op("tensor", mm, waits=[lwq] + prev_read)
                prev_read = []
                ev = p.op("vector", lambda v, ot=ot, csl=csl, dst=dst: v.tensor_copy(dst[ot][:, csl], ps0),
                          waits=[mt] + toks[-1:])
                toks.append(ev)
                if "sq" in phases:
                    sq_ = p.op("scalar", lambda s, ot=ot, csl=csl, dst=dst: s.activation(
                        q2c[:, :], dst[ot][:, csl], AF.Square), waits=[ev] + diag_toks[-1:])
                    prev_read.append(sq_)
                    if "diag" in phases:
                        def mmd(t, part=part, cc=cc, ot=ot):
                            for sc in range(4):
                                co = 16 * (4 * part + cc)
                                last = t.matmul(dbank[sc][:, co:co + 16],
                                                q2c[:, 128 * sc:128 * (sc + 1)], seld[:, 16 * ot:16 * (ot + 1)],
                                                start=(ot == 0), stop=(ot == 7))
                            return last
                        dt_ = p.op("tensor", mmd, waits=[sq_] + bufs["seld_ready"])
                        diag_toks.append(dt_)
                prev_read.append(ev)
    v_toks = []
    if "v" in phases:
        lwq = load_wpart(2, [lwq] + prev_read)
    for tc in (range(16) if "v" in phases else []):
        for half in range(2):
            def mmv(t, tc=tc, half=half):
                for kt in range(NT):
                    base = 1024 * kt + 512 * half
                    last = t.matmul(ps0, lnb[kt][:, 128 * tc:128 * (tc + 1)], wq[:, base:base + 512],
                                    start=(kt == 0), stop=(kt == NT - 1))
                return last
            mt = p.op("tensor", mmv, waits=[lwq] + ln_toks + prev_read)
            ev = p.op("vector", lambda v, tc=tc, half=half: v.tensor_copy(
                vre[:, tc, 8 * half:8 * (half + 1), 0:64],
                ps0[:, :].rearrange("p (h d) -> p h d", d=64)), waits=[mt, tones])
            prev_read = [ev]
            v_toks.append(ev)

    dq_sb, dk_sb = out["diagq"], out["diagk"]
    ed_toks = []
    for part in (range(2) if "diag" in phases else []):
        dsb = dq_sb if part == 0 else dk_sb
        for sc in range(4):
            def cp(v, part=part, sc=sc, dsb=dsb):
                return v.tensor_copy(
                    dsb.rearrange("p (cc four h) -> p cc four h", four=4, h=16)[:, :, sc, :],
                    dbank[sc][:, 0:128].rearrange("p (pp cc h) -> p pp cc h", pp=2, h=16)[:, part, :, :])
            ed_toks.append(p.op("vector", cp, waits=diag_toks))

    sk_toks = []
    ubank = [bufs["ps_mm1"], bufs["ps_rb"], bufs["ps_mrb"], bufs["ps_x0"]]
    for h in (range(16) if "kstab" in phases else []):
        g, off = h // 2, 64 * (h % 2)
        def mmu(t, g=g, off=off, h=h):
            for cc in range(4):
                last = t.matmul(ubank[cc][:, :], bufs["pmT_lhsT"][h], k_t[g][off:off + 64, 512 * cc:512 * (cc + 1)],
                                start=True, stop=True)
            return last
        mt = p.op("tensor", mmu, waits=toks + v_toks + ed_toks + sk_toks[-1:] + bufs["pm_ready"])
        def red(v, h=h):
            for cc in range(3):
                v.reduce_max(sk4[:, cc:cc + 1], ubank[cc][:, :], axis=AX.X)
            return v.reduce_max(sk4[:, 3:4], ubank[3][:, :], axis=AX.X)
        rd = p.op("vector", red, waits=[mt, tz2])
        rd2 = p.op("vector", lambda v, h=h: v.reduce_max(sk[:, h:h + 1], sk4[:, :], axis=AX.X), waits=[rd])
        sk_toks.append(rd2)
    if "kstab" in phases:
        ttr = p.op("tensor", lambda t: t.transpose(bufs["ps_mm1"][0:16, 0:128], sk[:, :], ident[:, :]),
                   waits=sk_toks + ed_toks + bufs["ident_ready"])
        tsk = p.op("vector", lambda v: v.reduce_max(out["skcol"], bufs["ps_mm1"][0:16, 0:128], axis=AX.X),
                   waits=[ttr])
    else:
        tsk = p.op("vector", lambda v: v.memset(out["skcol"], 0.0))

    sq_sb = out["sq"]
    qs_toks = [tsk]
    for tc in (range(16) if "qstab2" in phases else []):
        for pr in range(8):
            def mmp(t, tc=tc, pr=pr):
                return t.matmul(ps0[:, 0:256], q_t[pr][:, 128 * tc:128 * (tc + 1)],
                                pmPb[:, 256 * pr:256 * (pr + 1)], start=True, stop=True)
            mt = p.op("tensor", mmp, waits=qs_toks[-1:] + toks + [tsk] + bufs["pm_ready"])
            rd = p.op("vector", lambda v, tc=tc, pr=pr: v.tensor_reduce(
                sq_sb[:, 16 * tc + 2 * pr:16 * tc + 2 * pr + 2],
                ps0[:, 0:256].rearrange("p (h f) -> p h f", f=128), AX.X, ALU.max), waits=[mt, tz1])
            qs_toks.append(rd)
    for tc in (range(16) if "qstab" in phases else []):
        def mmp(t, tc=tc):
            for pr in range(8):
                last = t.matmul(ubank[pr // 2][:, 256 * (pr % 2):256 * (pr % 2) + 256],
                                q_t[pr][:, 128 * tc:128 * (tc + 1)],
                                pmPb[:, 256 * pr:256 * (pr + 1)], start=True, stop=True)
            return last
        mt = p.op("tensor", mmp, waits=qs_toks[-1:] + toks + [tsk] + bufs["pm_ready"])
        def red(v, tc=tc):
            for pr in range(8):
                last = v.tensor_reduce(
                    sq_sb[:, 16 * tc + 2 * pr:16 * tc + 2 * pr + 2],
                    ubank[pr // 2][:, 256 * (pr % 2):256 * (pr % 2) + 256].rearrange("p (h f) -> p h f", f=128),
                    AX.X, ALU.max)
            return last
        rd = p.op("vector", red, waits=[mt, tz1])
        qs_toks.append(rd)
    return toks + v_toks + ed_toks + qs_toks + [tsk]


def emit_perf_part2(p, bufs, dep, kin, out, tag="pf2"):
    k_t, vtm = kin["k"], kin["vtm"]
    diagk = kin["diagk"]
    pmTb = bufs["pmT_sb"]
    kf = p.sbuf(f"{tag}_kf", [128, 256], BF16)
    dkt = p.sbuf(f"{tag}_dkt", [128, 256], F32)
    sbb = p.sbuf(f"{tag}_sbb", [128, 16], F32)
    ps0 = bufs["ps_mm0"]
    kv0 = bufs["ps_mm1"][:, 0:65]
    kv1 = bufs["ps_rb"][:, 0:65]
    vre = vtm.rearrange("p (tc h c) -> p tc h c", h=16, c=65)

    t1 = p.op("tensor", lambda t: t.matmul(bufs["ps_mrb"][:, 0:16], bufs["ones_row_f"],
                                           bufs["stabk_row_sb"], start=True, stop=True),
              waits=dep + bufs["stabk_ready"] + bufs["consts_ready"])
    t2 = p.op("vector", lambda v: v.tensor_copy(sbb[:, :], bufs["ps_mrb"][:, 0:16]), waits=[t1])
    t3 = p.op("vector", lambda v: v.tensor_sub(
        dkt.rearrange("p (tc h) -> p tc h", h=16),
        diagk.rearrange("p (tc h) -> p tc h", h=16),
        sbb[:, :].unsqueeze(1).broadcast_to((128, 16, 16))), waits=[t2] + dep)

    kv_toks = []
    ev_toks = []
    for h in range(16):
        g, off = h // 2, 64 * (h % 2)
        htoks = []
        for tc in range(16):
            def mmu(t, g=g, off=off, h=h, tc=tc):
                return t.matmul(ps0[:, 0:128], k_t[g][off:off + 64, 128 * tc:128 * (tc + 1)],
                                bufs["pmT_rhs"][h], start=True, stop=True)
            mt = p.op("tensor", mmu, waits=[t3] + htoks[-2:] + ev_toks[-1:] + bufs["pm_ready"] + dep)
            e1 = p.op("scalar", lambda s, h=h, tc=tc: s.activation(
                kf[:, 0:128], ps0[:, 0:128], AF.Exp, bias=dkt[:, 16 * tc + h:16 * tc + h + 1],
                scale=0.125), waits=[mt])
            e2 = p.op("scalar", lambda s, h=h, tc=tc: s.activation(
                kf[:, 128:256], ps0[:, 0:128], AF.Exp, bias=dkt[:, 16 * tc + h:16 * tc + h + 1],
                scale=-0.125), waits=[mt])
            def mk0(t, h=h, tc=tc):
                return t.matmul(kv0, kf[:, 0:128], vre[:, tc, h, :], start=(tc == 0), stop=(tc == 15))
            def mk1(t, h=h, tc=tc):
                return t.matmul(kv1, kf[:, 128:256], vre[:, tc, h, :], start=(tc == 0), stop=(tc == 15))
            m1 = p.op("tensor", mk0, waits=[e1, e2])
            m2 = p.op("tensor", mk1, waits=[e1, e2])
            htoks = [m1, m2]
        c1 = p.op("vector", lambda v, h=h: v.tensor_copy(out["kvloc"][:, 130 * h:130 * h + 65], kv0),
                  waits=htoks)
        c2 = p.op("vector", lambda v, h=h: v.tensor_copy(out["kvloc"][:, 130 * h + 65:130 * h + 130], kv1),
                  waits=htoks)
        ev_toks = [c1, c2]
        kv_toks += [c1, c2]
    return kv_toks


def emit_perf_part3(p, bufs, x_tiles, dep, qin, tag="pf3"):
    q_t = qin["q"]
    kvb = qin["kvb"]
    diagq, sq_sb = qin["diagq"], qin["sq"]
    pmTb = bufs["pmT_sb"]
    ident = bufs["ident_sb"]
    wo = p.sbuf(f"{tag}_wo", [128, NT * 1024], BF16)
    qf = p.sbuf(f"{tag}_qf", [128, 1024], BF16)
    att = [p.sbuf(f"{tag}_att{g}", [128, T], BF16) for g in range(NT)]
    abuf = p.sbuf(f"{tag}_abuf", [64, 512], F32)
    zrow = p.sbuf(f"{tag}_zrow", [1, 512], F32)
    zrec = p.sbuf(f"{tag}_zrec", [1, 512], BF16)
    eps_t = p.sbuf(f"{tag}_epst", [128, 256], F32)
    eps_rows = p.sbuf(f"{tag}_epsr", [16, T], F32)
    eps_stage = p.sbuf(f"{tag}_epss", [1, T], F32)
    edram = p.nc.dram_tensor(p.name(f"{tag}_edram"), [16, T], F32).ap()
    ps0 = bufs["ps_mm0"]
    psA = bufs["ps_mm1"][:, 0:512]
    psRZ = bufs["ps_rb"][:, 0:512]
    psT = bufs["ps_mrb"]
    wor = bufs["wout_dram"].rearrange("(kt k) o -> k kt o", k=128)

    lwo = p.op("gpsimd", lambda g: g.dma_start(
        out=wo[:, :].rearrange("k (kt o) -> k kt o", o=1024), in_=wor[:, :, :]), dma=True)

    LNEPS = float(np.log(1e-6 * 256.0))
    et_toks = []
    for tc in range(16):
        s1 = p.op("vector", lambda v, tc=tc: v.scalar_tensor_tensor(
            eps_t[:, 16 * tc:16 * (tc + 1)], sq_sb[:, 16 * tc:16 * (tc + 1)], 0.125,
            diagq[:, 16 * tc:16 * (tc + 1)], ALU.mult, ALU.subtract), waits=dep)
        s1b = p.op("vector", lambda v, tc=tc: v.tensor_scalar_add(
            eps_t[:, 16 * tc:16 * (tc + 1)], eps_t[:, 16 * tc:16 * (tc + 1)], LNEPS), waits=[s1])
        s2 = p.op("scalar", lambda s, tc=tc: s.activation(
            eps_t[:, 16 * tc:16 * (tc + 1)], eps_t[:, 16 * tc:16 * (tc + 1)], AF.Exp),
            waits=[s1b])
        s3 = p.op("tensor", lambda t, tc=tc: t.transpose(psT[0:16, 0:128],
                                                         eps_t[:, 16 * tc:16 * (tc + 1)], ident[:, :]),
                  waits=[s2] + et_toks[-1:])
        s4 = p.op("vector", lambda v, tc=tc: v.tensor_copy(eps_rows[:, 128 * tc:128 * (tc + 1)],
                                                           psT[0:16, 0:128]), waits=[s3])
        et_toks.append(s4)
    d1 = p.op("sync", lambda s: s.dma_start(out=edram, in_=eps_rows[:, :]), waits=et_toks, dma=True)

    res_toks = []
    att_toks = []
    prev_u = []
    prev_att = []
    for h in range(16):
        g, off = h // 2, 64 * (h % 2)
        eh = p.op("sync", lambda s, h=h: s.dma_start(out=eps_stage[:, :], in_=edram[h:h + 1, :]),
                  waits=[d1] + prev_att, dma=True)
        for cc in range(4):
            csl = slice(512 * cc, 512 * (cc + 1))
            def mmu(t, g=g, off=off, h=h, cc=cc, csl=csl):
                return t.matmul(ps0[:, 0:512], bufs["pmT_lhsT"][h], q_t[g][off:off + 64, csl],
                                start=True, stop=True)
            mt = p.op("tensor", mmu, waits=prev_u + bufs["pm_ready"] + dep)
            e1 = p.op("scalar", lambda s: s.activation(qf[:, 0:512], ps0[:, 0:512], AF.Exp, scale=0.125),
                      waits=[mt])
            e2 = p.op("scalar", lambda s: s.activation(qf[:, 512:1024], ps0[:, 0:512], AF.Exp, scale=-0.125),
                      waits=[mt])
            prev_u = [e1, e2]
            def mA(t, h=h):
                t.matmul(psA[0:65, :], kvb[:, 130 * h:130 * h + 65], qf[:, 0:512], start=True, stop=False)
                return t.matmul(psA[0:65, :], kvb[:, 130 * h + 65:130 * (h + 1)], qf[:, 512:1024],
                                start=False, stop=True)
            mt2 = p.op("tensor", mA, waits=[e1, e2] + prev_att + qin["kvb_ready"])
            z1 = p.op("vector", lambda v, h=h, cc=cc: v.tensor_add(
                zrow[:, :], psA[64:65, :], eps_stage[0:1, 512 * cc:512 * (cc + 1)]),
                waits=[mt2, eh])
            z2 = p.op("vector", lambda v: v.reciprocal(zrow[:, :], zrow[:, :]), waits=[z1])
            z3 = p.op("vector", lambda v: v.tensor_copy(zrec[:, :], zrow[:, :]), waits=[z2])
            a1 = p.op("scalar", lambda s: s.activation(abuf[:, :], psA[0:64, :], AF.Copy), waits=[mt2])
            r1 = p.op("tensor", lambda t: t.matmul(psRZ[0:64, :], bufs["ones_row_bf"][0:1, 0:64],
                                                   zrec[:, :], start=True, stop=True),
                      waits=[z3] + prev_att)
            a2 = p.op("vector", lambda v, g=g, off=off, csl=csl: v.tensor_mul(
                att[g][off:off + 64, csl], abuf[:, :], psRZ[0:64, :]), waits=[a1, r1])
            prev_att = [a2]
            att_toks.append(a2)
    for ot in range(NT):
        for cc in range(4):
            csl = slice(512 * cc, 512 * (cc + 1))
            def mm(t, ot=ot, csl=csl):
                for kt in range(NT):
                    last = t.matmul(ps0[:, 0:512], wo[:, 1024 * kt + 128 * ot:1024 * kt + 128 * (ot + 1)],
                                    att[kt][:, csl], start=(kt == 0), stop=(kt == NT - 1))
                return last
            mt = p.op("tensor", mm, waits=[lwo] + att_toks + res_toks[-1:] + prev_u)
            rv_ = p.op("vector", lambda v, ot=ot, csl=csl: v.tensor_add(
                x_tiles[ot][:, csl], x_tiles[ot][:, csl], ps0[:, 0:512]), waits=[mt])
            res_toks.append(rv_)
    return res_toks



def _perf_consts(p, nc, bufs):
    pmT_in = nc.dram_tensor("pmT", [1024, 128], F32, kind="ExternalInput").ap()
    pmP_in = nc.dram_tensor("pmTpair", [128, 2048], F32, kind="ExternalInput").ap()
    seld_in = nc.dram_tensor("seld", [128, 128], F32, kind="ExternalInput").ap()
    id_in = nc.dram_tensor("ident", [128, 128], F32, kind="ExternalInput").ap()
    pmT_sb = p.sbuf("pmT", [128, NT * 128], BF16)
    pmP_sb = p.sbuf("pmP", [128, 2048], BF16)
    seld_sb = p.sbuf("seld", [128, 128], F32)
    id_sb = p.sbuf("identf", [128, 128], F32)
    t1 = p.op("gpsimd", lambda g: g.dma_start(
        out=pmT_sb[:, :].rearrange("k (j o) -> k j o", o=128),
        in_=pmT_in.rearrange("(j k) o -> k j o", k=128)), dma=True)
    t2 = p.op("gpsimd", lambda g: g.dma_start(out=pmP_sb[:, :], in_=pmP_in), dma=True)
    t3 = p.op("sync", lambda s: s.dma_start(out=seld_sb[:, :], in_=seld_in), dma=True)
    t4 = p.op("sync", lambda s: s.dma_start(out=id_sb[:, :], in_=id_in), dma=True)
    bufs["pmT_sb"] = pmT_sb
    bufs["pmT_lhsT"] = [pmT_sb[64 * (h % 2):64 * (h % 2) + 64, 128 * (h // 2):128 * (h // 2 + 1)]
                        for h in range(16)]
    bufs["pmT_rhs"] = bufs["pmT_lhsT"]
    bufs["pmTpair_sb"] = pmP_sb[:, :]
    bufs["seld_sb"] = seld_sb[:, :]
    bufs["ident_sb"] = id_sb[:, :]
    bufs["pm_ready"] = [t1, t2]
    bufs["seld_ready"] = [t3]
    bufs["ident_ready"] = [t4]
    return [t1, t2, t3, t4]


def build_stage_a1(phases=("qkv", "v", "kstab", "qstab")):
    nc = bass.Bass("TRN2", target_bir_lowering=False)
    x_in = nc.dram_tensor("x_in", [D, T], F32, kind="ExternalInput").ap()
    wqkv = nc.dram_tensor("wqkv", [D, 3072], F32, kind="ExternalInput").ap()
    q_out = nc.dram_tensor("q_out", [D, T], BF16, kind="ExternalOutput").ap()
    k_out = nc.dram_tensor("k_out", [D, T], BF16, kind="ExternalOutput").ap()
    v_out = nc.dram_tensor("v_out", [128, 16 * 1040], BF16, kind="ExternalOutput").ap()
    dq_out = nc.dram_tensor("dq_out", [128, 256], F32, kind="ExternalOutput").ap()
    dk_out = nc.dram_tensor("dk_out", [128, 256], F32, kind="ExternalOutput").ap()
    sq_out = nc.dram_tensor("sq_out", [128, 256], F32, kind="ExternalOutput").ap()
    sk_out = nc.dram_tensor("sk_out", [16, 1], F32, kind="ExternalOutput").ap()

    bufs = {"wqkv_dram": wqkv}
    p = Prog(nc)
    make_psum_pool(p, bufs)
    make_ln_shared(p, bufs)
    setup_common(p, nc, bufs)
    _perf_consts(p, nc, bufs)

    xs = p.sbuf("x_stream", [128, NT * 512], F32)
    xq_tiles = [xs[:, g * 512:(g + 1) * 512] for g in range(NT)]

    out = {
        "q": [p.sbuf(f"qfm{g}", [128, T], BF16)[:, :] for g in range(NT)],
        "k": [p.sbuf(f"kfm{g}", [128, T], BF16)[:, :] for g in range(NT)],
        "vtm": p.sbuf("vtm", [128, 16 * 1040], BF16)[:, :],
        "diagq": p.sbuf("diagq", [128, 256], F32)[:, :],
        "diagk": p.sbuf("diagk", [128, 256], F32)[:, :],
        "sq": p.sbuf("sq", [128, 256], F32)[:, :],
        "skcol": p.sbuf("skcol", [16, 1], F32)[:, :],
    }
    bufs["x_in_dram"] = x_in
    toks = emit_perf_part1(p, bufs, xq_tiles, bufs["consts_ready"], out, phases=phases)
    st = []
    for g in range(NT):
        st.append(p.op("sync", lambda s, g=g: s.dma_start(out=q_out[128 * g:128 * (g + 1), :], in_=out["q"][g]), waits=toks, dma=True))
        st.append(p.op("sync", lambda s, g=g: s.dma_start(out=k_out[128 * g:128 * (g + 1), :], in_=out["k"][g]), waits=toks, dma=True))
    st.append(p.op("sync", lambda s: s.dma_start(out=v_out, in_=out["vtm"]), waits=toks, dma=True))
    st.append(p.op("sync", lambda s: s.dma_start(out=dq_out, in_=out["diagq"]), waits=toks, dma=True))
    st.append(p.op("sync", lambda s: s.dma_start(out=dk_out, in_=out["diagk"]), waits=toks, dma=True))
    st.append(p.op("sync", lambda s: s.dma_start(out=sq_out, in_=out["sq"]), waits=toks, dma=True))
    st.append(p.op("sync", lambda s: s.dma_start(out=sk_out, in_=out["skcol"]), waits=toks, dma=True))
    p.op("gpsimd", lambda g: g.engine_nop(), waits=st)
    p.finalize()
    return nc


def build_stage_a2():
    nc = bass.Bass("TRN2", target_bir_lowering=False)
    k_in = nc.dram_tensor("k_in", [D, T], BF16, kind="ExternalInput").ap()
    v_in = nc.dram_tensor("v_in", [128, 16 * 1040], BF16, kind="ExternalInput").ap()
    dk_in = nc.dram_tensor("dk_in", [128, 256], F32, kind="ExternalInput").ap()
    skall = nc.dram_tensor("skall", [16, 8], F32, kind="ExternalInput").ap()
    kv_out = nc.dram_tensor("kv_out", [128, 2080], F32, kind="ExternalOutput").ap()

    bufs = {}
    p = Prog(nc)
    make_psum_pool(p, bufs)
    setup_common(p, nc, bufs)
    _perf_consts(p, nc, bufs)

    k_t = [p.sbuf(f"kfm{g}", [128, T], BF16)[:, :] for g in range(NT)]
    vtm = p.sbuf("vtm", [128, 16 * 1040], BF16)[:, :]
    dk = p.sbuf("diagk", [128, 256], F32)[:, :]
    skb = p.sbuf("skall", [16, 8], F32)
    skcol = p.sbuf("skcol", [16, 1], F32)
    skrow = p.sbuf("skrow", [1, 16], F32)
    skdram = nc.dram_tensor("skd", [1, 16], F32).ap()
    lds = [p.op("sync", lambda s, g=g: s.dma_start(out=k_t[g], in_=k_in[128 * g:128 * (g + 1), :]), dma=True)
           for g in range(NT)]
    lds.append(p.op("sync", lambda s: s.dma_start(out=vtm, in_=v_in), dma=True))
    lds.append(p.op("sync", lambda s: s.dma_start(out=dk, in_=dk_in), dma=True))
    l1 = p.op("sync", lambda s: s.dma_start(out=skb[:, :], in_=skall), dma=True)
    r1 = p.op("vector", lambda v: v.reduce_max(skcol[:, :], skb[:, :], axis=AX.X), waits=[l1])
    r1 = p.op("vector", lambda v: v.tensor_scalar_mul(skcol[:, :], skcol[:, :], 0.125), waits=[r1])
    d1 = p.op("sync", lambda s: s.dma_start(out=skdram.rearrange("one h -> h one"), in_=skcol[:, :]), waits=[r1], dma=True)
    d2 = p.op("sync", lambda s: s.dma_start(out=skrow[:, :], in_=skdram), waits=[d1], dma=True)
    bufs["stabk_row_sb"] = skrow[:, :]
    bufs["stabk_ready"] = [d2]

    kvloc = p.sbuf("kvloc", [128, 2080], F32)
    toks = emit_perf_part2(p, bufs, lds, {"k": k_t, "vtm": vtm, "diagk": dk}, {"kvloc": kvloc[:, :]})
    st = [p.op("sync", lambda s: s.dma_start(out=kv_out, in_=kvloc[:, :]), waits=toks, dma=True)]
    p.op("gpsimd", lambda g: g.engine_nop(), waits=st)
    p.finalize()
    return nc


def build_stage_a3():
    nc = bass.Bass("TRN2", target_bir_lowering=False)
    x_in = nc.dram_tensor("x_in", [D, T], F32, kind="ExternalInput").ap()
    q_in = nc.dram_tensor("q_in", [D, T], BF16, kind="ExternalInput").ap()
    dq_in = nc.dram_tensor("dq_in", [128, 256], F32, kind="ExternalInput").ap()
    sq_in = nc.dram_tensor("sq_in", [128, 256], F32, kind="ExternalInput").ap()
    kvall = nc.dram_tensor("kvall", [NCORES * 128, 2080], F32, kind="ExternalInput").ap()
    wout = nc.dram_tensor("wout", [D, D], F32, kind="ExternalInput").ap()
    x_out = nc.dram_tensor("x_out", [D, T], F32, kind="ExternalOutput").ap()

    bufs = {"wout_dram": wout}
    p = Prog(nc)
    make_psum_pool(p, bufs)
    setup_common(p, nc, bufs)
    _perf_consts(p, nc, bufs)

    xs = p.sbuf("x_res", [128, NT * T], F32)
    x_tiles = [xs[:, g * T:(g + 1) * T] for g in range(NT)]
    xloads = [p.op("sync", lambda s, g=g: s.dma_start(out=x_tiles[g], in_=x_in[128 * g:128 * (g + 1), :]), dma=True)
              for g in range(NT)]
    q_t = [p.sbuf(f"qfm{g}", [128, T], BF16)[:, :] for g in range(NT)]
    dq = p.sbuf("diagq", [128, 256], F32)[:, :]
    sq = p.sbuf("sq", [128, 256], F32)[:, :]
    lds = [p.op("sync", lambda s, g=g: s.dma_start(out=q_t[g], in_=q_in[128 * g:128 * (g + 1), :]), dma=True)
           for g in range(NT)]
    lds.append(p.op("sync", lambda s: s.dma_start(out=dq, in_=dq_in), dma=True))
    lds.append(p.op("sync", lambda s: s.dma_start(out=sq, in_=sq_in), dma=True))
    kvb_f = p.sbuf("kvbf", [128, 2080], F32)
    kvstg = p.sbuf("kvstg", [128, 2080], F32)
    kvb = p.sbuf("kvb", [128, 2080], BF16)
    t0 = p.op("sync", lambda s: s.dma_start(out=kvb_f[:, :], in_=kvall[0:128, :]), dma=True)
    acc = [t0]
    for c in range(1, NCORES):
        lc = p.op("sync", lambda s, c=c: s.dma_start(out=kvstg[:, :], in_=kvall[128 * c:128 * (c + 1), :]),
                  waits=acc[-1:], dma=True)
        ac = p.op("vector", lambda v: v.tensor_add(kvb_f[:, :], kvb_f[:, :], kvstg[:, :]), waits=[lc] + acc[-1:])
        acc.append(ac)
    cb = p.op("vector", lambda v: v.tensor_copy(kvb[:, :], kvb_f[:, :]), waits=acc[-1:])

    qin = {"q": q_t, "diagq": dq, "sq": sq, "kvb": kvb[:, :], "kvb_ready": [cb]}
    toks = emit_perf_part3(p, bufs, x_tiles, xloads + lds + bufs["consts_ready"], qin)
    st = [p.op("sync", lambda s, g=g: s.dma_start(out=x_out[128 * g:128 * (g + 1), :], in_=x_tiles[g]),
               waits=toks, dma=True) for g in range(NT)]
    p.op("gpsimd", lambda g: g.engine_nop(), waits=st)
    p.finalize()
    return nc


def perf_host_consts():
    ident = np.eye(128, dtype=np.float32)
    seld = np.zeros((128, 128), np.float32)
    for g in range(8):
        for r in range(128):
            h = 2 * g + (r >= 64)
            seld[r, 16 * g + h] = -0.5
    return ident, seld



def make_psum_pool_wide(p, bufs):
    bufs["ps_wide"] = p.psum("ps_wide", [128, 1024])[:, :]
    bufs["ps_wide2"] = p.psum("ps_wide2", [128, 1024])[:, :]
    bufs["ps_mm0"] = p.psum("ps_mm0", [128, 512])[:, :]
    bufs["ps_x0"] = p.psum("ps_x0", [128, 512])[:, :]


def emit_l1f_qkv(p, bufs, xin_tiles, dep, out, tag="dq"):
    wq = p.sbuf(f"{tag}_wq", [128, NT * 1536], BF16)
    q2c = p.sbuf(f"{tag}_q2", [128, 512], F32)
    qb = [p.sbuf(f"{tag}_qb{g}", [128, T], BF16) for g in range(4)]
    kb = [p.sbuf(f"{tag}_kb{g}", [128, T], BF16) for g in range(4)]
    rstd2 = p.sbuf(f"{tag}_rs", [128, 256], F32)
    rrow = p.sbuf(f"{tag}_rrow", [1, 1024], F32)
    qn = p.sbuf(f"{tag}_qn", [128, 512], F32)
    qnb = p.sbuf(f"{tag}_qnb", [128, 512], BF16)
    t1b = p.sbuf(f"{tag}_t1", [128, 512], F32)
    rdram = p.nc.dram_tensor(p.name(f"{tag}_rd"), [2, 128, 128], F32).ap()
    ps0 = bufs["ps_mm0"]
    psN = bufs["ps_mm1"]
    psB = bufs["ps_x0"]
    selh = bufs["selh2_sb"]
    permT = bufs["permT_sb"]
    wqr = bufs["l1_wqkv_dram"].rearrange("(kt k) o -> k kt o", k=128)

    lwq = p.op("gpsimd", lambda g: g.dma_start(
        out=wq[:, :].rearrange("k (kt o) -> k kt o", o=1536), in_=wqr[:, :, :]),
        waits=bufs.pop("wq_free", []), dma=True)

    toks = []
    v_toks = []
    prev_read = []
    nrm_done = []
    tzr = p.op("vector", lambda v: v.memset(rstd2[:, :], 0.0))
    for part in range(2):
        dst = qb if part == 0 else kb
        for cc in range(4):
            csl = slice(512 * cc, 512 * (cc + 1))
            grp = [tzr]
            for ot in range(4):
                def mm2(t, part=part, ot=ot, cc=cc):
                    for kt in range(NT):
                        base = 1536 * kt + 512 * part + 128 * ot
                        off = 1 if kt < 4 else 0
                        rhs = xin_tiles[kt][:, 512 * cc + off:512 * cc + off + 512]
                        last = t.matmul(ps0, wq[:, base:base + 128], rhs,
                                        start=(kt == 0), stop=(kt == NT - 1))
                    return last
                mt = p.op("tensor", mm2, waits=[lwq] + list(dep) + prev_read)
                ev = p.op("vector", lambda v, ot=ot, csl=csl, dst=dst: v.tensor_copy(dst[ot][:, csl], ps0),
                          waits=[mt] + toks[-1:])
                sq_ = p.op("scalar", lambda s, ot=ot, csl=csl, dst=dst: s.activation(
                    q2c[:, :], dst[ot][:, csl], AF.Square), waits=[ev] + grp[-1:])
                def mmn(t, part=part, cc=cc, ot=ot):
                    for sc in range(4):
                        co = 128 * part + 32 * cc + 8 * sc
                        last = t.matmul(psN[:, co:co + 8],
                                        q2c[:, 128 * sc:128 * (sc + 1)], selh[:, 8 * ot:8 * (ot + 1)],
                                        start=True, stop=True)
                    return last
                dt_ = p.op("tensor", mmn, waits=[sq_] + bufs["selh_ready"] + grp[-1:])
                co0 = 128 * part + 32 * cc
                ac_ = p.op("vector", lambda v, co0=co0: v.tensor_add(
                    rstd2[:, co0:co0 + 32], rstd2[:, co0:co0 + 32], psN[:, co0:co0 + 32]),
                    waits=[dt_, tzr])
                grp.append(ac_)
                prev_read = [sq_, ev]
                toks.append(ev)
            nrm_done += grp
    r1 = nrm_done[-1]
    r2 = p.op("scalar", lambda s: s.activation(rstd2[:, :], rstd2[:, :], AF.Sqrt), waits=nrm_done)
    r3 = p.op("vector", lambda v: v.tensor_scalar_max(rstd2[:, :], rstd2[:, :], 1e-12), waits=[r2])
    r4 = p.op("vector", lambda v: v.reciprocal(rstd2[:, :], rstd2[:, :]), waits=[r3])
    tr1 = p.op("tensor", lambda t: t.transpose(psB[:, 0:128], rstd2[:, 0:128],
                                               bufs["ident_sb"]), waits=[r4] + bufs["ident_ready"])
    te1 = p.op("vector", lambda v: v.tensor_copy(qn[:, 0:128], psB[:, 0:128]), waits=[tr1])
    td1 = p.op("sync", lambda s: s.dma_start(out=rdram[0], in_=qn[:, 0:128]), waits=[te1], dma=True)
    tr2 = p.op("tensor", lambda t: t.transpose(psB[:, 0:128], rstd2[:, 128:256],
                                               bufs["ident_sb"]), waits=[te1])
    te2 = p.op("vector", lambda v: v.tensor_copy(qn[:, 0:128], psB[:, 0:128]), waits=[tr2, td1])
    td2 = p.op("sync", lambda s: s.dma_start(out=rdram[1], in_=qn[:, 0:128]), waits=[te2], dma=True)
    rows = [rdram[i].rearrange("(tc h) u -> h tc u", h=8) for i in range(2)]

    for tc in range(16):
        def mmv(t, tc=tc):
            for kt in range(NT):
                base = 1536 * kt + 1024
                off = 1 if kt < 4 else 0
                lhsT = xin_tiles[kt][:, 128 * tc + off:128 * tc + off + 128]
                last = t.matmul(ps0, lhsT, wq[:, base:base + 512],
                                start=(kt == 0), stop=(kt == NT - 1))
            return last
        mt = p.op("tensor", mmv, waits=[lwq] + list(dep) + prev_read + [te2])
        ev = p.op("vector", lambda v, tc=tc: v.tensor_copy(out["vtm"][:, 512 * tc:512 * (tc + 1)], ps0),
                  waits=[mt] + v_toks[-1:])
        prev_read = [ev]
        v_toks.append(ev)
    bufs["wq_free"] = [v_toks[-1]]

    out["qrot"] = [t[:, :] for t in qb]
    out["krot"] = [t[:, :] for t in kb]
    csbuf = p.sbuf(f"{tag}_cs", [128, 512], F32)
    ssbuf = p.sbuf(f"{tag}_ss", [128, 512], F32)
    rope_toks = []
    ld_prev = []
    for part in range(2):
        srcb = qb if part == 0 else kb
        CSd = bufs["cssq_dram"] if part == 0 else bufs["csik_dram"]
        SSd = bufs["sssq_dram"] if part == 0 else bufs["ssik_dram"]
        scol = bufs["qs2_col"] if part == 0 else bufs["ks2_col"]
        tdx = td1 if part == 0 else td2
        for cc in range(4):
            csl = slice(512 * cc, 512 * (cc + 1))
            lcs = p.op("sync", lambda s, csl=csl, CSd=CSd: s.dma_start(out=csbuf[:, :], in_=CSd[:, csl]),
                       waits=rope_toks[-1:], dma=True)
            lss = p.op("sync", lambda s, csl=csl, SSd=SSd: s.dma_start(out=ssbuf[:, :], in_=SSd[:, csl]),
                       waits=rope_toks[-1:], dma=True)
            for g in range(4):
                l1 = p.op("sync", lambda s, g=g, cc=cc, part=part: s.dma_start(
                    out=rrow[0:1, 0:512].rearrange("a (tc u) -> a tc u", u=128),
                    in_=rows[part][2 * g:2 * g + 1, 4 * cc:4 * (cc + 1), :]),
                    waits=[tdx] + ld_prev, dma=True)
                l2 = p.op("sync", lambda s, g=g, cc=cc, part=part: s.dma_start(
                    out=rrow[0:1, 512:1024].rearrange("a (tc u) -> a tc u", u=128),
                    in_=rows[part][2 * g + 1:2 * g + 2, 4 * cc:4 * (cc + 1), :]),
                    waits=[tdx], dma=True)
                b1 = p.op("tensor", lambda t: t.matmul(psB[0:64, :], bufs["ones_row_f"][0:1, 0:64],
                                                       rrow[0:1, 0:512], start=True, stop=True),
                          waits=[l1, l2] + rope_toks[-1:])
                b2 = p.op("tensor", lambda t: t.matmul(psB[64:128, :], bufs["ones_row_f"][0:1, 0:64],
                                                       rrow[0:1, 512:1024], start=True, stop=True,
                                                       tile_position=(0, 64)), waits=[l1, l2, b1])
                qkmul = 8.0 if part == 0 else 1.0
                s1 = p.op("vector", lambda v, g=g, csl=csl, srcb=srcb, scol=scol, qkmul=qkmul: v.tensor_scalar(
                    qn[:, :], srcb[g][:, csl], scol, qkmul, ALU.mult, ALU.mult),
                    waits=toks + rope_toks[-1:] + bufs["l1cols_ready"] + [td1, td2])
                s2 = p.op("vector", lambda v: v.tensor_copy(qnb[:, :], qn[:, :]), waits=[s1])
                pm = p.op("tensor", lambda t: t.matmul(psN[:, 0:512], permT[:, :], qnb[:, :],
                                                       start=True, stop=True),
                          waits=[s2, b2] + bufs["perm_ready"] + [r2])
                s3 = p.op("vector", lambda v: v.tensor_mul(t1b[:, :], qn[:, :], csbuf[:, :]),
                          waits=[s1, lcs])
                s4 = p.op("vector", lambda v: v.tensor_mul(qn[:, :], psN[:, 0:512], ssbuf[:, :]),
                          waits=[pm, s3, lss])
                s5 = p.op("vector", lambda v: v.tensor_add(t1b[:, :], t1b[:, :], qn[:, :]), waits=[s4])
                s6 = p.op("vector", lambda v, g=g, csl=csl, srcb=srcb: v.tensor_mul(
                    srcb[g][:, csl], t1b[:, :], psB[:, :]), waits=[s5, b2])
                ld_prev = [s6]
                rope_toks.append(s6)
    return toks + v_toks + rope_toks


def emit_l1f_attn(p, bufs, xmid_tiles, dep, ain, tag="da"):
    qrot, krotF, vtmF = ain["qrot"], ain["krotF"], ain["vtmF"]
    M0, M4 = bufs["m0_sb"], bufs["m4_sb"]
    brow = bufs["biasrow_sb"]
    edgeT = bufs["edgeT_sb"]
    wo_sb = p.sbuf(f"{tag}_wo", [128, 4 * 1024], BF16)
    mst = p.sbuf(f"{tag}_mst", [128, 16], F32)
    dsum = p.sbuf(f"{tag}_dsum", [128, 16], F32)
    nmcol = p.sbuf(f"{tag}_nm", [128, 1], F32)
    scr = p.sbuf(f"{tag}_scr", [128, 640], BF16)
    expb = p.sbuf(f"{tag}_expb", [128, 5 * 640], BF16)
    att = [p.sbuf(f"{tag}_att{g}", [128, T], BF16) for g in range(4)]
    mldt = p.sbuf(f"{tag}_mldt", [128, 16], F32)
    mldtc = p.sbuf(f"{tag}_mldtc", [128, 16], BF16)
    mldtf = p.sbuf(f"{tag}_mldtf", [128, 16], F32)
    mrow_c = p.sbuf(f"{tag}_mrc", [1, T], BF16)
    mrow_f = p.sbuf(f"{tag}_mrf", [1, T], BF16)
    tposbuf = p.sbuf(f"{tag}_tp", [16, 128], F32)
    mdram = p.nc.dram_tensor(p.name(f"{tag}_md"), [2, 16, 128], F32).ap()
    psW = bufs["ps_wide"]
    psW2 = bufs["ps_wide2"]
    psPV = bufs["ps_x0"]
    psT = bufs["ps_mm0"]
    wor = bufs["l1_wo_dram"].rearrange("(kt k) o -> k kt o", k=128)

    lwo = p.op("gpsimd", lambda g: g.dma_start(
        out=wo_sb[:, :].rearrange("k (kt o) -> k kt o", o=1024), in_=wor[:, :, :]),
        waits=bufs.pop("wo_free", []), dma=True)

    att_toks = []
    h_prev = []
    for h in range(8):
        g, off = h // 2, 64 * (h % 2)
        p1 = []
        for qt in range(16):
            def mm1(t, g=g, off=off, qt=qt):
                last = t.matmul(psW[:, 0:512], qrot[g][off:off + 64, 128 * qt:128 * (qt + 1)],
                                krotF[g][off:off + 64, 128 * qt:128 * qt + 512],
                                start=True, stop=(qt >= 4))
                if qt < 4:
                    last = t.matmul(psW[:, 0:512], bufs["ones_row_bf"][0:1, :],
                                    brow[0:1, 640 * qt:640 * qt + 512], start=False, stop=True)
                last = t.matmul(psW[:, 512:640], qrot[g][off:off + 64, 128 * qt:128 * (qt + 1)],
                                krotF[g][off:off + 64, 128 * qt + 512:128 * qt + 640],
                                start=True, stop=(qt >= 4))
                if qt < 4:
                    last = t.matmul(psW[:, 512:640], bufs["ones_row_bf"][0:1, :],
                                    brow[0:1, 640 * qt + 512:640 * qt + 640], start=False, stop=True)
                return last
            mt = p.op("tensor", mm1, waits=list(dep) + p1[-1:] + h_prev + bufs["mask_ready"] + bufs["consts_ready"])
            mk1 = p.op("vector", lambda v: v.tensor_add(psW[:, 0:128], psW[:, 0:128], M0[:, :]), waits=[mt])
            mk2 = p.op("vector", lambda v: v.tensor_add(psW[:, 512:640], psW[:, 512:640], M4[:, :]),
                       waits=[mt, mk1])
            rd = p.op("vector", lambda v, qt=qt: v.reduce_max(mst[:, qt:qt + 1], psW[:, 0:640], axis=AX.X),
                      waits=[mk2] + h_prev)
            ng = p.op("vector", lambda v, qt=qt: v.tensor_scalar_mul(nmcol[:, :], mst[:, qt:qt + 1], -1.0),
                      waits=[rd])
            ex = p.op("scalar", lambda s, qt=qt: s.activation(scr[:, :], psW[:, 0:640], AF.Exp,
                                                              bias=nmcol[:, :],
                                                              accum_out=dsum[:, qt:qt + 1]), waits=[ng])
            p1.append(ex)
        if h == 0 and "dbg_attn" in bufs:
            dd = bufs["dbg_attn"]
            dt1 = p.op("gpsimd", lambda s: s.dma_start(out=dd["mst"], in_=mst[:, :]), waits=p1, dma=True)
            dt2 = p.op("gpsimd", lambda s: s.dma_start(out=dd["dsum"], in_=dsum[:, :]), waits=p1, dma=True)
            h_prev = h_prev + [dt1, dt2]
        c1 = p.op("scalar", lambda s: s.activation(mldt[:, :], dsum[:, :], AF.Ln), waits=p1)
        c2 = p.op("vector", lambda v: v.tensor_add(mldt[:, :], mldt[:, :], mst[:, :]), waits=[c1])
        c3 = p.op("vector", lambda v: v.tensor_scalar_mul(mldt[:, :], mldt[:, :], -1.0), waits=[c2])
        c4 = p.op("vector", lambda v: v.tensor_copy(mldtc[:, :], mldt[:, :]), waits=[c3])
        c5 = p.op("vector", lambda v: v.tensor_copy(mldtf[:, :], mldtc[:, :]), waits=[c4])
        c6 = p.op("vector", lambda v: v.tensor_sub(mldtf[:, :], mldt[:, :], mldtf[:, :]), waits=[c5])
        t1 = p.op("tensor", lambda t: t.transpose(psT[0:16, 0:128], mldtc[:, :].bitcast(BF16) if False
                                                  else mldt[:, :], bufs["ident_sb"]),
                  waits=[c6] + h_prev + bufs["ident_ready"])
        e1 = p.op("vector", lambda v: v.tensor_copy(tposbuf[:, :], psT[0:16, 0:128]), waits=[t1])
        d1 = p.op("sync", lambda s: s.dma_start(out=mdram[0], in_=tposbuf[:, :]), waits=[e1], dma=True)
        t2 = p.op("tensor", lambda t: t.transpose(psT[0:16, 0:128], mldtf[:, :], bufs["ident_sb"]),
                  waits=[e1, c6])
        e2 = p.op("vector", lambda v: v.tensor_copy(tposbuf[:, :], psT[0:16, 0:128]), waits=[t2, d1])
        d2 = p.op("sync", lambda s: s.dma_start(out=mdram[1], in_=tposbuf[:, :]), waits=[e2], dma=True)
        lc = p.op("gpsimd", lambda g_: g_.dma_start(out=mrow_c[:, :],
                                                    in_=mdram[0].rearrange("q u -> (q u)").unsqueeze(0)),
                  waits=[d1], dma=True)
        lf = p.op("gpsimd", lambda g_: g_.dma_start(out=mrow_f[:, :],
                                                    in_=mdram[1].rearrange("q u -> (q u)").unsqueeze(0)),
                  waits=[d2], dma=True)
        p2 = {}
        pv_toks = []
        for j in range(20):
            qlo = max(0, 128 * j - 512)
            qhi = min(T, 128 * j + 128)
            w = qhi - qlo
            ring = expb[:, 640 * (j % 5):640 * (j % 5) + w]
            def mm2(t, g=g, off=off, j=j, qlo=qlo, qhi=qhi, w=w):
                first = True
                for c0 in range(0, w, 512):
                    cw = min(512, w - c0)
                    t.matmul(psW2[:, c0:c0 + cw], krotF[g][off:off + 64, 128 * j:128 * (j + 1)],
                             qrot[g][off:off + 64, qlo + c0:qlo + c0 + cw], start=True, stop=False)
                    t.matmul(psW2[:, c0:c0 + cw], bufs["ones_row_bf"][0:1, :],
                             mrow_c[0:1, qlo + c0:qlo + c0 + cw], start=False, stop=False)
                    last = t.matmul(psW2[:, c0:c0 + cw], bufs["ones_row_bf"][0:1, :],
                                    mrow_f[0:1, qlo + c0:qlo + c0 + cw], start=False, stop=True)
                return last
            mt = p.op("tensor", mm2, waits=[lc, lf] + p2.get(j - 1, []) + pv_toks[-1:] + bufs["consts_ready"])
            wl = []
            qb0 = j - 4
            if 0 <= qb0 < 16:
                co = 128 * qb0 - qlo
                wl.append(p.op("vector", lambda v, co=co: v.tensor_add(
                    psW2[:, co:co + 128], psW2[:, co:co + 128], M0[:, :]), waits=[mt]))
            if j < 16:
                co = 128 * j - qlo
                wl.append(p.op("vector", lambda v, co=co: v.tensor_add(
                    psW2[:, co:co + 128], psW2[:, co:co + 128], M4[:, :]), waits=[mt] + wl[-1:]))
            if j < 4:
                wl.append(p.op("vector", lambda v, j=j: v.tensor_scalar_add(
                    psW2[:, 0:w], psW2[:, 0:w], edgeT[:, j:j + 1]), waits=[mt] + wl[-1:]))
            ex = p.op("scalar", lambda s, ring=ring, w=w: s.activation(ring, psW2[:, 0:w], AF.Exp),
                      waits=[mt] + wl)
            p2[j] = [ex]
            qt = j - 4
            if 0 <= qt < 16:
                def mpv(t, qt=qt, g=g, off=off, h=h):
                    for i in range(5):
                        jj = qt + i
                        qlo_j = max(0, 128 * jj - 512)
                        co = 128 * qt - qlo_j
                        rr = expb[:, 640 * (jj % 5) + co:640 * (jj % 5) + co + 128]
                        last = t.matmul(psPV[0:64, 0:128], vtmF[:, 512 * jj + 64 * h:512 * jj + 64 * h + 64],
                                        rr, start=(i == 0), stop=(i == 4))
                    return last
                deps = [p2[qt + i][0] for i in range(5)]
                mt2 = p.op("tensor", mpv, waits=deps + pv_toks[-1:])
                ev = p.op("vector", lambda v, qt=qt, g=g, off=off: v.tensor_copy(
                    att[g][off:off + 64, 128 * qt:128 * (qt + 1)], psPV[0:64, 0:128]), waits=[mt2])
                pv_toks.append(ev)
        h_prev = pv_toks[-1:]
        att_toks += pv_toks
    if "dbg_attn" in bufs:
        dd = bufs["dbg_attn"]
        for g in range(4):
            p.op("gpsimd", lambda s, g=g: s.dma_start(out=dd["att"][128 * g:128 * (g + 1), :], in_=att[g][:, :]),
                 waits=att_toks, dma=True)
    res_toks = []
    for ot in range(NT):
        for cc in range(4):
            csl = slice(512 * cc, 512 * (cc + 1))
            def mmo(t, ot=ot, csl=csl):
                for kt in range(4):
                    last = t.matmul(psT, wo_sb[:, 1024 * kt + 128 * ot:1024 * kt + 128 * (ot + 1)],
                                    att[kt][:, csl], start=(kt == 0), stop=(kt == 3))
                return last
            mt = p.op("tensor", mmo, waits=[lwo] + att_toks + res_toks[-1:] + h_prev)
            rv_ = p.op("vector", lambda v, ot=ot, csl=csl: v.tensor_add(
                xmid_tiles[ot][:, csl.start + 1:csl.stop + 1],
                xmid_tiles[ot][:, csl.start + 1:csl.stop + 1], psT), waits=[mt])
            res_toks.append(rv_)
    bufs["wo_free"] = res_toks[-1:]
    return res_toks


def emit_l1f_ffn(p, bufs, xmid_tiles, dep, final_add=None, tag="df"):
    lg, lb = bufs["l1_lng_col"], bufs["l1_lnb_col"]
    b1c, b2c = bufs["l1_b1_col"], bufs["l1_b2_col"]
    lnb = [p.sbuf(f"{tag}_ln{g}", [128, 512], BF16) for g in range(NT)]
    w1s = [p.sbuf(f"{tag}_w1_{i}", [128, NT * 128], BF16) for i in range(4)]
    w2s = [p.sbuf(f"{tag}_w2_{i}", [128, 22 * 128], BF16) for i in range(2)]
    abuf = p.sbuf(f"{tag}_a", [128, 512], F32)
    gbuf = p.sbuf(f"{tag}_g", [128, 512], F32)
    prodb = p.sbuf(f"{tag}_pr", [128, 22 * 512], BF16)
    xo = p.sbuf(f"{tag}_xo", [128, NT * 512], F32) if final_add else None
    w1r = bufs["l1_w1_dram"].rearrange("(kt k) o -> k kt o", k=128)
    w2r = bufs["l1_w2_dram"].rearrange("(kt k) o -> k kt o", k=128)
    ps = bufs["ps_mm0"]
    ps2 = bufs["ps_mm1"]
    mmh = {}
    x_toks = list(dep)
    xsh = [xmid_tiles[g][:, (1 if g < 4 else 0):(2049 if g < 4 else 2048)] for g in range(NT)]
    for tt in range(4):
        tsl = slice(512 * tt, 512 * (tt + 1))
        ln_toks = emit_ln_ttile(p, bufs, xsh, tsl, lnb, slice(0, 512), x_toks[-NT:] or x_toks,
                                gain_col=[lg[:, g:g + 1] for g in range(NT)],
                                bias_col=[lb[:, g:g + 1] for g in range(NT)],
                                tag=f"{tag}_ln{tt}")
        prod_toks = []
        for ot in range(22):
            wa, wg = w1s[ot % 2], w1s[2 + ot % 2]
            la = p.op("gpsimd", lambda s, ot=ot, wa=wa: s.dma_start(
                out=wa[:, :].rearrange("k (kt o) -> k kt o", o=128),
                in_=w1r[:, :, 128 * ot:128 * (ot + 1)]), waits=[mmh.get(("a", ot % 2))], dma=True)
            lg_ = p.op("gpsimd", lambda s, ot=ot, wg=wg: s.dma_start(
                out=wg[:, :].rearrange("k (kt o) -> k kt o", o=128),
                in_=w1r[:, :, 2816 + 128 * ot:2816 + 128 * (ot + 1)]), waits=[mmh.get(("g", ot % 2))], dma=True)
            def mma(t, wa=wa):
                for kt in range(NT):
                    last = t.matmul(ps, wa[:, 128 * kt:128 * (kt + 1)], lnb[kt][:, :],
                                    start=(kt == 0), stop=(kt == NT - 1))
                return last
            def mmg(t, wg=wg):
                for kt in range(NT):
                    last = t.matmul(ps2, wg[:, 128 * kt:128 * (kt + 1)], lnb[kt][:, :],
                                    start=(kt == 0), stop=(kt == NT - 1))
                return last
            ma = p.op("tensor", mma, waits=[la] + ln_toks + prod_toks[-1:])
            mmh[("a", ot % 2)] = ma
            ea = p.op("vector", lambda v, ot=ot: v.tensor_scalar_add(abuf[:, :], ps, b1c[:, ot:ot + 1]),
                      waits=[ma])
            mg = p.op("tensor", mmg, waits=[lg_] + ln_toks + [ea])
            mmh[("g", ot % 2)] = mg
            eg = p.op("vector", lambda v, ot=ot: v.tensor_scalar_add(gbuf[:, :], ps2, b1c[:, 22 + ot:23 + ot]),
                      waits=[mg])
            gg = p.op("scalar", lambda s: s.activation(gbuf[:, :], gbuf[:, :], AF.Gelu), waits=[eg])
            pr = p.op("vector", lambda v, ot=ot: v.tensor_mul(prodb[:, 512 * ot:512 * (ot + 1)],
                                                              abuf[:, :], gbuf[:, :]), waits=[gg, ea])
            prod_toks.append(pr)
        res_toks = []
        if final_add:
            xl = [p.op("sync", lambda s, g=g, tsl=tsl: s.dma_start(
                out=xo[:, 512 * g:512 * (g + 1)],
                in_=final_add["x_outer_dram"][128 * g:128 * (g + 1), tsl]),
                waits=x_toks[-NT:], dma=True) for g in range(NT)]
        for ot in range(NT):
            wb = w2s[ot % 2]
            lt = p.op("gpsimd", lambda s, ot=ot, wb=wb: s.dma_start(
                out=wb[:, :].rearrange("k (kt o) -> k kt o", o=128),
                in_=w2r[:, :, 128 * ot:128 * (ot + 1)]), waits=[mmh.get(("w2", ot % 2))], dma=True)
            def mm2(t, wb=wb):
                for kt in range(22):
                    last = t.matmul(ps, wb[:, 128 * kt:128 * (kt + 1)], prodb[:, 512 * kt:512 * (kt + 1)],
                                    start=(kt == 0), stop=(kt == 21))
                return last
            mt = p.op("tensor", mm2, waits=[lt] + prod_toks + res_toks[-1:])
            mmh[("w2", ot % 2)] = mt
            rv_ = p.op("vector", lambda v, ot=ot, tsl=tsl: v.scalar_tensor_tensor(
                xmid_tiles[ot][:, tsl.start + 1:tsl.stop + 1], ps, b2c[:, ot:ot + 1],
                xmid_tiles[ot][:, tsl.start + 1:tsl.stop + 1], ALU.add, ALU.add), waits=[mt])
            if final_add:
                rv_ = p.op("vector", lambda v, ot=ot, tsl=tsl: v.tensor_add(
                    xo[:, 512 * ot:512 * (ot + 1)], xo[:, 512 * ot:512 * (ot + 1)],
                    xmid_tiles[ot][:, tsl.start + 1:tsl.stop + 1]), waits=[rv_, xl[ot]])
                rv_ = p.op("sync", lambda s, ot=ot, tsl=tsl: s.dma_start(
                    out=final_add["x_out_dram"][128 * ot:128 * (ot + 1), tsl],
                    in_=xo[:, 512 * ot:512 * (ot + 1)]), waits=[rv_], dma=True)
            res_toks.append(rv_)
        x_toks = res_toks
    return x_toks



def emit_token_ln_row(p, bufs, hrow, hdep, outcol_writer, tag):
    hscratch = p.sbuf(f"{tag}_hs", [1, 8], F32)
    hsq = p.sbuf(f"{tag}_hsq", [1, D], F32)
    hlnrow = p.sbuf(f"{tag}_hln", [1, D], BF16)
    hdram = p.nc.dram_tensor(p.name(f"{tag}_hd"), [1, D], BF16).ap()
    a1 = p.op("vector", lambda v: v.reduce_sum(hscratch[:, 0:1], hrow, axis=AX.X), waits=hdep)
    a2 = p.op("scalar", lambda s: s.activation(hsq[:, :], hrow, AF.Square), waits=hdep)
    a3 = p.op("vector", lambda v: v.reduce_sum(hscratch[:, 1:2], hsq[:, :], axis=AX.X), waits=[a2])
    a4 = p.op("vector", lambda v: v.tensor_scalar_mul(hscratch[:, 0:1], hscratch[:, 0:1], 1.0 / 1024.0), waits=[a1])
    a5 = p.op("vector", lambda v: v.tensor_scalar_mul(hscratch[:, 1:2], hscratch[:, 1:2], 1.0 / 1024.0), waits=[a3])
    a6 = p.op("vector", lambda v: v.tensor_mul(hscratch[:, 2:3], hscratch[:, 0:1], hscratch[:, 0:1]), waits=[a4])
    a7 = p.op("vector", lambda v: v.tensor_sub(hscratch[:, 1:2], hscratch[:, 1:2], hscratch[:, 2:3]), waits=[a5, a6])
    a8 = p.op("vector", lambda v: v.tensor_scalar_add(hscratch[:, 1:2], hscratch[:, 1:2], 1e-5), waits=[a7])
    a9 = p.op("scalar", lambda s: s.activation(hscratch[:, 1:2], hscratch[:, 1:2], AF.Ln), waits=[a8])
    a10 = p.op("scalar", lambda s: s.activation(hscratch[:, 1:2], hscratch[:, 1:2], AF.Exp, scale=-0.5), waits=[a9])
    a11 = p.op("vector", lambda v: v.tensor_scalar(hlnrow[:, :], hrow, hscratch[:, 0:1], hscratch[:, 1:2],
                                                   ALU.subtract, ALU.mult), waits=[a10] + hdep)
    a12 = p.op("sync", lambda s: s.dma_start(out=hdram, in_=hlnrow[:, :]), waits=[a11], dma=True)
    toks = [p.op("gpsimd", lambda s, g=g: outcol_writer(s, g, hdram), waits=[a12], dma=True)
            for g in range(NT)]
    return toks


def _l1_layer_consts(p, nc, bufs):
    selh = nc.dram_tensor("selh2", [128, 32], F32, kind="ExternalInput").ap()
    permT = nc.dram_tensor("permT", [128, 128], F32, kind="ExternalInput").ap()
    cssq = nc.dram_tensor("cssq", [128, T], F32, kind="ExternalInput").ap()
    sssq = nc.dram_tensor("sssq", [128, T], F32, kind="ExternalInput").ap()
    csik = nc.dram_tensor("csik", [128, T], F32, kind="ExternalInput").ap()
    ssik = nc.dram_tensor("ssik", [128, T], F32, kind="ExternalInput").ap()
    qs2 = nc.dram_tensor("qs2", [128, 1], F32, kind="ExternalInput").ap()
    ks2 = nc.dram_tensor("ks2", [128, 1], F32, kind="ExternalInput").ap()
    id_in = nc.dram_tensor("ident", [128, 128], F32, kind="ExternalInput").ap()
    s_selh = p.sbuf("selh", [128, 32], F32)
    s_perm = p.sbuf("permT", [128, 128], BF16)
    s_qs2 = p.sbuf("qs2c", [128, 1], F32)
    s_ks2 = p.sbuf("ks2c", [128, 1], F32)
    s_id = p.sbuf("identf", [128, 128], F32)
    t0 = p.op("sync", lambda s: s.dma_start(out=s_id[:, :], in_=id_in), dma=True)
    t1 = p.op("sync", lambda s: s.dma_start(out=s_selh[:, :], in_=selh), dma=True)
    t2 = p.op("gpsimd", lambda g: g.dma_start(out=s_perm[:, :], in_=permT), dma=True)
    t7 = p.op("sync", lambda s: s.dma_start(out=s_qs2[:, :], in_=qs2), dma=True)
    t8 = p.op("sync", lambda s: s.dma_start(out=s_ks2[:, :], in_=ks2), dma=True)
    bufs.update(selh2_sb=s_selh[:, :], permT_sb=s_perm[:, :],
                cssq_dram=cssq, sssq_dram=sssq, csik_dram=csik, ssik_dram=ssik,
                qs2_col=s_qs2[:, :], ks2_col=s_ks2[:, :])
    bufs["selh_ready"] = [t1]
    bufs["perm_ready"] = [t2]
    bufs["l1cols_ready"] = [t7, t8]
    bufs["ident_sb"] = s_id[:, :]
    bufs["ident_ready"] = [t0]


def build_stage_d_qkv(with_ln4):
    nc = bass.Bass("TRN2", target_bir_lowering=False)
    p = Prog(nc)
    bufs = {}
    make_psum_pool(p, bufs)
    make_ln_shared(p, bufs)
    setup_common(p, nc, bufs)
    _l1_layer_consts(p, nc, bufs)
    bufs["l1_wqkv_dram"] = nc.dram_tensor("wqkv", [D, 1536], F32, kind="ExternalInput").ap()
    q_out = nc.dram_tensor("q_out", [512, T], BF16, kind="ExternalOutput").ap()
    k_out = nc.dram_tensor("k_out", [512, T], BF16, kind="ExternalOutput").ap()
    v_out = nc.dram_tensor("v_out", [128, 16 * 512], BF16, kind="ExternalOutput").ap()

    xin = p.sbuf("xin", [128, NT * (T + 1)], BF16)
    xin_tiles = [xin[:, g * (T + 1):(g + 1) * (T + 1)] for g in range(NT)]
    if with_ln4:
        x_outer = nc.dram_tensor("x_outer", [D, T], F32, kind="ExternalInput").ap()
        halo_row = nc.dram_tensor("halo_row", [1, D], F32, kind="ExternalInput").ap()
        ln4_out = nc.dram_tensor("ln4_out", [D, T], F32, kind="ExternalOutput").ap()
        xo = p.sbuf("xo_s", [128, NT * 512], F32)
        xo_t = [xo[:, g * 512:(g + 1) * 512] for g in range(NT)]
        l4 = p.sbuf("ln4f", [128, NT * 512], F32)
        l4_t = [l4[:, g * 512:(g + 1) * 512] for g in range(NT)]
        dep = []
        for tt in range(4):
            xl = [p.op("sync", lambda s, g=g, tt=tt: s.dma_start(
                out=xo_t[g], in_=x_outer[128 * g:128 * (g + 1), 512 * tt:512 * (tt + 1)]),
                waits=dep[-3 * NT:], dma=True) for g in range(NT)]
            lt = emit_ln_ttile(p, bufs, xo_t, slice(0, 512), l4_t,
                               slice(0, 512), xl, tag=f"ln4_{tt}")
            cp = [p.op("vector", lambda v, g=g, tt=tt: v.tensor_copy(
                xin_tiles[g][:, 512 * tt + 1:512 * (tt + 1) + 1], l4_t[g]), waits=[lt[g]])
                for g in range(NT)]
            dm = [p.op("sync", lambda s, g=g, tt=tt: s.dma_start(
                out=ln4_out[128 * g:128 * (g + 1), 512 * tt:512 * (tt + 1)], in_=l4_t[g]),
                waits=[cp[g]], dma=True) for g in range(NT)]
            dep += cp + dm
        hr = p.sbuf("hrow", [1, D], F32)
        hl = p.op("sync", lambda s: s.dma_start(out=hr[:, :], in_=halo_row), dma=True)
        def wrcol(s, g, hdram):
            return s.dma_start(out=xin_tiles[g][:, 0:1], in_=hdram[0:1, 128 * g:128 * (g + 1)])
        dep0 = emit_token_ln_row(p, bufs, hr[:, :], [hl], wrcol, tag="h4")
        dep = dep + dep0
    else:
        xin_in = nc.dram_tensor("xin", [D, T + 1], F32, kind="ExternalInput").ap()
        dep = [p.op("gpsimd", lambda s, g=g: s.dma_start(out=xin_tiles[g], in_=xin_in[128 * g:128 * (g + 1), :]), dma=True)
               for g in range(NT)]

    out = {"vtm": p.sbuf("vtm", [128, 16 * 512], BF16)[:, :]}
    toks = emit_l1f_qkv(p, bufs, xin_tiles, dep, out)
    st = []
    for g in range(4):
        st.append(p.op("sync", lambda s, g=g: s.dma_start(out=q_out[128 * g:128 * (g + 1), :], in_=out["qrot"][g]), waits=toks, dma=True))
        st.append(p.op("sync", lambda s, g=g: s.dma_start(out=k_out[128 * g:128 * (g + 1), :], in_=out["krot"][g]), waits=toks, dma=True))
    st.append(p.op("sync", lambda s: s.dma_start(out=v_out, in_=out["vtm"]), waits=toks, dma=True))
    p.op("gpsimd", lambda g: g.engine_nop(), waits=st)
    p.finalize()
    return nc


def build_stage_d_attn():
    nc = bass.Bass("TRN2", target_bir_lowering=False)
    p = Prog(nc)
    bufs = {}
    make_psum_pool_wide(p, bufs)
    setup_common(p, nc, bufs)
    m0 = nc.dram_tensor("m0", [128, 128], F32, kind="ExternalInput").ap()
    m4 = nc.dram_tensor("m4", [128, 128], F32, kind="ExternalInput").ap()
    browi = nc.dram_tensor("biasrow", [1, 2560], F32, kind="ExternalInput").ap()
    edgeti = nc.dram_tensor("edgeT", [128, 4], F32, kind="ExternalInput").ap()
    identi = nc.dram_tensor("ident", [128, 128], F32, kind="ExternalInput").ap()
    s_id2 = p.sbuf("identf", [128, 128], F32)
    ti = p.op("sync", lambda s: s.dma_start(out=s_id2[:, :], in_=identi), dma=True)
    bufs["ident_sb"] = s_id2[:, :]
    bufs["ident_ready"] = [ti]
    bufs["l1_wo_dram"] = nc.dram_tensor("wo", [512, D], F32, kind="ExternalInput").ap()
    q_in = nc.dram_tensor("q_in", [512, T], BF16, kind="ExternalInput").ap()
    k_in = nc.dram_tensor("k_in", [512, T], BF16, kind="ExternalInput").ap()
    kh_in = nc.dram_tensor("kh_in", [512, 512], BF16, kind="ExternalInput").ap()
    v_in = nc.dram_tensor("v_in", [128, 16 * 512], BF16, kind="ExternalInput").ap()
    vh_in = nc.dram_tensor("vh_in", [128, 4 * 512], BF16, kind="ExternalInput").ap()
    xi_in = nc.dram_tensor("x_inner", [D, T], F32, kind="ExternalInput").ap()
    xm_out = nc.dram_tensor("x_mid", [D, T], F32, kind="ExternalOutput").ap()

    s_m0 = p.sbuf("m0", [128, 128], F32)
    s_m4 = p.sbuf("m4", [128, 128], F32)
    s_br = p.sbuf("brow", [1, 2560], BF16)
    s_et = p.sbuf("edgeT", [128, 4], F32)
    w1 = p.op("sync", lambda s: s.dma_start(out=s_m0[:, :], in_=m0), dma=True)
    w2 = p.op("sync", lambda s: s.dma_start(out=s_m4[:, :], in_=m4), dma=True)
    w3 = p.op("gpsimd", lambda g: g.dma_start(out=s_br[:, :], in_=browi), dma=True)
    w4 = p.op("sync", lambda s: s.dma_start(out=s_et[:, :], in_=edgeti), dma=True)
    bufs["m0_sb"], bufs["m4_sb"] = s_m0[:, :], s_m4[:, :]
    bufs["biasrow_sb"], bufs["edgeT_sb"] = s_br[:, :], s_et[:, :]
    bufs["mask_ready"] = [w1, w2, w3, w4]

    qrot = [p.sbuf(f"qr{g}", [128, T], BF16) for g in range(4)]
    krotF = [p.sbuf(f"krF{g}", [128, 512 + T], BF16) for g in range(4)]
    vtmF = p.sbuf("vtmF", [128, 20 * 512], BF16)
    xm = p.sbuf("xm", [128, NT * (T + 1)], F32)
    xm_tiles = [xm[:, g * (T + 1):(g + 1) * (T + 1)] for g in range(NT)]
    dep = []
    for g in range(4):
        dep.append(p.op("sync", lambda s, g=g: s.dma_start(out=qrot[g][:, :], in_=q_in[128 * g:128 * (g + 1), :]), dma=True))
        dep.append(p.op("sync", lambda s, g=g: s.dma_start(out=krotF[g][:, 0:512], in_=kh_in[128 * g:128 * (g + 1), :]), dma=True))
        dep.append(p.op("sync", lambda s, g=g: s.dma_start(out=krotF[g][:, 512:512 + T], in_=k_in[128 * g:128 * (g + 1), :]), dma=True))
    dep.append(p.op("sync", lambda s: s.dma_start(out=vtmF[:, 0:4 * 512], in_=vh_in), dma=True))
    dep.append(p.op("sync", lambda s: s.dma_start(out=vtmF[:, 4 * 512:], in_=v_in), dma=True))
    for g in range(NT):
        dep.append(p.op("sync", lambda s, g=g: s.dma_start(out=xm_tiles[g][:, 1:T + 1], in_=xi_in[128 * g:128 * (g + 1), :]), dma=True))

    bufs["dbg_attn"] = {
        "mst": nc.dram_tensor("dbg_mst", [128, 16], F32, kind="ExternalOutput").ap(),
        "dsum": nc.dram_tensor("dbg_dsum", [128, 16], F32, kind="ExternalOutput").ap(),
        "att": nc.dram_tensor("dbg_att", [512, T], BF16, kind="ExternalOutput").ap(),
    }
    ain = {"qrot": [t[:, :] for t in qrot], "krotF": [t[:, :] for t in krotF], "vtmF": vtmF[:, :]}
    toks = emit_l1f_attn(p, bufs, xm_tiles, dep, ain)
    st = [p.op("sync", lambda s, g=g: s.dma_start(out=xm_out[128 * g:128 * (g + 1), :], in_=xm_tiles[g][:, 1:T + 1]),
               waits=toks, dma=True) for g in range(NT)]
    p.op("gpsimd", lambda g: g.engine_nop(), waits=st)
    p.finalize()
    return nc


def build_stage_d_ffn(final):
    nc = bass.Bass("TRN2", target_bir_lowering=False)
    p = Prog(nc)
    bufs = {}
    make_psum_pool(p, bufs)
    make_ln_shared(p, bufs)
    setup_common(p, nc, bufs)
    bufs["l1_w1_dram"] = nc.dram_tensor("w1p", [D, 5632], F32, kind="ExternalInput").ap()
    bufs["l1_w2_dram"] = nc.dram_tensor("w2p", [2816, D], F32, kind="ExternalInput").ap()
    lncols = nc.dram_tensor("lncols", [128, 16], F32, kind="ExternalInput").ap()
    bcols = nc.dram_tensor("bcols", [128, 52], F32, kind="ExternalInput").ap()
    xm_in = nc.dram_tensor("x_mid", [D, T], F32, kind="ExternalInput").ap()
    halo_col = nc.dram_tensor("halo_col", [D, 1], F32, kind="ExternalInput").ap()
    s_lnc = p.sbuf("lncols", [128, 16], F32)
    s_bc = p.sbuf("bcols", [128, 52], F32)
    c1 = p.op("sync", lambda s: s.dma_start(out=s_lnc[:, :], in_=lncols), dma=True)
    c2 = p.op("sync", lambda s: s.dma_start(out=s_bc[:, :], in_=bcols), dma=True)
    bufs["l1_lng_col"] = s_lnc[:, 0:8]
    bufs["l1_lnb_col"] = s_lnc[:, 8:16]
    bufs["l1_b1_col"] = s_bc[:, 0:44]
    bufs["l1_b2_col"] = s_bc[:, 44:52]

    xm = p.sbuf("xm", [128, NT * (T + 1)], F32)
    xm_tiles = [xm[:, g * (T + 1):(g + 1) * (T + 1)] for g in range(NT)]
    dep = [c1, c2]
    for g in range(NT):
        dep.append(p.op("sync", lambda s, g=g: s.dma_start(out=xm_tiles[g][:, 1:T + 1], in_=xm_in[128 * g:128 * (g + 1), :]), dma=True))
        dep.append(p.op("sync", lambda s, g=g: s.dma_start(out=xm_tiles[g][:, 0:1], in_=halo_col[128 * g:128 * (g + 1), :]), dma=True))

    fa = None
    if final:
        x_outer = nc.dram_tensor("x_outer", [D, T], F32, kind="ExternalInput").ap()
        x_out = nc.dram_tensor("x_out", [D, T], F32, kind="ExternalOutput").ap()
        fa = {"x_outer_dram": x_outer, "x_out_dram": x_out}
    else:
        x_out = nc.dram_tensor("x_next", [D, T], F32, kind="ExternalOutput").ap()

    toks = emit_l1f_ffn(p, bufs, xm_tiles, dep, final_add=fa)
    if not final:
        st = [p.op("sync", lambda s, g=g: s.dma_start(out=x_out[128 * g:128 * (g + 1), :], in_=xm_tiles[g][:, 1:T + 1]),
                   waits=toks, dma=True) for g in range(NT)]
        p.op("gpsimd", lambda g: g.engine_nop(), waits=st)
    else:
        p.op("gpsimd", lambda g: g.engine_nop(), waits=toks)
    p.finalize()
    return nc


def l1_host_consts():
    Dh, W = 64, 512
    half = np.arange(0, Dh, 2, dtype=np.float64)
    inv_freq = 1.0 / 10000.0 ** (half / Dh)
    base = (half + 0.4 * Dh) / (1.4 * Dh)

    def fields(tpos):
        freqs = tpos[:, None] * inv_freq[None, :]
        power = (tpos - 16384 // 2) / float(W // 2)
        scale = base[None, :] ** power[:, None]
        cos = np.cos(freqs); sin = np.sin(freqs)
        cs = np.concatenate([cos * scale, cos * scale], axis=1)
        ss = np.concatenate([sin * scale, sin * scale], axis=1)
        ci = np.concatenate([cos / scale, cos / scale], axis=1)
        si = np.concatenate([sin / scale, sin / scale], axis=1)
        f = lambda a: np.ascontiguousarray(np.tile(a.T, (2, 1)).astype(np.float32))
        return f(cs), f(ss), f(ci), f(si)

    selh = np.zeros((128, 32), np.float32)
    for g in range(4):
        for r in range(128):
            selh[r, 8 * g + 2 * g + (1 if r >= 64 else 0)] = 1.0
    P64 = np.zeros((64, 64), np.float32)
    for r in range(32):
        P64[r, r + 32] = -1.0
    for r in range(32, 64):
        P64[r, r - 32] = 1.0
    permT = np.zeros((128, 128), np.float32)
    permT[0:64, 0:64] = P64.T
    permT[64:128, 64:128] = P64.T
    m0 = np.where(np.arange(128)[None, :] >= np.arange(128)[:, None], 0.0, -3e38).astype(np.float32)
    m4 = np.where(np.arange(128)[None, :] <= np.arange(128)[:, None], 0.0, -3e38).astype(np.float32)
    return fields, selh, permT, m0, m4




_PROG_CACHE = {}


def _get_prog(name, builder, *args):
    key = (name,) + args
    if key not in _PROG_CACHE:
        _PROG_CACHE[key] = builder(*args)
    return _PROG_CACHE[key]


def kernel(**inputs):
    NC = NCORES
    f32c = lambda a: np.ascontiguousarray(a, dtype=np.float32)
    x0 = f32c(inputs["L0_attn_x"][0])
    ident = np.eye(128, dtype=np.float32)
    ones_row = np.ones((1, 128), np.float32)
    ones_col = np.ones((128, 1), np.float32)
    base = {"ones_row": ones_row, "ones_col": ones_col}

    wqkv0 = f32c(inputs["L0_attn_W_qkv"])
    wout0 = f32c(inputs["L0_attn_W_out"])
    proj = f32c(inputs["L0_attn_proj"])
    pmT = np.ascontiguousarray(proj.transpose(0, 2, 1).reshape(1024, 128))
    pmP = np.zeros((128, 2048), np.float32)
    for pr in range(8):
        pmP[0:64, 256 * pr:256 * pr + 128] = pmT[128 * pr:128 * pr + 64]
        pmP[64:128, 256 * pr + 128:256 * pr + 256] = pmT[128 * pr + 64:128 * pr + 128]
    identA, seld = perf_host_consts()
    pbase = dict(base, pmT=pmT, pmTpair=pmP, seld=seld, ident=ident)

    nc1 = _get_prog("a1", build_stage_a1, ("qkv", "sq", "diag", "v", "kstab", "qstab"))
    maps = [dict(pbase, x_in=np.ascontiguousarray(x0[2048 * c:2048 * (c + 1)].T), wqkv=wqkv0)
            for c in range(NC)]
    r1 = run_bass_kernel_spmd(nc1, maps, core_ids=list(range(NC))).results

    nc2 = _get_prog("a2", build_stage_a2)
    skall = np.stack([np.asarray(r1[c]["sk_out"])[:, 0] for c in range(NC)], axis=1).astype(np.float32)
    maps = [dict(pbase, k_in=np.asarray(r1[c]["k_out"]), v_in=np.asarray(r1[c]["v_out"]),
                 dk_in=np.asarray(r1[c]["dk_out"]), skall=skall) for c in range(NC)]
    r2 = run_bass_kernel_spmd(nc2, maps, core_ids=list(range(NC))).results

    nc3 = _get_prog("a3", build_stage_a3)
    kvall = np.ascontiguousarray(np.concatenate([np.asarray(r2[c]["kv_out"]) for c in range(NC)], axis=0))
    maps = [dict(pbase, x_in=np.ascontiguousarray(x0[2048 * c:2048 * (c + 1)].T),
                 q_in=np.asarray(r1[c]["q_out"]), dq_in=np.asarray(r1[c]["dq_out"]),
                 sq_in=np.asarray(r1[c]["sq_out"]), kvall=kvall, wout=wout0) for c in range(NC)]
    r3 = run_bass_kernel_spmd(nc3, maps, core_ids=list(range(NC))).results
    x1 = np.concatenate([np.asarray(r["x_out"]).T for r in r3], axis=0)

    k0 = np.ascontiguousarray(inputs["L0_ffn_k0"]).view(np.uint8).reshape(2048, 4096)
    k1w = np.ascontiguousarray(inputs["L0_ffn_k1"]).view(np.uint8).reshape(4096, 4096)
    k2w = np.ascontiguousarray(inputs["L0_ffn_k2"]).view(np.uint8).reshape(4096, 1024)
    ncb = _get_prog("b", build_stage_b)
    maps = [dict(base, x_in=np.ascontiguousarray(x1[2048 * c:2048 * (c + 1)].T),
                 k0a=k0[:1024], k0b=k0[1024:], k1=k1w, k2=k2w,
                 s0=np.asarray(inputs["L0_ffn_s0"], np.float32).reshape(1, 1),
                 s1=np.asarray(inputs["L0_ffn_s1"], np.float32).reshape(1, 1),
                 s2=np.asarray(inputs["L0_ffn_s2"], np.float32).reshape(1, 1)) for c in range(NC)]
    rb = run_bass_kernel_spmd(ncb, maps, core_ids=list(range(NC))).results
    x2 = np.concatenate([np.asarray(r["x_out"]).T for r in rb], axis=0)

    def colpack(v):
        return np.ascontiguousarray(np.asarray(v, np.float32).reshape(8, 128).T)
    rcols = np.concatenate([
        colpack(inputs["L1_attn_time_mix_key"].ravel()),
        colpack(inputs["L1_attn_time_mix_value"].ravel()),
        colpack(inputs["L1_attn_time_mix_receptance"].ravel()),
        colpack(inputs["L1_attn_bo"].ravel()),
        colpack(inputs["L1_attn_time_first"].ravel()),
        colpack(inputs["L1_attn_time_decay"].ravel()),
    ], axis=1)
    ncc = _get_prog("c", build_stage_c)
    maps = []
    for c in range(NC):
        sl = slice(2048 * c, 2048 * (c + 1))
        halo = np.zeros((1024, 257), np.float32) if c == 0 else np.ascontiguousarray(x2[2048 * c - 257:2048 * c].T)
        cm = np.zeros((128, 1), np.float32) if c == 0 else np.ones((128, 1), np.float32)
        maps.append(dict(base, x_in=np.ascontiguousarray(x2[sl].T), halo_x=halo, cmask=cm,
                         wkT=np.ascontiguousarray(f32c(inputs["L1_attn_Wk"]).T),
                         wvT=np.ascontiguousarray(f32c(inputs["L1_attn_Wv"]).T),
                         wrT=np.ascontiguousarray(f32c(inputs["L1_attn_Wr"]).T),
                         woT=np.ascontiguousarray(f32c(inputs["L1_attn_Wo"]).T),
                         rwkv_cols=rcols))
    rc = run_bass_kernel_spmd(ncc, maps, core_ids=list(range(NC))).results
    x3 = np.concatenate([np.asarray(r["x_out"]).T for r in rc], axis=0)

    fields, selh, permT, m0, m4 = l1_host_consts()
    dbase = dict(base, selh2=selh, permT=permT, ident=ident)
    rope_per_core = []
    for c in range(NC):
        tq = np.arange(2048 * c, 2048 * (c + 1)).astype(np.float64)
        cssq, sssq, csik, ssik = fields(tq)
        rope_per_core.append({"cssq": cssq, "sssq": sssq, "csik": csik, "ssik": ssik})
    browc, edgec = [], []
    for c in range(NC):
        br = np.zeros((1, 2560), np.float32)
        et = np.zeros((128, 4), np.float32)
        if c == 0:
            for qt in range(4):
                ncut = 512 - 128 * qt
                br[0, 640 * qt:640 * qt + ncut] = -3e38
            for j in range(4):
                et[:512 - 128 * j, j] = -3e38
        browc.append(br)
        edgec.append(et)

    def layer_consts(l):
        qs2 = np.concatenate([f32c(inputs["L1_ffn_q_scale"][l])] * 2).reshape(128, 1)
        ks2 = np.concatenate([f32c(inputs["L1_ffn_k_scale"][l])] * 2).reshape(128, 1)
        w1 = f32c(inputs["L1_ffn_w1"][l]); b1 = f32c(inputs["L1_ffn_b1"][l])
        w2 = f32c(inputs["L1_ffn_w2"][l]); b2 = f32c(inputs["L1_ffn_b2"][l])
        w1p = np.zeros((1024, 5632), np.float32)
        w1p[:, :2730] = w1[:, :2730]; w1p[:, 2816:2816 + 2730] = w1[:, 2730:]
        w2p = np.zeros((2816, 1024), np.float32)
        w2p[:2730] = w2
        b1p = np.zeros(5632, np.float32)
        b1p[:2730] = b1[:2730]; b1p[2816:2816 + 2730] = b1[2730:]
        bcols = np.zeros((128, 52), np.float32)
        bcols[:, 0:44] = b1p.reshape(44, 128).T
        bcols[:, 44:52] = b2.reshape(8, 128).T
        lncols = np.zeros((128, 16), np.float32)
        lncols[:, 0:8] = f32c(inputs["L1_ffn_ln_g"][l]).reshape(8, 128).T
        lncols[:, 8:16] = f32c(inputs["L1_ffn_ln_b"][l]).reshape(8, 128).T
        return dict(qs2=np.ascontiguousarray(qs2), ks2=np.ascontiguousarray(ks2),
                    wqkv=f32c(inputs["L1_ffn_wqkv"][l]), wo=f32c(inputs["L1_ffn_wo"][l]),
                    w1p=w1p, w2p=w2p, bcols=bcols, lncols=lncols)

    x_outer = x3
    xin = None
    for l in range(2):
        lc = layer_consts(l)
        final = (l == 1)
        if l == 0:
            ncq = _get_prog("dqkv1", build_stage_d_qkv, True)
            maps = []
            for c in range(NC):
                sl = slice(2048 * c, 2048 * (c + 1))
                halo = x_outer[2048 * c - 1] if c else np.zeros(1024, np.float32)
                maps.append(dict(dbase, **rope_per_core[c],
                                 x_outer=np.ascontiguousarray(x_outer[sl].T),
                                 halo_row=np.ascontiguousarray(halo.reshape(1, 1024)),
                                 wqkv=lc["wqkv"], qs2=lc["qs2"], ks2=lc["ks2"]))
        else:
            ncq = _get_prog("dqkv0", build_stage_d_qkv, False)
            maps = []
            for c in range(NC):
                sl = slice(2048 * c, 2048 * (c + 1))
                halo = xin[2048 * c - 1] if c else np.zeros(1024, np.float32)
                xi = np.concatenate([halo[:, None], xin[sl].T], axis=1)
                maps.append(dict(dbase, **rope_per_core[c], xin=np.ascontiguousarray(xi),
                                 wqkv=lc["wqkv"], qs2=lc["qs2"], ks2=lc["ks2"]))
        ra = run_bass_kernel_spmd(ncq, maps, core_ids=list(range(NC))).results
        if l == 0:
            x_inner = np.concatenate([np.asarray(r["ln4_out"]).T for r in ra], axis=0)
        else:
            x_inner = xin
        ncat = _get_prog("dattn", build_stage_d_attn)
        maps = []
        for c in range(NC):
            sl = slice(2048 * c, 2048 * (c + 1))
            kh = np.asarray(ra[c - 1]["k_out"])[:, 1536:2048] if c else np.zeros((512, 512), ml_dtypes.bfloat16)
            vh = np.asarray(ra[c - 1]["v_out"])[:, 12 * 512:] if c else np.zeros((128, 2048), ml_dtypes.bfloat16)
            maps.append(dict(base, m0=m0, m4=m4, ident=ident, biasrow=browc[c], edgeT=edgec[c],
                             wo=lc["wo"], q_in=np.asarray(ra[c]["q_out"]), k_in=np.asarray(ra[c]["k_out"]),
                             kh_in=np.ascontiguousarray(kh), v_in=np.asarray(ra[c]["v_out"]),
                             vh_in=np.ascontiguousarray(vh),
                             x_inner=np.ascontiguousarray(x_inner[sl].T)))
        rb2 = run_bass_kernel_spmd(ncat, maps, core_ids=list(range(NC))).results
        x_mid = np.concatenate([np.asarray(r["x_mid"]).T for r in rb2], axis=0)
        ncf = _get_prog("dffn", build_stage_d_ffn, final)
        maps = []
        for c in range(NC):
            sl = slice(2048 * c, 2048 * (c + 1))
            halo = x_mid[2048 * c - 1] if c else np.zeros(1024, np.float32)
            mm = dict(base, w1p=lc["w1p"], w2p=lc["w2p"], bcols=lc["bcols"], lncols=lc["lncols"],
                      x_mid=np.ascontiguousarray(x_mid[sl].T),
                      halo_col=np.ascontiguousarray(halo.reshape(1024, 1)))
            if final:
                mm["x_outer"] = np.ascontiguousarray(x_outer[sl].T)
            maps.append(mm)
        rc2 = run_bass_kernel_spmd(ncf, maps, core_ids=list(range(NC))).results
        key = "x_out" if final else "x_next"
        xin = np.concatenate([np.asarray(r[key]).T for r in rc2], axis=0)
    return xin.reshape(1, 16384, 1024).astype(np.float32)



# version 04
# speedup vs baseline: 1.0031x; 1.0031x over previous
import numpy as np
import ml_dtypes
from contextlib import ExitStack
import concourse.bass as bass
import concourse.mybir as mybir
from concourse.bass_utils import run_bass_kernel_spmd
from concourse.bass_core_rust import IndirectOffsetOnAxis

F32 = mybir.dt.float32
BF16 = mybir.dt.bfloat16
U8 = mybir.dt.uint8
I32 = mybir.dt.int32
AF = mybir.ActivationFunctionType
ALU = mybir.AluOpType
AX = mybir.AxisListType

NCORES = 8
T = 2048
D = 1024
NT = D // 128


class Prog:

    ENG = ("sync", "gpsimd", "tensor", "vector", "scalar")
    DMA_Q = ("sync", "gpsimd")
    NSLOT = 8

    def __init__(self, nc):
        self.nc = nc
        self.es = ExitStack()
        self.ops = []
        self.counts = {e: 0 for e in self.ENG}
        self.sems = {e: self.es.enter_context(nc.semaphore(f"sem_{e}")) for e in self.ENG}
        for q in self.DMA_Q:
            for s in range(self.NSLOT):
                self.sems[f"{q}_s{s}"] = self.es.enter_context(nc.semaphore(f"sem_{q}_s{s}"))
                self.counts[f"{q}_s{s}"] = 0
        self.dma_idx = {q: 0 for q in self.DMA_Q}
        self._names = {}

    def sbuf(self, base, shape, dtype):
        return self.es.enter_context(self.nc.sbuf_tensor(self.name(base), shape, dtype))

    def psum(self, base, shape, dtype=F32):
        return self.es.enter_context(self.nc.psum_tensor(self.name(base), shape, dtype))

    def name(self, base):
        i = self._names.get(base, 0)
        self._names[base] = i + 1
        return f"{base}_{i}"

    def op(self, engine, fn, waits=(), dma=False):
        ws = [w for w in waits if w is not None]
        if dma:
            assert engine in self.DMA_Q
            q = engine
            slot = f"{q}_s{self.dma_idx[q] % self.NSLOT}"
            self.dma_idx[q] += 1
            if self.counts[slot] > 0:
                ws.append((slot, self.counts[slot]))
            self.counts[slot] += 16
            tok = (slot, self.counts[slot])
            self.ops.append((engine, fn, tuple(ws), 16, slot))
            return tok
        self.counts[engine] += 1
        tok = (engine, self.counts[engine])
        self.ops.append((engine, fn, tuple(ws), 1, engine))
        return tok

    def finalize(self):
        nc = self.nc
        with nc.Block() as block:
            for eng_name in self.ENG:
                ops = [(f, w, a, s) for (e, f, w, a, s) in self.ops if e == eng_name]
                if not ops:
                    continue

                def make(eng_name_, _ops):
                    def sec(eng):
                        seen = {}
                        for f, waits, amount, sem in _ops:
                            for (we, wv) in waits:
                                if seen.get(we, 0) >= wv:
                                    continue
                                seen[we] = wv
                                eng.wait_ge(self.sems[we], wv)
                            last = f(eng)
                            last.then_inc(self.sems[sem], amount)
                    return sec

                getattr(block, eng_name)(make(eng_name, ops))
        self.es.close()


def emit_ln_ttile(p, bufs, x_tiles, tsl, out_tiles, osl, dep, gain_col=None,
                  bias_col=None, eps=1e-5, tag="ln", ps_override=None):
    W = tsl.stop - tsl.start
    onescol_f = bufs["ones_col_f"]
    onesrow_f = bufs["ones_row_f"]
    if ps_override is None:
        sxp = bufs["ps_row0"][:, 0:W]
        sxxp = bufs["ps_row1"][:, 0:W]
        rb_f = bufs["ps_rb"][:, 0:W]
        mrb_f = bufs["ps_mrb"][:, 0:W]
    else:
        sxp, sxxp, rb_f, mrb_f = ps_override
    sh = bufs["ln_shared"]
    x2c = sh["x2"]
    mrow, rrow, mrrow, tmp = sh["m"], sh["r"], sh["mr"], sh["t"]

    def mm_sx(t):
        for g in range(NT):
            last = t.matmul(sxp, onescol_f[:, :], x_tiles[g][:, tsl],
                            start=(g == 0), stop=(g == NT - 1))
        return last
    t_sx = p.op("tensor", mm_sx, waits=dep)

    sq_toks = []
    mm_toks = [t_sx]
    for g in range(NT):
        tq = p.op("scalar", lambda s, g=g: s.activation(x2c[:, 0:W], x_tiles[g][:, tsl], AF.Square),
                  waits=list(dep) + mm_toks[-1:])
        tm = p.op("tensor", lambda t, g=g: t.matmul(sxxp, onescol_f[:, :], x2c[:, 0:W],
                                                    start=(g == 0), stop=(g == NT - 1)), waits=[tq])
        mm_toks.append(tm)
        sq_toks.append(tq)

    tkm = p.op("vector", lambda v: v.tensor_scalar_mul(mrow[:, 0:W], sxp, 1.0 / 1024.0), waits=[t_sx])
    tke = p.op("vector", lambda v: v.tensor_scalar_mul(tmp[:, 0:W], sxxp, 1.0 / 1024.0), waits=[mm_toks[-1]])
    t1 = p.op("vector", lambda v: v.tensor_mul(rrow[:, 0:W], mrow[:, 0:W], mrow[:, 0:W]), waits=[tkm])
    t2 = p.op("vector", lambda v: v.tensor_sub(tmp[:, 0:W], tmp[:, 0:W], rrow[:, 0:W]), waits=[tke, t1])
    t2b = p.op("vector", lambda v: v.tensor_scalar_add(tmp[:, 0:W], tmp[:, 0:W], float(eps)), waits=[t2])
    t3 = p.op("scalar", lambda s: s.activation(tmp[:, 0:W], tmp[:, 0:W], AF.Ln), waits=[t2b])
    t4 = p.op("scalar", lambda s: s.activation(rrow[:, 0:W], tmp[:, 0:W], AF.Exp, scale=-0.5), waits=[t3])
    t5 = p.op("vector", lambda v: v.tensor_mul(mrrow[:, 0:W], mrow[:, 0:W], rrow[:, 0:W]), waits=[t4])

    rb = rb_f
    mrb = mrb_f
    scratch = sh["scr"]
    tb1 = p.op("tensor", lambda t: t.matmul(rb, onesrow_f[:, :], rrow[:, 0:W], start=True, stop=True), waits=[t4])
    tb2 = p.op("tensor", lambda t: t.matmul(mrb, onesrow_f[:, :], mrrow[:, 0:W], start=True, stop=True), waits=[t5])
    out_toks = []
    for g in range(NT):
        tn1 = p.op("vector", lambda v, g=g: v.tensor_mul(scratch[:, 0:W], x_tiles[g][:, tsl], rb),
                   waits=[tb1] + out_toks[-1:])
        if gain_col is None:
            tn2 = p.op("vector", lambda v, g=g: v.tensor_sub(out_tiles[g][:, osl], scratch[:, 0:W], mrb),
                       waits=[tb2, tn1])
        else:
            tn2a = p.op("vector", lambda v, g=g: v.tensor_sub(scratch[:, 0:W], scratch[:, 0:W], mrb),
                        waits=[tb2, tn1])
            tn2 = p.op("vector", lambda v, g=g: v.tensor_scalar(
                out_tiles[g][:, osl], scratch[:, 0:W], gain_col[g], bias_col[g], ALU.mult, ALU.add),
                waits=[tn2a])
        out_toks.append(tn2)
    return out_toks


def emit_signnet_convert(p, bufs, tag="cvt"):
    k0a, k0b = bufs["k0a_hbm"], bufs["k0b_hbm"]
    k1, k2 = bufs["k1_hbm"], bufs["k2_hbm"]
    d0, d1, d2 = bufs["d0p_dram"], bufs["d1p_dram"], bufs["d2p_dram"]

    W = 2048
    sa = p.sbuf(f"{tag}_a", [128, W], BF16)
    sb = p.sbuf(f"{tag}_b", [128, W], BF16)
    sc = p.sbuf(f"{tag}_c", [128, W], BF16)
    last = []
    prev = []
    for r in range(8):
        for h in range(2):
            rs = slice(128 * r, 128 * (r + 1))
            cs = slice(W * h, W * (h + 1))
            l1 = p.op("gpsimd", lambda g, rs=rs, cs=cs: g.dma_start(out=sa[:, :], in_=k0a[rs, cs]), waits=prev, dma=True)
            l2 = p.op("gpsimd", lambda g, rs=rs, cs=cs: g.dma_start(out=sb[:, :], in_=k0b[rs, cs]), waits=prev, dma=True)
            c1 = p.op("vector", lambda v: v.tensor_sub(sc[:, :], sa[:, :], sb[:, :]), waits=[l1, l2])
            s1 = p.op("sync", lambda s, rs=rs, cs=cs: s.dma_start(out=d0[rs, cs], in_=sc[:, :]), waits=[c1], dma=True)
            prev = [s1]
    d0_tok = prev[0]
    for r in range(32):
        for h in range(2):
            rs = slice(128 * r, 128 * (r + 1))
            cs = slice(W * h, W * (h + 1))
            l1 = p.op("gpsimd", lambda g, rs=rs, cs=cs: g.dma_start(out=sa[:, :], in_=k1[rs, cs]), waits=prev, dma=True)
            c1 = p.op("vector", lambda v: v.tensor_scalar(sc[:, :], sa[:, :], 2.0, -1.0, ALU.mult, ALU.add), waits=[l1])
            s1 = p.op("sync", lambda s, rs=rs, cs=cs: s.dma_start(out=d1[rs, cs], in_=sc[:, :]), waits=[c1], dma=True)
            prev = [s1]
    d1_tok = prev[0]
    for r in range(32):
        rs = slice(128 * r, 128 * (r + 1))
        l1 = p.op("gpsimd", lambda g, rs=rs: g.dma_start(out=sa[:, 0:1024], in_=k2[rs, :]), waits=prev, dma=True)
        c1 = p.op("vector", lambda v: v.tensor_scalar(sc[:, 0:1024], sa[:, 0:1024], 2.0, -1.0, ALU.mult, ALU.add), waits=[l1])
        s1 = p.op("sync", lambda s, rs=rs: s.dma_start(out=d2[rs, :], in_=sc[:, 0:1024]), waits=[c1], dma=True)
        prev = [s1]
    d2_tok = prev[0]
    return d0_tok, d1_tok, d2_tok


def emit_l0ffn(p, bufs, x_tiles, dep, tag="ffn0"):
    d0 = bufs["d0p_dram"]
    d1 = bufs["d1p_dram"]
    d2 = bufs["d2p_dram"]
    s0c, s1c, s2c = bufs["s0_col"], bufs["s1_col"], bufs["s2_col"]

    lnb = [p.sbuf(f"{tag}_ln{g}", [128, 512], BF16) for g in range(NT)]
    h1 = p.sbuf(f"{tag}_h1", [128, 32 * 512], BF16)
    h2 = p.sbuf(f"{tag}_h2", [128, 32 * 512], BF16)
    h1t = [h1[:, 512 * i:512 * (i + 1)] for i in range(32)]
    h2t = [h2[:, 512 * i:512 * (i + 1)] for i in range(32)]
    w0 = [p.sbuf(f"{tag}_w0", [128, 8 * 128], BF16) for _ in range(2)]
    w1 = [p.sbuf(f"{tag}_w1", [128, 32 * 128], BF16) for _ in range(2)]
    w2 = [p.sbuf(f"{tag}_w2", [128, 32 * 128], BF16) for _ in range(2)]
    ps = bufs["ps_mm0"]
    ps2 = bufs["ps_mm1"]

    d0r = d0.rearrange("(kt k) o -> k kt o", k=128)
    d1r = d1.rearrange("(kt k) o -> k kt o", k=128)
    d2r = d2.rearrange("(kt k) o -> k kt o", k=128)

    x_toks = list(dep)
    for tt in range(4):
        tsl = slice(512 * tt, 512 * (tt + 1))
        ln_toks = emit_ln_ttile(p, bufs, x_tiles, tsl, lnb, slice(0, 512),
                                x_toks[-NT:] if tt else list(dep), tag=f"{tag}_ln{tt}")
        h1_toks = []
        mm_hist = bufs.setdefault("_mm_hist", {})
        prev_ps_free = x_toks[-1:]
        for ot in range(32):
            wb = w0[ot % 2]
            lt = p.op("sync", lambda s, ot=ot, wb=wb: s.dma_start(
                out=wb[:, :].rearrange("k (kt o) -> k kt o", o=128),
                in_=d0r[:, :, 128 * ot:128 * (ot + 1)]),
                waits=[bufs["d0_ready"], mm_hist.get(("w0", ot % 2))], dma=True)
            def mm(t, wb=wb, tsl=tsl):
                for kt in range(8):
                    last = t.matmul(ps, wb[:, 128 * kt:128 * (kt + 1)], lnb[kt][:, :],
                                    start=(kt == 0), stop=(kt == 7))
                return last
            mt = p.op("tensor", mm, waits=[lt] + ln_toks + h1_toks[-1:] + prev_ps_free)
            prev_ps_free = []
            mm_hist[("w0", ot % 2)] = mt
            et = p.op("vector", lambda v, ot=ot: v.tensor_scalar(
                h1t[ot], ps, s0c, 0.0, ALU.mult, ALU.max), waits=[mt])
            h1_toks.append(et)
        if tt == 0 and "dbg" in bufs:
            for g in range(NT):
                p.op("gpsimd", lambda gg, g=g: gg.dma_start(
                    out=bufs["dbg"]["ln"][128 * g:128 * (g + 1), :], in_=lnb[g][:, :]),
                    waits=ln_toks, dma=True)
            for i in range(32):
                p.op("gpsimd", lambda gg, i=i: gg.dma_start(
                    out=bufs["dbg"]["h1"][128 * i:128 * (i + 1), :], in_=h1t[i]),
                    waits=h1_toks, dma=True)
        h2_toks = []
        for ot in range(32):
            wb = w1[ot % 2]
            lt = p.op("sync", lambda s, ot=ot, wb=wb: s.dma_start(
                out=wb[:, :].rearrange("k (kt o) -> k kt o", o=128),
                in_=d1r[:, :, 128 * ot:128 * (ot + 1)]),
                waits=[bufs["d1_ready"], mm_hist.get(("w1", ot % 2))], dma=True)
            def mm(t, wb=wb):
                for kt in range(32):
                    last = t.matmul(ps2, wb[:, 128 * kt:128 * (kt + 1)], h1t[kt],
                                    start=(kt == 0), stop=(kt == 31))
                return last
            mt = p.op("tensor", mm, waits=[lt] + h1_toks + h2_toks[-1:])
            mm_hist[("w1", ot % 2)] = mt
            et = p.op("vector", lambda v, ot=ot: v.tensor_scalar(
                h2t[ot], ps2, s1c, 0.0, ALU.mult, ALU.max), waits=[mt])
            h2_toks.append(et)
        if tt == 0 and "dbg" in bufs:
            for i in range(32):
                p.op("gpsimd", lambda gg, i=i: gg.dma_start(
                    out=bufs["dbg"]["h2"][128 * i:128 * (i + 1), :], in_=h2t[i]),
                    waits=h2_toks, dma=True)
        res_toks = []
        for g in range(NT):
            wb = w2[g % 2]
            lt = p.op("sync", lambda s, g=g, wb=wb: s.dma_start(
                out=wb[:, :].rearrange("k (kt o) -> k kt o", o=128),
                in_=d2r[:, :, 128 * g:128 * (g + 1)]),
                waits=[bufs["d2_ready"], mm_hist.get(("w2", g % 2))], dma=True)
            def mm(t, wb=wb):
                for kt in range(32):
                    last = t.matmul(ps, wb[:, 128 * kt:128 * (kt + 1)], h2t[kt],
                                    start=(kt == 0), stop=(kt == 31))
                return last
            mt = p.op("tensor", mm, waits=[lt] + h2_toks + res_toks[-1:])
            mm_hist[("w2", g % 2)] = mt
            et = p.op("vector", lambda v, g=g, tsl=tsl: v.scalar_tensor_tensor(
                x_tiles[g][:, tsl], ps, s2c, x_tiles[g][:, tsl], ALU.mult, ALU.add),
                waits=[mt])
            res_toks.append(et)
        x_toks = res_toks
    return x_toks


def make_ln_shared(p, bufs):
    bufs["ln_shared"] = {
        "x2": p.sbuf("ln_x2", [128, 512], F32),
        "scr": p.sbuf("ln_scr", [128, 512], F32),
        "m": p.sbuf("ln_m", [1, 512], F32),
        "r": p.sbuf("ln_r", [1, 512], F32),
        "mr": p.sbuf("ln_mr", [1, 512], F32),
        "t": p.sbuf("ln_t", [1, 512], F32),
    }


def make_psum_pool(p, bufs):
    bufs["ps_mm0"] = p.psum("ps_mm0", [128, 512])[:, :]
    bufs["ps_mm1"] = p.psum("ps_mm1", [128, 512])[:, :]
    bufs["ps_rb"] = p.psum("ps_rb", [128, 512])[:, :]
    bufs["ps_mrb"] = p.psum("ps_mrb", [128, 512])[:, :]
    bufs["ps_row0"] = p.psum("ps_row0", [1, 512])[:, :]
    bufs["ps_row1"] = p.psum("ps_row1", [1, 512])[:, :]
    bufs["ps_x0"] = p.psum("ps_x0", [128, 512])[:, :]


def setup_common(p, nc, bufs):
    consts_r = nc.dram_tensor("ones_row", [1, 128], F32, kind="ExternalInput").ap()
    consts_c = nc.dram_tensor("ones_col", [128, 1], F32, kind="ExternalInput").ap()
    cr = p.sbuf("cr", [1, 128], F32)
    cc = p.sbuf("cc", [128, 1], F32)
    ccb = p.sbuf("ccb", [128, 1], BF16)
    t1 = p.op("sync", lambda s: s.dma_start(out=cr[:, :], in_=consts_r), dma=True)
    t2 = p.op("sync", lambda s: s.dma_start(out=cc[:, :], in_=consts_c), dma=True)
    t3 = p.op("gpsimd", lambda g: g.dma_start(out=ccb[:, :], in_=consts_c), dma=True)
    crb = p.sbuf("crb", [1, 128], BF16)
    t4 = p.op("gpsimd", lambda g: g.dma_start(out=crb[:, :], in_=consts_r), dma=True)
    bufs["ones_row_f"] = cr[:, :]
    bufs["ones_col_f"] = cc[:, :]
    bufs["ones_col_bf"] = ccb[:, :]
    bufs["ones_row_bf"] = crb[:, :]
    bufs["consts_ready"] = [t1, t2, t3, t4]
    return [t1, t2, t3, t4]


def emit_scalar_bcast(p, bufs, scalar_dram_list, scales, tag="sb"):
    n = len(scalar_dram_list)
    s_sb = p.sbuf(f"{tag}_row", [1, n], F32)
    scol = p.sbuf(f"{tag}_col", [128, n], F32)
    pscol = bufs["ps_rb"][:, 0:n]
    toks = [p.op("sync", lambda s, i=i, a=a: s.dma_start(out=s_sb[:, i:i + 1], in_=a), dma=True)
            for i, a in enumerate(scalar_dram_list)]
    tb = p.op("tensor", lambda t: t.matmul(pscol, bufs["ones_row_f"], s_sb[:, :],
                                           start=True, stop=True), waits=toks + bufs["consts_ready"])
    tv = []
    for i, sc in enumerate(scales):
        tv.append(p.op("vector", lambda v, i=i, sc=sc: v.tensor_scalar_mul(
            scol[:, i:i + 1], pscol[:, i:i + 1], float(sc)), waits=[tb]))
    return scol, tv


def build_stage_b():
    nc = bass.Bass("TRN2", target_bir_lowering=False)
    x_in = nc.dram_tensor("x_in", [D, T], F32, kind="ExternalInput").ap()
    k0a = nc.dram_tensor("k0a", [1024, 4096], U8, kind="ExternalInput").ap()
    k0b = nc.dram_tensor("k0b", [1024, 4096], U8, kind="ExternalInput").ap()
    k1 = nc.dram_tensor("k1", [4096, 4096], U8, kind="ExternalInput").ap()
    k2 = nc.dram_tensor("k2", [4096, 1024], U8, kind="ExternalInput").ap()
    s0 = nc.dram_tensor("s0", [1, 1], F32, kind="ExternalInput").ap()
    s1 = nc.dram_tensor("s1", [1, 1], F32, kind="ExternalInput").ap()
    s2 = nc.dram_tensor("s2", [1, 1], F32, kind="ExternalInput").ap()
    x_out = nc.dram_tensor("x_out", [D, T], F32, kind="ExternalOutput").ap()

    bufs = {}
    bufs["d0p_dram"] = nc.dram_tensor("d0p", [1024, 4096], BF16).ap()
    bufs["d1p_dram"] = nc.dram_tensor("d1p", [4096, 4096], BF16).ap()
    bufs["d2p_dram"] = nc.dram_tensor("d2p", [4096, 1024], BF16).ap()
    bufs["k0a_hbm"], bufs["k0b_hbm"] = k0a, k0b
    bufs["k1_hbm"], bufs["k2_hbm"] = k1, k2

    p = Prog(nc)
    make_psum_pool(p, bufs)
    make_ln_shared(p, bufs)
    setup_common(p, nc, bufs)
    scol, stoks = emit_scalar_bcast(p, bufs, [s0, s1, s2], [4.0, 2.0, 1.0])
    bufs["s0_col"], bufs["s1_col"], bufs["s2_col"] = scol[:, 0:1], scol[:, 1:2], scol[:, 2:3]

    xs = p.sbuf("x_res", [128, NT * T], F32)
    x_tiles = [xs[:, g * T:(g + 1) * T] for g in range(NT)]
    xloads = [p.op("sync", lambda s, g=g: s.dma_start(out=x_tiles[g], in_=x_in[128 * g:128 * (g + 1), :]), dma=True)
              for g in range(NT)]

    td0, td1, td2 = emit_signnet_convert(p, bufs)
    bufs["d0_ready"], bufs["d1_ready"], bufs["d2_ready"] = td0, td1, td2

    x_toks = emit_l0ffn(p, bufs, x_tiles, xloads + stoks)

    st = [p.op("sync", lambda s, g=g: s.dma_start(out=x_out[128 * g:128 * (g + 1), :], in_=x_tiles[g]),
               waits=x_toks, dma=True) for g in range(NT)]
    p.op("gpsimd", lambda g: g.engine_nop(), waits=st)
    p.finalize()
    return nc



def emit_rwkv(p, bufs, x_tiles, dep, tag="rwkv"):
    Q = 512
    QC = Q + 1
    ln3 = p.sbuf(f"{tag}_ln3", [128, NT * QC], BF16)
    ln3t = [ln3[:, g * QC:(g + 1) * QC] for g in range(NT)]
    cols = p.sbuf(f"{tag}_cols", [128, 48], F32)
    ecols = p.sbuf(f"{tag}_ecols", [128, 16], F32)
    tcol = p.op("sync", lambda s: s.dma_start(out=cols[:, :], in_=bufs["rwkv_cols"]), dma=True)
    te1 = p.op("scalar", lambda s: s.activation(ecols[:, 0:8], cols[:, 32:40], AF.Exp), waits=[tcol])
    te2 = p.op("scalar", lambda s: s.activation(ecols[:, 8:16], cols[:, 40:48], AF.Exp), waits=[tcol])
    te3 = p.op("scalar", lambda s: s.activation(ecols[:, 8:16], ecols[:, 8:16], AF.Exp, scale=-1.0), waits=[te2])
    col_toks = [te1, te3]

    kin = p.sbuf(f"{tag}_kin", [128, NT * Q], BF16)
    vin = p.sbuf(f"{tag}_vin", [128, NT * Q], BF16)
    rin = p.sbuf(f"{tag}_rin", [128, NT * Q], BF16)
    dQ = p.sbuf(f"{tag}_d", [128, Q], F32)
    ek = p.sbuf(f"{tag}_ek", [128, NT * Q], F32)
    ekv = p.sbuf(f"{tag}_ekv", [128, NT * Q], F32)
    rqs = p.sbuf(f"{tag}_rq", [128, Q], F32)
    rw = p.sbuf(f"{tag}_rw", [128, NT * Q], BF16)
    NUM = p.sbuf(f"{tag}_num", [128, NT * QC], F32)
    DEN = p.sbuf(f"{tag}_den", [128, NT * QC], F32)
    carry = p.sbuf(f"{tag}_carry", [128, 16], F32)
    _w = p.sbuf(f"{tag}_w", [128, NT * D], BF16)
    wbuf = [_w, _w]

    kin_t = [kin[:, g * Q:(g + 1) * Q] for g in range(NT)]
    vin_t = [vin[:, g * Q:(g + 1) * Q] for g in range(NT)]
    rin_t = [rin[:, g * Q:(g + 1) * Q] for g in range(NT)]
    ek_t = [ek[:, g * Q:(g + 1) * Q] for g in range(NT)]
    ek_t_pre = [ek[:, g * Q:g * Q + 257] for g in range(NT)]
    ekv_t = [ekv[:, g * Q:(g + 1) * Q] for g in range(NT)]
    rw_t = [rw[:, g * Q:(g + 1) * Q] for g in range(NT)]
    NUM_t = [NUM[:, g * QC:(g + 1) * QC] for g in range(NT)]
    DEN_t = [DEN[:, g * QC:(g + 1) * QC] for g in range(NT)]
    ps = bufs["ps_mm0"]

    def wslice(b, kt, g):
        return wbuf[b][:, kt * D + 128 * g: kt * D + 128 * (g + 1)]

    tz = p.op("vector", lambda v: v.memset(carry[:, :], 0.0))
    mmh = {}
    scan_toks = {}

    wk = bufs["wkT_dram"].rearrange("(kt k) o -> k kt o", k=128)
    wv = bufs["wvT_dram"].rearrange("(kt k) o -> k kt o", k=128)
    wr = bufs["wrT_dram"].rearrange("(kt k) o -> k kt o", k=128)
    wo = bufs["woT_dram"].rearrange("(kt k) o -> k kt o", k=128)
    wre = lambda b: wbuf[b][:, :].rearrange("k (kt o) -> k kt o", o=D)

    HW = 256
    hx = [ek_t_pre[g] for g in range(NT)]
    hloads = [p.op("sync", lambda s, g=g: s.dma_start(out=hx[g], in_=bufs["halo_x_dram"][128 * g:128 * (g + 1), :]), dma=True)
              for g in range(NT)]
    bufs["halo_x_ready"] = hloads
    hln = [rw[:, g * Q:g * Q + HW + 1] for g in range(NT)]
    ln_toksH = emit_ln_ttile(p, bufs, hx, slice(0, HW + 1), hln, slice(0, HW + 1),
                             bufs["halo_x_ready"] + list(dep), tag=f"{tag}_lnw")
    warm_mix = []
    for g in range(NT):
        t1 = p.op("vector", lambda v, g=g: v.tensor_sub(dQ[:, 0:HW], hln[g][:, 1:HW + 1], hln[g][:, 0:HW]),
                  waits=ln_toksH + warm_mix[-2:])
        t2 = p.op("vector", lambda v, g=g: v.scalar_tensor_tensor(
            kin_t[g][:, 0:HW], dQ[:, 0:HW], cols[:, 0 + g:1 + g], hln[g][:, 0:HW], ALU.mult, ALU.add),
            waits=[t1, tcol])
        t3 = p.op("vector", lambda v, g=g: v.scalar_tensor_tensor(
            vin_t[g][:, 0:HW], dQ[:, 0:HW], cols[:, 8 + g:9 + g], hln[g][:, 0:HW], ALU.mult, ALU.add),
            waits=[t1])
        warm_mix += [t2, t3]
    ln_toks0 = [p.op("vector", lambda v, g=g: v.tensor_copy(ln3t[g][:, 0:1], hln[g][:, HW:HW + 1]),
                     waits=ln_toksH) for g in range(NT)]
    lwk0 = p.op("gpsimd", lambda g_: g_.dma_start(out=wre(0), in_=wk[:, :, :]), dma=True)
    warm_ek = []
    for g in range(NT):
        def mmk(t, g=g):
            for kt in range(NT):
                last = t.matmul(ps[:, 0:HW], wslice(0, kt, g), kin_t[kt][:, 0:HW],
                                start=(kt == 0), stop=(kt == NT - 1))
            return last
        mt = p.op("tensor", mmk, waits=[lwk0] + warm_mix + warm_ek[-1:])
        mmh["w"] = mt
        et = p.op("scalar", lambda s, g=g: s.activation(ek_t[g][:, 0:HW], ps[:, 0:HW], AF.Exp), waits=[mt])
        warm_ek.append(et)
    lwv0 = p.op("gpsimd", lambda g_: g_.dma_start(out=wre(1), in_=wv[:, :, :]),
                waits=[mmh.get("w")], dma=True)
    warm_v = []
    for g in range(NT):
        def mmv(t, g=g):
            for kt in range(NT):
                last = t.matmul(ps[:, 0:HW], wslice(1, kt, g), vin_t[kt][:, 0:HW],
                                start=(kt == 0), stop=(kt == NT - 1))
            return last
        mt = p.op("tensor", mmv, waits=[lwv0] + warm_mix + (warm_v[-1:] if warm_v else warm_ek) + [warm_ek[g]])
        mmh["w1"] = mt
        et = p.op("vector", lambda v, g=g: v.tensor_copy(ekv_t[g][:, 0:HW], ps[:, 0:HW]), waits=[mt])
        et2 = p.op("vector", lambda v, g=g: v.tensor_mul(ekv_t[g][:, 0:HW], ekv_t[g][:, 0:HW], ek_t[g][:, 0:HW]),
                   waits=[et, warm_ek[g]])
        warm_v.append(et2)
    for g in range(NT):
        sN = p.op("vector", lambda v, g=g: v.tensor_tensor_scan(
            NUM_t[g][:, 0:HW], ecols[:, 8 + g:9 + g].broadcast_to((128, HW)), ekv_t[g][:, 0:HW],
            0.0, ALU.mult, ALU.add), waits=[warm_v[g]] + col_toks)
        ccN = p.op("vector", lambda v, g=g: v.tensor_scalar_mul(
            carry[:, g:g + 1], NUM_t[g][:, HW - 1:HW], bufs["carry_mask"]),
            waits=[sN, tz] + bufs["cmask_ready"])
        sD = p.op("vector", lambda v, g=g: v.tensor_tensor_scan(
            DEN_t[g][:, 0:HW], ecols[:, 8 + g:9 + g].broadcast_to((128, HW)), ek_t[g][:, 0:HW],
            0.0, ALU.mult, ALU.add), waits=[warm_ek[g], ccN] + col_toks)
        ccD = p.op("vector", lambda v, g=g: v.tensor_scalar_mul(
            carry[:, 8 + g:9 + g], DEN_t[g][:, HW - 1:HW], bufs["carry_mask"]), waits=[sD])
        scan_toks[("n", g)] = ccD
        scan_toks[("d", g)] = ccD

    warm_done = [scan_toks[("n", g)] for g in range(NT)]
    x_toks = list(dep)
    halo_toks = ln_toks0
    prev_mix = []
    for q in range(4):
        tsl = slice(Q * q, Q * (q + 1))
        ln_toks = emit_ln_ttile(p, bufs, x_tiles, tsl, ln3t, slice(1, QC),
                                (x_toks[-NT:] if q else list(dep)) + prev_mix,
                                tag=f"{tag}_ln{q}")
        dep_ln = ln_toks + halo_toks + (warm_done if q == 0 else [])
        mix_toks = []
        for g in range(NT):
            t1 = p.op("vector", lambda v, g=g: v.tensor_sub(dQ[:, :], ln3t[g][:, 1:QC], ln3t[g][:, 0:Q]),
                      waits=dep_ln + mix_toks[-3:])
            t2 = p.op("vector", lambda v, g=g: v.scalar_tensor_tensor(
                kin_t[g], dQ[:, :], cols[:, 0 + g:1 + g], ln3t[g][:, 0:Q], ALU.mult, ALU.add),
                waits=[t1, tcol])
            t3 = p.op("vector", lambda v, g=g: v.scalar_tensor_tensor(
                vin_t[g], dQ[:, :], cols[:, 8 + g:9 + g], ln3t[g][:, 0:Q], ALU.mult, ALU.add),
                waits=[t1])
            t4 = p.op("vector", lambda v, g=g: v.scalar_tensor_tensor(
                rin_t[g], dQ[:, :], cols[:, 16 + g:17 + g], ln3t[g][:, 0:Q], ALU.mult, ALU.add),
                waits=[t1])
            mix_toks += [t2, t3, t4]
        halo_toks = []
        for g in range(NT):
            hc = p.op("vector", lambda v, g=g: v.tensor_copy(ln3t[g][:, 0:1], ln3t[g][:, Q:QC]),
                      waits=mix_toks)
            halo_toks.append(hc)
        prev_mix = halo_toks

        lw = p.op("gpsimd", lambda g_: g_.dma_start(out=wre(0), in_=wk[:, :, :]),
                  waits=[mmh.get("w"), mmh.get("w1")], dma=True)
        ev_toks = []
        for g in range(NT):
            def mmk(t, g=g):
                for kt in range(NT):
                    last = t.matmul(ps, wslice(0, kt, g), kin_t[kt],
                                    start=(kt == 0), stop=(kt == NT - 1))
                return last
            mt = p.op("tensor", mmk, waits=[lw] + mix_toks + ev_toks[-1:])
            mmh["w"] = mt
            et = p.op("scalar", lambda s, g=g: s.activation(ek_t[g], ps, AF.Exp), waits=[mt])
            ev_toks.append(et)
        lwv = p.op("gpsimd", lambda g_: g_.dma_start(out=wre(1), in_=wv[:, :, :]),
                   waits=[mmh.get("w"), mmh.get("w1")], dma=True)
        vv_toks = []
        for g in range(NT):
            def mmv(t, g=g):
                for kt in range(NT):
                    last = t.matmul(ps, wslice(1, kt, g), vin_t[kt],
                                    start=(kt == 0), stop=(kt == NT - 1))
                return last
            mt = p.op("tensor", mmv, waits=[lwv] + mix_toks + (vv_toks[-1:] if vv_toks else ev_toks) + [ev_toks[g]])
            mmh["w"] = mt
            et = p.op("vector", lambda v, g=g: v.tensor_copy(ekv_t[g], ps), waits=[mt])
            et2 = p.op("vector", lambda v, g=g: v.tensor_mul(ekv_t[g], ekv_t[g], ek_t[g]),
                       waits=[et, ev_toks[g]])
            vv_toks.append(et2)
        new_scan = {}
        for g in range(NT):
            prevN = scan_toks.get(("n", g))
            cN = p.op("vector", lambda v, g=g: v.tensor_copy(NUM_t[g][:, 0:1], carry[:, g:g + 1]),
                      waits=[tz, prevN])
            sN = p.op("vector", lambda v, g=g: v.tensor_tensor_scan(
                NUM_t[g][:, 1:QC], ecols[:, 8 + g:9 + g].broadcast_to((128, Q)), ekv_t[g],
                NUM_t[g][:, 0:1], ALU.mult, ALU.add), waits=[cN, vv_toks[g]] + col_toks)
            ccN = p.op("vector", lambda v, g=g: v.tensor_copy(carry[:, g:g + 1], NUM_t[g][:, Q:QC]),
                       waits=[sN])
            prevD = scan_toks.get(("d", g))
            cD = p.op("vector", lambda v, g=g: v.tensor_copy(DEN_t[g][:, 0:1], carry[:, 8 + g:9 + g]),
                      waits=[tz, prevD])
            sD = p.op("vector", lambda v, g=g: v.tensor_tensor_scan(
                DEN_t[g][:, 1:QC], ecols[:, 8 + g:9 + g].broadcast_to((128, Q)), ek_t[g],
                DEN_t[g][:, 0:1], ALU.mult, ALU.add), waits=[cD, ev_toks[g]] + col_toks)
            ccD = p.op("vector", lambda v, g=g: v.tensor_copy(carry[:, 8 + g:9 + g], DEN_t[g][:, Q:QC]),
                       waits=[sD])
            new_scan[("n", g)] = ccN
            new_scan[("d", g)] = ccD
        scan_toks = new_scan
        ekvdump = []
        if q == 0 and "dbg_rwkv" in bufs and "ekvraw" in bufs["dbg_rwkv"]:
            dd = bufs["dbg_rwkv"]
            ekvdump = [p.op("gpsimd", lambda s, g=g: s.dma_start(
                out=dd["ekvraw"][128 * g:128 * (g + 1), :], in_=ekv_t[g]), waits=vv_toks, dma=True)
                for g in range(NT)]
        wkv_toks = []
        for g in range(NT):
            e1 = p.op("vector", lambda v, g=g: v.tensor_scalar_mul(ekv_t[g], ekv_t[g], ecols[:, 0 + g:1 + g]),
                      waits=[scan_toks[("n", g)]] + ekvdump)
            e2 = p.op("vector", lambda v, g=g: v.tensor_add(ekv_t[g], ekv_t[g], NUM_t[g][:, 0:Q]), waits=[e1])
            e3 = p.op("vector", lambda v, g=g: v.tensor_scalar_mul(ek_t[g], ek_t[g], ecols[:, 0 + g:1 + g]),
                      waits=[scan_toks[("d", g)]])
            e4 = p.op("vector", lambda v, g=g: v.tensor_add(ek_t[g], ek_t[g], DEN_t[g][:, 0:Q]), waits=[e3])
            e5 = p.op("vector", lambda v, g=g: v.reciprocal(ek_t[g], ek_t[g]), waits=[e4])
            e6 = p.op("vector", lambda v, g=g: v.tensor_mul(ekv_t[g], ekv_t[g], ek_t[g]), waits=[e2, e5])
            wkv_toks.append(e6)
        if q == 0 and "dbg_rwkv" in bufs:
            dd = bufs["dbg_rwkv"]
            for g in range(NT):
                p.op("gpsimd", lambda s, g=g: s.dma_start(out=dd["ln3"][128 * g:128 * (g + 1), :], in_=ln3t[g][:, 0:QC]), waits=mix_toks + halo_toks, dma=True)
                p.op("gpsimd", lambda s, g=g: s.dma_start(out=dd["kin"][128 * g:128 * (g + 1), :], in_=kin_t[g]), waits=mix_toks, dma=True)
                p.op("gpsimd", lambda s, g=g: s.dma_start(out=dd["ek"][128 * g:128 * (g + 1), :], in_=ek_t[g]), waits=[e for e in wkv_toks], dma=True)
                p.op("gpsimd", lambda s, g=g: s.dma_start(out=dd["wkv"][128 * g:128 * (g + 1), :], in_=ekv_t[g]), waits=wkv_toks, dma=True)
                p.op("gpsimd", lambda s, g=g: s.dma_start(out=dd["num"][128 * g:128 * (g + 1), :], in_=NUM_t[g]), waits=wkv_toks, dma=True)
        lwr = p.op("gpsimd", lambda g_: g_.dma_start(out=wre(0), in_=wr[:, :, :]),
                   waits=[mmh.get("w")], dma=True)
        rw_toks = []
        for g in range(NT):
            def mmr(t, g=g):
                for kt in range(NT):
                    last = t.matmul(ps, wslice(0, kt, g), rin_t[kt],
                                    start=(kt == 0), stop=(kt == NT - 1))
                return last
            mt = p.op("tensor", mmr, waits=[lwr] + mix_toks + (rw_toks[-1:] if rw_toks else vv_toks))
            mmh["w"] = mt
            st1 = p.op("scalar", lambda s, g=g: s.activation(rqs[:, :], ps, AF.Sigmoid),
                       waits=[mt] + rw_toks[-1:])
            st2 = p.op("vector", lambda v, g=g: v.tensor_mul(rw_t[g], rqs[:, :], ekv_t[g]),
                       waits=[st1, wkv_toks[g]])
            rw_toks.append(st2)
        lwo = p.op("gpsimd", lambda g_: g_.dma_start(out=wre(1), in_=wo[:, :, :]),
                   waits=[mmh.get("w")], dma=True)
        res_toks = []
        for g in range(NT):
            def mmo(t, g=g):
                for kt in range(NT):
                    last = t.matmul(ps, wslice(1, kt, g), rw_t[kt],
                                    start=(kt == 0), stop=(kt == NT - 1))
                return last
            mt = p.op("tensor", mmo, waits=[lwo] + rw_toks + res_toks[-1:])
            mmh["w"] = mt
            et = p.op("vector", lambda v, g=g, tsl=tsl: v.scalar_tensor_tensor(
                x_tiles[g][:, tsl], ps, cols[:, 24 + g:25 + g], x_tiles[g][:, tsl], ALU.add, ALU.add),
                waits=[mt])
            res_toks.append(et)
        x_toks = res_toks
    return x_toks


def build_stage_c():
    nc = bass.Bass("TRN2", target_bir_lowering=False)
    x_in = nc.dram_tensor("x_in", [D, T], F32, kind="ExternalInput").ap()
    halo_x = nc.dram_tensor("halo_x", [D, 257], F32, kind="ExternalInput").ap()
    cmask = nc.dram_tensor("cmask", [128, 1], F32, kind="ExternalInput").ap()
    wkT = nc.dram_tensor("wkT", [D, D], F32, kind="ExternalInput").ap()
    wvT = nc.dram_tensor("wvT", [D, D], F32, kind="ExternalInput").ap()
    wrT = nc.dram_tensor("wrT", [D, D], F32, kind="ExternalInput").ap()
    woT = nc.dram_tensor("woT", [D, D], F32, kind="ExternalInput").ap()
    rcols = nc.dram_tensor("rwkv_cols", [128, 48], F32, kind="ExternalInput").ap()
    x_out = nc.dram_tensor("x_out", [D, T], F32, kind="ExternalOutput").ap()
    bufs = {}
    bufs["wkT_dram"], bufs["wvT_dram"] = wkT, wvT
    bufs["wrT_dram"], bufs["woT_dram"] = wrT, woT
    bufs["rwkv_cols"] = rcols

    p = Prog(nc)
    make_psum_pool(p, bufs)
    make_ln_shared(p, bufs)
    setup_common(p, nc, bufs)

    xs = p.sbuf("x_res", [128, NT * T], F32)
    x_tiles = [xs[:, g * T:(g + 1) * T] for g in range(NT)]
    xloads = [p.op("sync", lambda s, g=g: s.dma_start(out=x_tiles[g], in_=x_in[128 * g:128 * (g + 1), :]), dma=True)
              for g in range(NT)]
    hx_tiles = None
    bufs["halo_x_dram"] = halo_x
    cm_sb = p.sbuf("cmask", [128, 1], F32)
    cml = p.op("sync", lambda s: s.dma_start(out=cm_sb[:, :], in_=cmask), dma=True)
    bufs["carry_mask"] = cm_sb[:, :]
    bufs["cmask_ready"] = [cml]

    x_toks = emit_rwkv(p, bufs, x_tiles, xloads + bufs["consts_ready"])

    st = [p.op("sync", lambda s, g=g: s.dma_start(out=x_out[128 * g:128 * (g + 1), :], in_=x_tiles[g]),
               waits=x_toks, dma=True) for g in range(NT)]
    p.op("gpsimd", lambda g: g.engine_nop(), waits=st)
    p.finalize()
    return nc



def emit_perf_part1(p, bufs, x_tiles, dep, out, tag="pf1", phases=("qkv", "v", "kstab", "qstab")):
    lnb = [p.sbuf(f"{tag}_ln{g}", [128, T], BF16) for g in range(NT)]
    wq = p.sbuf(f"{tag}_wq", [128, NT * 1024], BF16)
    q2c = p.sbuf(f"{tag}_q2", [128, 512], F32)
    sk4 = p.sbuf(f"{tag}_sk4", [128, 4], F32)
    sk = p.sbuf(f"{tag}_sk", [128, 16], F32)
    seld = bufs["seld_sb"]
    pmTb = bufs["pmT_sb"]
    pmPb = bufs["pmTpair_sb"]
    ident = bufs["ident_sb"]
    q_t, k_t, vtm = out["q"], out["k"], out["vtm"]
    ps0 = bufs["ps_mm0"]
    dbank = [bufs["ps_mm1"], bufs["ps_rb"], bufs["ps_mrb"], bufs["ps_x0"]]
    wqr = bufs["wqkv_dram"].rearrange("(kt k) o -> k kt o", k=128)

    def load_wpart(part, waits):
        return p.op("gpsimd", lambda g, part=part: g.dma_start(
            out=wq[:, :].rearrange("k (kt o) -> k kt o", o=1024),
            in_=wqr[:, :, 1024 * part:1024 * (part + 1)]), waits=waits, dma=True)

    ln_toks = []
    for tt in range(4):
        prev = ln_toks[-NT:]
        xl = [p.op("sync", lambda s, g=g, tt=tt: s.dma_start(
            out=x_tiles[g], in_=bufs["x_in_dram"][128 * g:128 * (g + 1), 512 * tt:512 * (tt + 1)]),
            waits=list(dep) + prev, dma=True) for g in range(NT)]
        ln_toks += emit_ln_ttile(p, bufs, x_tiles, slice(0, 512),
                                 lnb, slice(512 * tt, 512 * (tt + 1)),
                                 xl + prev, tag=f"{tag}_ln{tt}")

    vre = vtm.rearrange("p (tc h c) -> p tc h c", h=16, c=65)
    tones = p.op("vector", lambda v: v.memset(vre[:, :, :, 64:65], 1.0))
    tz1 = p.op("vector", lambda v: v.memset(out["sq"], 0.0))
    tz2 = p.op("vector", lambda v: v.memset(sk[:, :], 0.0))

    toks = []
    diag_toks = []
    prev_read = list(ln_toks)
    lwq = None
    for part in (range(2) if "qkv" in phases else []):
        lwq = load_wpart(part, [lwq] + prev_read if part else [lwq])
        dst = q_t if part == 0 else k_t
        for cc in range(4):
            csl = slice(512 * cc, 512 * (cc + 1))
            for ot in range(8):
                def mm(t, ot=ot, csl=csl):
                    for kt in range(NT):
                        base = 1024 * kt + 128 * ot
                        last = t.matmul(ps0, wq[:, base:base + 128], lnb[kt][:, csl],
                                        start=(kt == 0), stop=(kt == NT - 1))
                    return last
                mt = p.op("tensor", mm, waits=[lwq] + prev_read)
                prev_read = []
                ev = p.op("vector", lambda v, ot=ot, csl=csl, dst=dst: v.tensor_copy(dst[ot][:, csl], ps0),
                          waits=[mt] + toks[-1:])
                toks.append(ev)
                if "sq" in phases:
                    sq_ = p.op("scalar", lambda s, ot=ot, csl=csl, dst=dst: s.activation(
                        q2c[:, :], dst[ot][:, csl], AF.Square), waits=[ev] + diag_toks[-1:])
                    prev_read.append(sq_)
                    if "diag" in phases:
                        def mmd(t, part=part, cc=cc, ot=ot):
                            for sc in range(4):
                                co = 16 * (4 * part + cc)
                                last = t.matmul(dbank[sc][:, co:co + 16],
                                                q2c[:, 128 * sc:128 * (sc + 1)], seld[:, 16 * ot:16 * (ot + 1)],
                                                start=(ot == 0), stop=(ot == 7))
                            return last
                        dt_ = p.op("tensor", mmd, waits=[sq_] + bufs["seld_ready"])
                        diag_toks.append(dt_)
                prev_read.append(ev)
    v_toks = []
    if "v" in phases:
        lwq = load_wpart(2, [lwq] + prev_read)
    for tc in (range(16) if "v" in phases else []):
        for half in range(2):
            def mmv(t, tc=tc, half=half):
                for kt in range(NT):
                    base = 1024 * kt + 512 * half
                    last = t.matmul(ps0, lnb[kt][:, 128 * tc:128 * (tc + 1)], wq[:, base:base + 512],
                                    start=(kt == 0), stop=(kt == NT - 1))
                return last
            mt = p.op("tensor", mmv, waits=[lwq] + ln_toks + prev_read)
            ev = p.op("vector", lambda v, tc=tc, half=half: v.tensor_copy(
                vre[:, tc, 8 * half:8 * (half + 1), 0:64],
                ps0[:, :].rearrange("p (h d) -> p h d", d=64)), waits=[mt, tones])
            prev_read = [ev]
            v_toks.append(ev)

    dq_sb, dk_sb = out["diagq"], out["diagk"]
    ed_toks = []
    for part in (range(2) if "diag" in phases else []):
        dsb = dq_sb if part == 0 else dk_sb
        for sc in range(4):
            def cp(v, part=part, sc=sc, dsb=dsb):
                return v.tensor_copy(
                    dsb.rearrange("p (cc four h) -> p cc four h", four=4, h=16)[:, :, sc, :],
                    dbank[sc][:, 0:128].rearrange("p (pp cc h) -> p pp cc h", pp=2, h=16)[:, part, :, :])
            ed_toks.append(p.op("vector", cp, waits=diag_toks))

    sk_toks = []
    ubank = [bufs["ps_mm1"], bufs["ps_rb"], bufs["ps_mrb"], bufs["ps_x0"]]
    for h in (range(16) if "kstab" in phases else []):
        g, off = h // 2, 64 * (h % 2)
        def mmu(t, g=g, off=off, h=h):
            for cc in range(4):
                last = t.matmul(ubank[cc][:, :], bufs["pmT_lhsT"][h], k_t[g][off:off + 64, 512 * cc:512 * (cc + 1)],
                                start=True, stop=True)
            return last
        mt = p.op("tensor", mmu, waits=toks + v_toks + ed_toks + sk_toks[-1:] + bufs["pm_ready"])
        def red(v, h=h):
            for cc in range(3):
                v.reduce_max(sk4[:, cc:cc + 1], ubank[cc][:, :], axis=AX.X)
            return v.reduce_max(sk4[:, 3:4], ubank[3][:, :], axis=AX.X)
        rd = p.op("vector", red, waits=[mt, tz2])
        rd2 = p.op("vector", lambda v, h=h: v.reduce_max(sk[:, h:h + 1], sk4[:, :], axis=AX.X), waits=[rd])
        sk_toks.append(rd2)
    if "kstab" in phases:
        ttr = p.op("tensor", lambda t: t.transpose(bufs["ps_mm1"][0:16, 0:128], sk[:, :], ident[:, :]),
                   waits=sk_toks + ed_toks + bufs["ident_ready"])
        tsk = p.op("vector", lambda v: v.reduce_max(out["skcol"], bufs["ps_mm1"][0:16, 0:128], axis=AX.X),
                   waits=[ttr])
    else:
        tsk = p.op("vector", lambda v: v.memset(out["skcol"], 0.0))

    sq_sb = out["sq"]
    qs_toks = [tsk]
    for tc in (range(16) if "qstab2" in phases else []):
        for pr in range(8):
            def mmp(t, tc=tc, pr=pr):
                return t.matmul(ps0[:, 0:256], q_t[pr][:, 128 * tc:128 * (tc + 1)],
                                pmPb[:, 256 * pr:256 * (pr + 1)], start=True, stop=True)
            mt = p.op("tensor", mmp, waits=qs_toks[-1:] + toks + [tsk] + bufs["pm_ready"])
            rd = p.op("vector", lambda v, tc=tc, pr=pr: v.tensor_reduce(
                sq_sb[:, 16 * tc + 2 * pr:16 * tc + 2 * pr + 2],
                ps0[:, 0:256].rearrange("p (h f) -> p h f", f=128), AX.X, ALU.max), waits=[mt, tz1])
            qs_toks.append(rd)
    for tc in (range(16) if "qstab" in phases else []):
        def mmp(t, tc=tc):
            for pr in range(8):
                last = t.matmul(ubank[pr // 2][:, 256 * (pr % 2):256 * (pr % 2) + 256],
                                q_t[pr][:, 128 * tc:128 * (tc + 1)],
                                pmPb[:, 256 * pr:256 * (pr + 1)], start=True, stop=True)
            return last
        mt = p.op("tensor", mmp, waits=qs_toks[-1:] + toks + [tsk] + bufs["pm_ready"])
        def red(v, tc=tc):
            for pr in range(8):
                last = v.tensor_reduce(
                    sq_sb[:, 16 * tc + 2 * pr:16 * tc + 2 * pr + 2],
                    ubank[pr // 2][:, 256 * (pr % 2):256 * (pr % 2) + 256].rearrange("p (h f) -> p h f", f=128),
                    AX.X, ALU.max)
            return last
        rd = p.op("vector", red, waits=[mt, tz1])
        qs_toks.append(rd)
    return toks + v_toks + ed_toks + qs_toks + [tsk]


def emit_perf_part2(p, bufs, dep, kin, out, tag="pf2"):
    k_t, vtm = kin["k"], kin["vtm"]
    diagk = kin["diagk"]
    pmTb = bufs["pmT_sb"]
    kf = p.sbuf(f"{tag}_kf", [128, 256], BF16)
    dkt = p.sbuf(f"{tag}_dkt", [128, 256], F32)
    sbb = p.sbuf(f"{tag}_sbb", [128, 16], F32)
    ps0 = bufs["ps_mm0"]
    kv0 = bufs["ps_mm1"][:, 0:65]
    kv1 = bufs["ps_rb"][:, 0:65]
    vre = vtm.rearrange("p (tc h c) -> p tc h c", h=16, c=65)

    t1 = p.op("tensor", lambda t: t.matmul(bufs["ps_mrb"][:, 0:16], bufs["ones_row_f"],
                                           bufs["stabk_row_sb"], start=True, stop=True),
              waits=dep + bufs["stabk_ready"] + bufs["consts_ready"])
    t2 = p.op("vector", lambda v: v.tensor_copy(sbb[:, :], bufs["ps_mrb"][:, 0:16]), waits=[t1])
    t3 = p.op("vector", lambda v: v.tensor_sub(
        dkt.rearrange("p (tc h) -> p tc h", h=16),
        diagk.rearrange("p (tc h) -> p tc h", h=16),
        sbb[:, :].unsqueeze(1).broadcast_to((128, 16, 16))), waits=[t2] + dep)

    kv_toks = []
    ev_toks = []
    for h in range(16):
        g, off = h // 2, 64 * (h % 2)
        htoks = []
        for tc in range(16):
            def mmu(t, g=g, off=off, h=h, tc=tc):
                return t.matmul(ps0[:, 0:128], k_t[g][off:off + 64, 128 * tc:128 * (tc + 1)],
                                bufs["pmT_rhs"][h], start=True, stop=True)
            mt = p.op("tensor", mmu, waits=[t3] + htoks[-2:] + ev_toks[-1:] + bufs["pm_ready"] + dep)
            e1 = p.op("scalar", lambda s, h=h, tc=tc: s.activation(
                kf[:, 0:128], ps0[:, 0:128], AF.Exp, bias=dkt[:, 16 * tc + h:16 * tc + h + 1],
                scale=0.125), waits=[mt])
            e2 = p.op("scalar", lambda s, h=h, tc=tc: s.activation(
                kf[:, 128:256], ps0[:, 0:128], AF.Exp, bias=dkt[:, 16 * tc + h:16 * tc + h + 1],
                scale=-0.125), waits=[mt])
            def mk0(t, h=h, tc=tc):
                return t.matmul(kv0, kf[:, 0:128], vre[:, tc, h, :], start=(tc == 0), stop=(tc == 15))
            def mk1(t, h=h, tc=tc):
                return t.matmul(kv1, kf[:, 128:256], vre[:, tc, h, :], start=(tc == 0), stop=(tc == 15))
            m1 = p.op("tensor", mk0, waits=[e1, e2])
            m2 = p.op("tensor", mk1, waits=[e1, e2])
            htoks = [m1, m2]
        c1 = p.op("vector", lambda v, h=h: v.tensor_copy(out["kvloc"][:, 130 * h:130 * h + 65], kv0),
                  waits=htoks)
        c2 = p.op("vector", lambda v, h=h: v.tensor_copy(out["kvloc"][:, 130 * h + 65:130 * h + 130], kv1),
                  waits=htoks)
        ev_toks = [c1, c2]
        kv_toks += [c1, c2]
    return kv_toks


def emit_perf_part3(p, bufs, x_tiles, dep, qin, tag="pf3"):
    q_t = qin["q"]
    kvb = qin["kvb"]
    diagq, sq_sb = qin["diagq"], qin["sq"]
    pmTb = bufs["pmT_sb"]
    ident = bufs["ident_sb"]
    wo = p.sbuf(f"{tag}_wo", [128, NT * 1024], BF16)
    qf = p.sbuf(f"{tag}_qf", [128, 1024], BF16)
    att = [p.sbuf(f"{tag}_att{g}", [128, T], BF16) for g in range(NT)]
    abuf = p.sbuf(f"{tag}_abuf", [64, 512], F32)
    zrow = p.sbuf(f"{tag}_zrow", [1, 512], F32)
    zrec = p.sbuf(f"{tag}_zrec", [1, 512], BF16)
    eps_t = p.sbuf(f"{tag}_epst", [128, 256], F32)
    eps_rows = p.sbuf(f"{tag}_epsr", [16, T], F32)
    eps_stage = p.sbuf(f"{tag}_epss", [1, T], F32)
    edram = p.nc.dram_tensor(p.name(f"{tag}_edram"), [16, T], F32).ap()
    ps0 = bufs["ps_mm0"]
    psA = bufs["ps_mm1"][:, 0:512]
    psRZ = bufs["ps_rb"][:, 0:512]
    psT = bufs["ps_mrb"]
    wor = bufs["wout_dram"].rearrange("(kt k) o -> k kt o", k=128)

    lwo = p.op("gpsimd", lambda g: g.dma_start(
        out=wo[:, :].rearrange("k (kt o) -> k kt o", o=1024), in_=wor[:, :, :]), dma=True)

    LNEPS = float(np.log(1e-6 * 256.0))
    et_toks = []
    for tc in range(16):
        s1 = p.op("vector", lambda v, tc=tc: v.scalar_tensor_tensor(
            eps_t[:, 16 * tc:16 * (tc + 1)], sq_sb[:, 16 * tc:16 * (tc + 1)], 0.125,
            diagq[:, 16 * tc:16 * (tc + 1)], ALU.mult, ALU.subtract), waits=dep)
        s1b = p.op("vector", lambda v, tc=tc: v.tensor_scalar_add(
            eps_t[:, 16 * tc:16 * (tc + 1)], eps_t[:, 16 * tc:16 * (tc + 1)], LNEPS), waits=[s1])
        s2 = p.op("scalar", lambda s, tc=tc: s.activation(
            eps_t[:, 16 * tc:16 * (tc + 1)], eps_t[:, 16 * tc:16 * (tc + 1)], AF.Exp),
            waits=[s1b])
        s3 = p.op("tensor", lambda t, tc=tc: t.transpose(psT[0:16, 0:128],
                                                         eps_t[:, 16 * tc:16 * (tc + 1)], ident[:, :]),
                  waits=[s2] + et_toks[-1:])
        s4 = p.op("vector", lambda v, tc=tc: v.tensor_copy(eps_rows[:, 128 * tc:128 * (tc + 1)],
                                                           psT[0:16, 0:128]), waits=[s3])
        et_toks.append(s4)
    d1 = p.op("sync", lambda s: s.dma_start(out=edram, in_=eps_rows[:, :]), waits=et_toks, dma=True)

    res_toks = []
    att_toks = []
    prev_u = []
    prev_att = []
    for h in range(16):
        g, off = h // 2, 64 * (h % 2)
        eh = p.op("sync", lambda s, h=h: s.dma_start(out=eps_stage[:, :], in_=edram[h:h + 1, :]),
                  waits=[d1] + prev_att, dma=True)
        for cc in range(4):
            csl = slice(512 * cc, 512 * (cc + 1))
            def mmu(t, g=g, off=off, h=h, cc=cc, csl=csl):
                return t.matmul(ps0[:, 0:512], bufs["pmT_lhsT"][h], q_t[g][off:off + 64, csl],
                                start=True, stop=True)
            mt = p.op("tensor", mmu, waits=prev_u + bufs["pm_ready"] + dep)
            e1 = p.op("scalar", lambda s: s.activation(qf[:, 0:512], ps0[:, 0:512], AF.Exp, scale=0.125),
                      waits=[mt])
            e2 = p.op("scalar", lambda s: s.activation(qf[:, 512:1024], ps0[:, 0:512], AF.Exp, scale=-0.125),
                      waits=[mt])
            prev_u = [e1, e2]
            def mA(t, h=h):
                t.matmul(psA[0:65, :], kvb[:, 130 * h:130 * h + 65], qf[:, 0:512], start=True, stop=False)
                return t.matmul(psA[0:65, :], kvb[:, 130 * h + 65:130 * (h + 1)], qf[:, 512:1024],
                                start=False, stop=True)
            mt2 = p.op("tensor", mA, waits=[e1, e2] + prev_att + qin["kvb_ready"])
            z1 = p.op("vector", lambda v, h=h, cc=cc: v.tensor_add(
                zrow[:, :], psA[64:65, :], eps_stage[0:1, 512 * cc:512 * (cc + 1)]),
                waits=[mt2, eh])
            z2 = p.op("vector", lambda v: v.reciprocal(zrow[:, :], zrow[:, :]), waits=[z1])
            z3 = p.op("vector", lambda v: v.tensor_copy(zrec[:, :], zrow[:, :]), waits=[z2])
            a1 = p.op("scalar", lambda s: s.activation(abuf[:, :], psA[0:64, :], AF.Copy), waits=[mt2])
            r1 = p.op("tensor", lambda t: t.matmul(psRZ[0:64, :], bufs["ones_row_bf"][0:1, 0:64],
                                                   zrec[:, :], start=True, stop=True),
                      waits=[z3] + prev_att)
            a2 = p.op("vector", lambda v, g=g, off=off, csl=csl: v.tensor_mul(
                att[g][off:off + 64, csl], abuf[:, :], psRZ[0:64, :]), waits=[a1, r1])
            prev_att = [a2]
            att_toks.append(a2)
    for ot in range(NT):
        for cc in range(4):
            csl = slice(512 * cc, 512 * (cc + 1))
            def mm(t, ot=ot, csl=csl):
                for kt in range(NT):
                    last = t.matmul(ps0[:, 0:512], wo[:, 1024 * kt + 128 * ot:1024 * kt + 128 * (ot + 1)],
                                    att[kt][:, csl], start=(kt == 0), stop=(kt == NT - 1))
                return last
            mt = p.op("tensor", mm, waits=[lwo] + att_toks + res_toks[-1:] + prev_u)
            rv_ = p.op("vector", lambda v, ot=ot, csl=csl: v.tensor_add(
                x_tiles[ot][:, csl], x_tiles[ot][:, csl], ps0[:, 0:512]), waits=[mt])
            res_toks.append(rv_)
    return res_toks



def _perf_consts(p, nc, bufs):
    pmT_in = nc.dram_tensor("pmT", [1024, 128], F32, kind="ExternalInput").ap()
    pmP_in = nc.dram_tensor("pmTpair", [128, 2048], F32, kind="ExternalInput").ap()
    seld_in = nc.dram_tensor("seld", [128, 128], F32, kind="ExternalInput").ap()
    id_in = nc.dram_tensor("ident", [128, 128], F32, kind="ExternalInput").ap()
    pmT_sb = p.sbuf("pmT", [128, NT * 128], BF16)
    pmP_sb = p.sbuf("pmP", [128, 2048], BF16)
    seld_sb = p.sbuf("seld", [128, 128], F32)
    id_sb = p.sbuf("identf", [128, 128], F32)
    t1 = p.op("gpsimd", lambda g: g.dma_start(
        out=pmT_sb[:, :].rearrange("k (j o) -> k j o", o=128),
        in_=pmT_in.rearrange("(j k) o -> k j o", k=128)), dma=True)
    t2 = p.op("gpsimd", lambda g: g.dma_start(out=pmP_sb[:, :], in_=pmP_in), dma=True)
    t3 = p.op("sync", lambda s: s.dma_start(out=seld_sb[:, :], in_=seld_in), dma=True)
    t4 = p.op("sync", lambda s: s.dma_start(out=id_sb[:, :], in_=id_in), dma=True)
    bufs["pmT_sb"] = pmT_sb
    bufs["pmT_lhsT"] = [pmT_sb[64 * (h % 2):64 * (h % 2) + 64, 128 * (h // 2):128 * (h // 2 + 1)]
                        for h in range(16)]
    bufs["pmT_rhs"] = bufs["pmT_lhsT"]
    bufs["pmTpair_sb"] = pmP_sb[:, :]
    bufs["seld_sb"] = seld_sb[:, :]
    bufs["ident_sb"] = id_sb[:, :]
    bufs["pm_ready"] = [t1, t2]
    bufs["seld_ready"] = [t3]
    bufs["ident_ready"] = [t4]
    return [t1, t2, t3, t4]


def build_stage_a1(phases=("qkv", "v", "kstab", "qstab")):
    nc = bass.Bass("TRN2", target_bir_lowering=False)
    x_in = nc.dram_tensor("x_in", [D, T], F32, kind="ExternalInput").ap()
    wqkv = nc.dram_tensor("wqkv", [D, 3072], F32, kind="ExternalInput").ap()
    q_out = nc.dram_tensor("q_out", [D, T], BF16, kind="ExternalOutput").ap()
    k_out = nc.dram_tensor("k_out", [D, T], BF16, kind="ExternalOutput").ap()
    v_out = nc.dram_tensor("v_out", [128, 16 * 1040], BF16, kind="ExternalOutput").ap()
    dq_out = nc.dram_tensor("dq_out", [128, 256], F32, kind="ExternalOutput").ap()
    dk_out = nc.dram_tensor("dk_out", [128, 256], F32, kind="ExternalOutput").ap()
    sq_out = nc.dram_tensor("sq_out", [128, 256], F32, kind="ExternalOutput").ap()
    sk_out = nc.dram_tensor("sk_out", [16, 1], F32, kind="ExternalOutput").ap()

    bufs = {"wqkv_dram": wqkv}
    p = Prog(nc)
    make_psum_pool(p, bufs)
    make_ln_shared(p, bufs)
    setup_common(p, nc, bufs)
    _perf_consts(p, nc, bufs)

    xs = p.sbuf("x_stream", [128, NT * 512], F32)
    xq_tiles = [xs[:, g * 512:(g + 1) * 512] for g in range(NT)]

    out = {
        "q": [p.sbuf(f"qfm{g}", [128, T], BF16)[:, :] for g in range(NT)],
        "k": [p.sbuf(f"kfm{g}", [128, T], BF16)[:, :] for g in range(NT)],
        "vtm": p.sbuf("vtm", [128, 16 * 1040], BF16)[:, :],
        "diagq": p.sbuf("diagq", [128, 256], F32)[:, :],
        "diagk": p.sbuf("diagk", [128, 256], F32)[:, :],
        "sq": p.sbuf("sq", [128, 256], F32)[:, :],
        "skcol": p.sbuf("skcol", [16, 1], F32)[:, :],
    }
    bufs["x_in_dram"] = x_in
    toks = emit_perf_part1(p, bufs, xq_tiles, bufs["consts_ready"], out, phases=phases)
    st = []
    for g in range(NT):
        st.append(p.op("sync", lambda s, g=g: s.dma_start(out=q_out[128 * g:128 * (g + 1), :], in_=out["q"][g]), waits=toks, dma=True))
        st.append(p.op("sync", lambda s, g=g: s.dma_start(out=k_out[128 * g:128 * (g + 1), :], in_=out["k"][g]), waits=toks, dma=True))
    st.append(p.op("sync", lambda s: s.dma_start(out=v_out, in_=out["vtm"]), waits=toks, dma=True))
    st.append(p.op("sync", lambda s: s.dma_start(out=dq_out, in_=out["diagq"]), waits=toks, dma=True))
    st.append(p.op("sync", lambda s: s.dma_start(out=dk_out, in_=out["diagk"]), waits=toks, dma=True))
    st.append(p.op("sync", lambda s: s.dma_start(out=sq_out, in_=out["sq"]), waits=toks, dma=True))
    st.append(p.op("sync", lambda s: s.dma_start(out=sk_out, in_=out["skcol"]), waits=toks, dma=True))
    p.op("gpsimd", lambda g: g.engine_nop(), waits=st)
    p.finalize()
    return nc


def build_stage_a2():
    nc = bass.Bass("TRN2", target_bir_lowering=False)
    k_in = nc.dram_tensor("k_in", [D, T], BF16, kind="ExternalInput").ap()
    v_in = nc.dram_tensor("v_in", [128, 16 * 1040], BF16, kind="ExternalInput").ap()
    dk_in = nc.dram_tensor("dk_in", [128, 256], F32, kind="ExternalInput").ap()
    skall = nc.dram_tensor("skall", [16, 8], F32, kind="ExternalInput").ap()
    kv_out = nc.dram_tensor("kv_out", [128, 2080], F32, kind="ExternalOutput").ap()

    bufs = {}
    p = Prog(nc)
    make_psum_pool(p, bufs)
    setup_common(p, nc, bufs)
    _perf_consts(p, nc, bufs)

    k_t = [p.sbuf(f"kfm{g}", [128, T], BF16)[:, :] for g in range(NT)]
    vtm = p.sbuf("vtm", [128, 16 * 1040], BF16)[:, :]
    dk = p.sbuf("diagk", [128, 256], F32)[:, :]
    skb = p.sbuf("skall", [16, 8], F32)
    skcol = p.sbuf("skcol", [16, 1], F32)
    skrow = p.sbuf("skrow", [1, 16], F32)
    skdram = nc.dram_tensor("skd", [1, 16], F32).ap()
    lds = [p.op("sync", lambda s, g=g: s.dma_start(out=k_t[g], in_=k_in[128 * g:128 * (g + 1), :]), dma=True)
           for g in range(NT)]
    lds.append(p.op("sync", lambda s: s.dma_start(out=vtm, in_=v_in), dma=True))
    lds.append(p.op("sync", lambda s: s.dma_start(out=dk, in_=dk_in), dma=True))
    l1 = p.op("sync", lambda s: s.dma_start(out=skb[:, :], in_=skall), dma=True)
    r1 = p.op("vector", lambda v: v.reduce_max(skcol[:, :], skb[:, :], axis=AX.X), waits=[l1])
    r1 = p.op("vector", lambda v: v.tensor_scalar_mul(skcol[:, :], skcol[:, :], 0.125), waits=[r1])
    d1 = p.op("sync", lambda s: s.dma_start(out=skdram.rearrange("one h -> h one"), in_=skcol[:, :]), waits=[r1], dma=True)
    d2 = p.op("sync", lambda s: s.dma_start(out=skrow[:, :], in_=skdram), waits=[d1], dma=True)
    bufs["stabk_row_sb"] = skrow[:, :]
    bufs["stabk_ready"] = [d2]

    kvloc = p.sbuf("kvloc", [128, 2080], F32)
    toks = emit_perf_part2(p, bufs, lds, {"k": k_t, "vtm": vtm, "diagk": dk}, {"kvloc": kvloc[:, :]})
    st = [p.op("sync", lambda s: s.dma_start(out=kv_out, in_=kvloc[:, :]), waits=toks, dma=True)]
    p.op("gpsimd", lambda g: g.engine_nop(), waits=st)
    p.finalize()
    return nc


def build_stage_a3():
    nc = bass.Bass("TRN2", target_bir_lowering=False)
    x_in = nc.dram_tensor("x_in", [D, T], F32, kind="ExternalInput").ap()
    q_in = nc.dram_tensor("q_in", [D, T], BF16, kind="ExternalInput").ap()
    dq_in = nc.dram_tensor("dq_in", [128, 256], F32, kind="ExternalInput").ap()
    sq_in = nc.dram_tensor("sq_in", [128, 256], F32, kind="ExternalInput").ap()
    kvall = nc.dram_tensor("kvall", [NCORES * 128, 2080], F32, kind="ExternalInput").ap()
    wout = nc.dram_tensor("wout", [D, D], F32, kind="ExternalInput").ap()
    x_out = nc.dram_tensor("x_out", [D, T], F32, kind="ExternalOutput").ap()

    bufs = {"wout_dram": wout}
    p = Prog(nc)
    make_psum_pool(p, bufs)
    setup_common(p, nc, bufs)
    _perf_consts(p, nc, bufs)

    xs = p.sbuf("x_res", [128, NT * T], F32)
    x_tiles = [xs[:, g * T:(g + 1) * T] for g in range(NT)]
    xloads = [p.op("sync", lambda s, g=g: s.dma_start(out=x_tiles[g], in_=x_in[128 * g:128 * (g + 1), :]), dma=True)
              for g in range(NT)]
    q_t = [p.sbuf(f"qfm{g}", [128, T], BF16)[:, :] for g in range(NT)]
    dq = p.sbuf("diagq", [128, 256], F32)[:, :]
    sq = p.sbuf("sq", [128, 256], F32)[:, :]
    lds = [p.op("sync", lambda s, g=g: s.dma_start(out=q_t[g], in_=q_in[128 * g:128 * (g + 1), :]), dma=True)
           for g in range(NT)]
    lds.append(p.op("sync", lambda s: s.dma_start(out=dq, in_=dq_in), dma=True))
    lds.append(p.op("sync", lambda s: s.dma_start(out=sq, in_=sq_in), dma=True))
    kvb_f = p.sbuf("kvbf", [128, 2080], F32)
    kvstg = p.sbuf("kvstg", [128, 2080], F32)
    kvb = p.sbuf("kvb", [128, 2080], BF16)
    t0 = p.op("sync", lambda s: s.dma_start(out=kvb_f[:, :], in_=kvall[0:128, :]), dma=True)
    acc = [t0]
    for c in range(1, NCORES):
        lc = p.op("sync", lambda s, c=c: s.dma_start(out=kvstg[:, :], in_=kvall[128 * c:128 * (c + 1), :]),
                  waits=acc[-1:], dma=True)
        ac = p.op("vector", lambda v: v.tensor_add(kvb_f[:, :], kvb_f[:, :], kvstg[:, :]), waits=[lc] + acc[-1:])
        acc.append(ac)
    cb = p.op("vector", lambda v: v.tensor_copy(kvb[:, :], kvb_f[:, :]), waits=acc[-1:])

    qin = {"q": q_t, "diagq": dq, "sq": sq, "kvb": kvb[:, :], "kvb_ready": [cb]}
    toks = emit_perf_part3(p, bufs, x_tiles, xloads + lds + bufs["consts_ready"], qin)
    st = [p.op("sync", lambda s, g=g: s.dma_start(out=x_out[128 * g:128 * (g + 1), :], in_=x_tiles[g]),
               waits=toks, dma=True) for g in range(NT)]
    p.op("gpsimd", lambda g: g.engine_nop(), waits=st)
    p.finalize()
    return nc


def perf_host_consts():
    ident = np.eye(128, dtype=np.float32)
    seld = np.zeros((128, 128), np.float32)
    for g in range(8):
        for r in range(128):
            h = 2 * g + (r >= 64)
            seld[r, 16 * g + h] = -0.5
    return ident, seld



def make_psum_pool_wide(p, bufs):
    bufs["ps_wide"] = p.psum("ps_wide", [128, 1024])[:, :]
    bufs["ps_wide2"] = p.psum("ps_wide2", [128, 1024])[:, :]
    bufs["ps_mm0"] = p.psum("ps_mm0", [128, 512])[:, :]
    bufs["ps_x0"] = p.psum("ps_x0", [128, 512])[:, :]


def emit_l1f_qkv(p, bufs, xin_tiles, dep, out, tag="dq"):
    wq = p.sbuf(f"{tag}_wq", [128, NT * 1536], BF16)
    q2c = p.sbuf(f"{tag}_q2", [128, 512], F32)
    qb = [p.sbuf(f"{tag}_qb{g}", [128, T], BF16) for g in range(4)]
    kb = [p.sbuf(f"{tag}_kb{g}", [128, T], BF16) for g in range(4)]
    rstd2 = p.sbuf(f"{tag}_rs", [128, 256], F32)
    rrow = p.sbuf(f"{tag}_rrow", [1, 1024], F32)
    qn = p.sbuf(f"{tag}_qn", [128, 512], F32)
    qnb = p.sbuf(f"{tag}_qnb", [128, 512], BF16)
    t1b = p.sbuf(f"{tag}_t1", [128, 512], F32)
    rdram = p.nc.dram_tensor(p.name(f"{tag}_rd"), [2, 128, 128], F32).ap()
    ps0 = bufs["ps_mm0"]
    psN = bufs["ps_mm1"]
    psB = bufs["ps_x0"]
    selh = bufs["selh2_sb"]
    permT = bufs["permT_sb"]
    wqr = bufs["l1_wqkv_dram"].rearrange("(kt k) o -> k kt o", k=128)

    lwq = p.op("gpsimd", lambda g: g.dma_start(
        out=wq[:, :].rearrange("k (kt o) -> k kt o", o=1536), in_=wqr[:, :, :]),
        waits=bufs.pop("wq_free", []), dma=True)

    toks = []
    v_toks = []
    prev_read = []
    nrm_done = []
    tzr = p.op("vector", lambda v: v.memset(rstd2[:, :], 0.0))
    for part in range(2):
        dst = qb if part == 0 else kb
        for cc in range(4):
            csl = slice(512 * cc, 512 * (cc + 1))
            grp = [tzr]
            for ot in range(4):
                def mm2(t, part=part, ot=ot, cc=cc):
                    for kt in range(NT):
                        base = 1536 * kt + 512 * part + 128 * ot
                        off = 1 if kt < 4 else 0
                        rhs = xin_tiles[kt][:, 512 * cc + off:512 * cc + off + 512]
                        last = t.matmul(ps0, wq[:, base:base + 128], rhs,
                                        start=(kt == 0), stop=(kt == NT - 1))
                    return last
                mt = p.op("tensor", mm2, waits=[lwq] + list(dep) + prev_read)
                ev = p.op("vector", lambda v, ot=ot, csl=csl, dst=dst: v.tensor_copy(dst[ot][:, csl], ps0),
                          waits=[mt] + toks[-1:])
                sq_ = p.op("scalar", lambda s, ot=ot, csl=csl, dst=dst: s.activation(
                    q2c[:, :], dst[ot][:, csl], AF.Square), waits=[ev] + grp[-1:])
                def mmn(t, part=part, cc=cc, ot=ot):
                    for sc in range(4):
                        co = 128 * part + 32 * cc + 8 * sc
                        last = t.matmul(psN[:, co:co + 8],
                                        q2c[:, 128 * sc:128 * (sc + 1)], selh[:, 8 * ot:8 * (ot + 1)],
                                        start=True, stop=True)
                    return last
                dt_ = p.op("tensor", mmn, waits=[sq_] + bufs["selh_ready"] + grp[-1:])
                co0 = 128 * part + 32 * cc
                ac_ = p.op("vector", lambda v, co0=co0: v.tensor_add(
                    rstd2[:, co0:co0 + 32], rstd2[:, co0:co0 + 32], psN[:, co0:co0 + 32]),
                    waits=[dt_, tzr])
                grp.append(ac_)
                prev_read = [sq_, ev]
                toks.append(ev)
            nrm_done += grp
    r1 = nrm_done[-1]
    r2 = p.op("scalar", lambda s: s.activation(rstd2[:, :], rstd2[:, :], AF.Sqrt), waits=nrm_done)
    r3 = p.op("vector", lambda v: v.tensor_scalar_max(rstd2[:, :], rstd2[:, :], 1e-12), waits=[r2])
    r4 = p.op("vector", lambda v: v.reciprocal(rstd2[:, :], rstd2[:, :]), waits=[r3])
    tr1 = p.op("tensor", lambda t: t.transpose(psB[:, 0:128], rstd2[:, 0:128],
                                               bufs["ident_sb"]), waits=[r4] + bufs["ident_ready"])
    te1 = p.op("vector", lambda v: v.tensor_copy(qn[:, 0:128], psB[:, 0:128]), waits=[tr1])
    td1 = p.op("sync", lambda s: s.dma_start(out=rdram[0], in_=qn[:, 0:128]), waits=[te1], dma=True)
    tr2 = p.op("tensor", lambda t: t.transpose(psB[:, 0:128], rstd2[:, 128:256],
                                               bufs["ident_sb"]), waits=[te1])
    te2 = p.op("vector", lambda v: v.tensor_copy(qn[:, 0:128], psB[:, 0:128]), waits=[tr2, td1])
    td2 = p.op("sync", lambda s: s.dma_start(out=rdram[1], in_=qn[:, 0:128]), waits=[te2], dma=True)
    rows = [rdram[i].rearrange("(tc h) u -> h tc u", h=8) for i in range(2)]

    for tc in range(16):
        def mmv(t, tc=tc):
            for kt in range(NT):
                base = 1536 * kt + 1024
                off = 1 if kt < 4 else 0
                lhsT = xin_tiles[kt][:, 128 * tc + off:128 * tc + off + 128]
                last = t.matmul(ps0, lhsT, wq[:, base:base + 512],
                                start=(kt == 0), stop=(kt == NT - 1))
            return last
        mt = p.op("tensor", mmv, waits=[lwq] + list(dep) + prev_read + [te2])
        ev = p.op("vector", lambda v, tc=tc: v.tensor_copy(out["vtm"][:, 512 * tc:512 * (tc + 1)], ps0),
                  waits=[mt] + v_toks[-1:])
        prev_read = [ev]
        v_toks.append(ev)
    bufs["wq_free"] = [v_toks[-1]]

    out["qrot"] = [t[:, :] for t in qb]
    out["krot"] = [t[:, :] for t in kb]
    csbuf = p.sbuf(f"{tag}_cs", [128, 512], F32)
    ssbuf = p.sbuf(f"{tag}_ss", [128, 512], F32)
    rope_toks = []
    ld_prev = []
    for part in range(2):
        srcb = qb if part == 0 else kb
        CSd = bufs["cssq_dram"] if part == 0 else bufs["csik_dram"]
        SSd = bufs["sssq_dram"] if part == 0 else bufs["ssik_dram"]
        scol = bufs["qs2_col"] if part == 0 else bufs["ks2_col"]
        tdx = td1 if part == 0 else td2
        for cc in range(4):
            csl = slice(512 * cc, 512 * (cc + 1))
            lcs = p.op("sync", lambda s, csl=csl, CSd=CSd: s.dma_start(out=csbuf[:, :], in_=CSd[:, csl]),
                       waits=rope_toks[-1:], dma=True)
            lss = p.op("sync", lambda s, csl=csl, SSd=SSd: s.dma_start(out=ssbuf[:, :], in_=SSd[:, csl]),
                       waits=rope_toks[-1:], dma=True)
            for g in range(4):
                l1 = p.op("sync", lambda s, g=g, cc=cc, part=part: s.dma_start(
                    out=rrow[0:1, 0:512].rearrange("a (tc u) -> a tc u", u=128),
                    in_=rows[part][2 * g:2 * g + 1, 4 * cc:4 * (cc + 1), :]),
                    waits=[tdx] + ld_prev, dma=True)
                l2 = p.op("sync", lambda s, g=g, cc=cc, part=part: s.dma_start(
                    out=rrow[0:1, 512:1024].rearrange("a (tc u) -> a tc u", u=128),
                    in_=rows[part][2 * g + 1:2 * g + 2, 4 * cc:4 * (cc + 1), :]),
                    waits=[tdx], dma=True)
                b1 = p.op("tensor", lambda t: t.matmul(psB[0:64, :], bufs["ones_row_f"][0:1, 0:64],
                                                       rrow[0:1, 0:512], start=True, stop=True),
                          waits=[l1, l2] + rope_toks[-1:])
                b2 = p.op("tensor", lambda t: t.matmul(psB[64:128, :], bufs["ones_row_f"][0:1, 0:64],
                                                       rrow[0:1, 512:1024], start=True, stop=True,
                                                       tile_position=(0, 64)), waits=[l1, l2, b1])
                qkmul = 8.0 if part == 0 else 1.0
                s1 = p.op("vector", lambda v, g=g, csl=csl, srcb=srcb, scol=scol, qkmul=qkmul: v.tensor_scalar(
                    qn[:, :], srcb[g][:, csl], scol, qkmul, ALU.mult, ALU.mult),
                    waits=toks + rope_toks[-1:] + bufs["l1cols_ready"] + [td1, td2])
                s2 = p.op("vector", lambda v: v.tensor_copy(qnb[:, :], qn[:, :]), waits=[s1])
                pm = p.op("tensor", lambda t: t.matmul(psN[:, 0:512], permT[:, :], qnb[:, :],
                                                       start=True, stop=True),
                          waits=[s2, b2] + bufs["perm_ready"] + [r2])
                s3 = p.op("vector", lambda v: v.tensor_mul(t1b[:, :], qn[:, :], csbuf[:, :]),
                          waits=[s1, lcs])
                s4 = p.op("vector", lambda v: v.tensor_mul(qn[:, :], psN[:, 0:512], ssbuf[:, :]),
                          waits=[pm, s3, lss])
                s5 = p.op("vector", lambda v: v.tensor_add(t1b[:, :], t1b[:, :], qn[:, :]), waits=[s4])
                s6 = p.op("vector", lambda v, g=g, csl=csl, srcb=srcb: v.tensor_mul(
                    srcb[g][:, csl], t1b[:, :], psB[:, :]), waits=[s5, b2])
                ld_prev = [s6]
                rope_toks.append(s6)
    return toks + v_toks + rope_toks


def emit_l1f_attn(p, bufs, xmid_tiles, dep, ain, tag="da"):
    qrot, krotF, vtmF = ain["qrot"], ain["krotF"], ain["vtmF"]
    M0, M4 = bufs["m0_sb"], bufs["m4_sb"]
    brow = bufs["biasrow_sb"]
    edgeT = bufs["edgeT_sb"]
    wo_sb = p.sbuf(f"{tag}_wo", [128, 4 * 1024], BF16)
    mst = p.sbuf(f"{tag}_mst", [128, 16], F32)
    dsum = p.sbuf(f"{tag}_dsum", [128, 16], F32)
    nmcol = p.sbuf(f"{tag}_nm", [128, 1], F32)
    scr = p.sbuf(f"{tag}_scr", [128, 640], BF16)
    expb = p.sbuf(f"{tag}_expb", [128, 5 * 640], BF16)
    att = [p.sbuf(f"{tag}_att{g}", [128, T], BF16) for g in range(4)]
    mldt = p.sbuf(f"{tag}_mldt", [128, 16], F32)
    mldtc = p.sbuf(f"{tag}_mldtc", [128, 16], BF16)
    mldtf = p.sbuf(f"{tag}_mldtf", [128, 16], F32)
    mrow_c = p.sbuf(f"{tag}_mrc", [1, T], BF16)
    mrow_f = p.sbuf(f"{tag}_mrf", [1, T], BF16)
    tposbuf = p.sbuf(f"{tag}_tp", [16, 128], F32)
    mdram = p.nc.dram_tensor(p.name(f"{tag}_md"), [2, 16, 128], F32).ap()
    psW = bufs["ps_wide"]
    psW2 = bufs["ps_wide2"]
    psPV = bufs["ps_x0"]
    psT = bufs["ps_mm0"]
    wor = bufs["l1_wo_dram"].rearrange("(kt k) o -> k kt o", k=128)

    lwo = p.op("gpsimd", lambda g: g.dma_start(
        out=wo_sb[:, :].rearrange("k (kt o) -> k kt o", o=1024), in_=wor[:, :, :]),
        waits=bufs.pop("wo_free", []), dma=True)

    att_toks = []
    h_prev = []
    for h in range(8):
        g, off = h // 2, 64 * (h % 2)
        p1 = []
        for qt in range(16):
            def mm1(t, g=g, off=off, qt=qt):
                last = t.matmul(psW[:, 0:512], qrot[g][off:off + 64, 128 * qt:128 * (qt + 1)],
                                krotF[g][off:off + 64, 128 * qt:128 * qt + 512],
                                start=True, stop=(qt >= 4))
                if qt < 4:
                    last = t.matmul(psW[:, 0:512], bufs["ones_row_bf"][0:1, :],
                                    brow[0:1, 640 * qt:640 * qt + 512], start=False, stop=True)
                last = t.matmul(psW[:, 512:640], qrot[g][off:off + 64, 128 * qt:128 * (qt + 1)],
                                krotF[g][off:off + 64, 128 * qt + 512:128 * qt + 640],
                                start=True, stop=(qt >= 4))
                if qt < 4:
                    last = t.matmul(psW[:, 512:640], bufs["ones_row_bf"][0:1, :],
                                    brow[0:1, 640 * qt + 512:640 * qt + 640], start=False, stop=True)
                return last
            mt = p.op("tensor", mm1, waits=list(dep) + p1[-1:] + h_prev + bufs["mask_ready"] + bufs["consts_ready"])
            mk1 = p.op("vector", lambda v: v.tensor_add(psW[:, 0:128], psW[:, 0:128], M0[:, :]), waits=[mt])
            mk2 = p.op("vector", lambda v: v.tensor_add(psW[:, 512:640], psW[:, 512:640], M4[:, :]),
                       waits=[mt, mk1])
            rd = p.op("vector", lambda v, qt=qt: v.reduce_max(mst[:, qt:qt + 1], psW[:, 0:640], axis=AX.X),
                      waits=[mk2] + h_prev)
            ng = p.op("vector", lambda v, qt=qt: v.tensor_scalar_mul(nmcol[:, :], mst[:, qt:qt + 1], -1.0),
                      waits=[rd])
            ex = p.op("scalar", lambda s, qt=qt: s.activation(scr[:, :], psW[:, 0:640], AF.Exp,
                                                              bias=nmcol[:, :],
                                                              accum_out=dsum[:, qt:qt + 1]), waits=[ng])
            p1.append(ex)
        if h == 0 and "dbg_attn" in bufs:
            dd = bufs["dbg_attn"]
            dt1 = p.op("gpsimd", lambda s: s.dma_start(out=dd["mst"], in_=mst[:, :]), waits=p1, dma=True)
            dt2 = p.op("gpsimd", lambda s: s.dma_start(out=dd["dsum"], in_=dsum[:, :]), waits=p1, dma=True)
            h_prev = h_prev + [dt1, dt2]
        c1 = p.op("scalar", lambda s: s.activation(mldt[:, :], dsum[:, :], AF.Ln), waits=p1)
        c2 = p.op("vector", lambda v: v.tensor_add(mldt[:, :], mldt[:, :], mst[:, :]), waits=[c1])
        c3 = p.op("vector", lambda v: v.tensor_scalar_mul(mldt[:, :], mldt[:, :], -1.0), waits=[c2])
        c4 = p.op("vector", lambda v: v.tensor_copy(mldtc[:, :], mldt[:, :]), waits=[c3])
        c5 = p.op("vector", lambda v: v.tensor_copy(mldtf[:, :], mldtc[:, :]), waits=[c4])
        c6 = p.op("vector", lambda v: v.tensor_sub(mldtf[:, :], mldt[:, :], mldtf[:, :]), waits=[c5])
        t1 = p.op("tensor", lambda t: t.transpose(psT[0:16, 0:128], mldtc[:, :].bitcast(BF16) if False
                                                  else mldt[:, :], bufs["ident_sb"]),
                  waits=[c6] + h_prev + bufs["ident_ready"])
        e1 = p.op("vector", lambda v: v.tensor_copy(tposbuf[:, :], psT[0:16, 0:128]), waits=[t1])
        d1 = p.op("sync", lambda s: s.dma_start(out=mdram[0], in_=tposbuf[:, :]), waits=[e1], dma=True)
        t2 = p.op("tensor", lambda t: t.transpose(psT[0:16, 0:128], mldtf[:, :], bufs["ident_sb"]),
                  waits=[e1, c6])
        e2 = p.op("vector", lambda v: v.tensor_copy(tposbuf[:, :], psT[0:16, 0:128]), waits=[t2, d1])
        d2 = p.op("sync", lambda s: s.dma_start(out=mdram[1], in_=tposbuf[:, :]), waits=[e2], dma=True)
        lc = p.op("gpsimd", lambda g_: g_.dma_start(out=mrow_c[:, :],
                                                    in_=mdram[0].rearrange("q u -> (q u)").unsqueeze(0)),
                  waits=[d1], dma=True)
        lf = p.op("gpsimd", lambda g_: g_.dma_start(out=mrow_f[:, :],
                                                    in_=mdram[1].rearrange("q u -> (q u)").unsqueeze(0)),
                  waits=[d2], dma=True)
        p2 = {}
        pv_toks = []
        for j in range(20):
            qlo = max(0, 128 * j - 512)
            qhi = min(T, 128 * j + 128)
            w = qhi - qlo
            ring = expb[:, 640 * (j % 5):640 * (j % 5) + w]
            def mm2(t, g=g, off=off, j=j, qlo=qlo, qhi=qhi, w=w):
                first = True
                for c0 in range(0, w, 512):
                    cw = min(512, w - c0)
                    t.matmul(psW2[:, c0:c0 + cw], krotF[g][off:off + 64, 128 * j:128 * (j + 1)],
                             qrot[g][off:off + 64, qlo + c0:qlo + c0 + cw], start=True, stop=False)
                    t.matmul(psW2[:, c0:c0 + cw], bufs["ones_row_bf"][0:1, :],
                             mrow_c[0:1, qlo + c0:qlo + c0 + cw], start=False, stop=False)
                    last = t.matmul(psW2[:, c0:c0 + cw], bufs["ones_row_bf"][0:1, :],
                                    mrow_f[0:1, qlo + c0:qlo + c0 + cw], start=False, stop=True)
                return last
            mt = p.op("tensor", mm2, waits=[lc, lf] + p2.get(j - 1, []) + pv_toks[-1:] + bufs["consts_ready"])
            wl = []
            qb0 = j - 4
            if 0 <= qb0 < 16:
                co = 128 * qb0 - qlo
                wl.append(p.op("vector", lambda v, co=co: v.tensor_add(
                    psW2[:, co:co + 128], psW2[:, co:co + 128], M0[:, :]), waits=[mt]))
            if j < 16:
                co = 128 * j - qlo
                wl.append(p.op("vector", lambda v, co=co: v.tensor_add(
                    psW2[:, co:co + 128], psW2[:, co:co + 128], M4[:, :]), waits=[mt] + wl[-1:]))
            if j < 4:
                wl.append(p.op("vector", lambda v, j=j: v.tensor_scalar_add(
                    psW2[:, 0:w], psW2[:, 0:w], edgeT[:, j:j + 1]), waits=[mt] + wl[-1:]))
            ex = p.op("scalar", lambda s, ring=ring, w=w: s.activation(ring, psW2[:, 0:w], AF.Exp),
                      waits=[mt] + wl)
            p2[j] = [ex]
            qt = j - 4
            if 0 <= qt < 16:
                def mpv(t, qt=qt, g=g, off=off, h=h):
                    for i in range(5):
                        jj = qt + i
                        qlo_j = max(0, 128 * jj - 512)
                        co = 128 * qt - qlo_j
                        rr = expb[:, 640 * (jj % 5) + co:640 * (jj % 5) + co + 128]
                        last = t.matmul(psPV[0:64, 0:128], vtmF[:, 512 * jj + 64 * h:512 * jj + 64 * h + 64],
                                        rr, start=(i == 0), stop=(i == 4))
                    return last
                deps = [p2[qt + i][0] for i in range(5)]
                mt2 = p.op("tensor", mpv, waits=deps + pv_toks[-1:])
                ev = p.op("vector", lambda v, qt=qt, g=g, off=off: v.tensor_copy(
                    att[g][off:off + 64, 128 * qt:128 * (qt + 1)], psPV[0:64, 0:128]), waits=[mt2])
                pv_toks.append(ev)
        h_prev = pv_toks[-1:]
        att_toks += pv_toks
    if "dbg_attn" in bufs:
        dd = bufs["dbg_attn"]
        for g in range(4):
            p.op("gpsimd", lambda s, g=g: s.dma_start(out=dd["att"][128 * g:128 * (g + 1), :], in_=att[g][:, :]),
                 waits=att_toks, dma=True)
    res_toks = []
    for ot in range(NT):
        for cc in range(4):
            csl = slice(512 * cc, 512 * (cc + 1))
            def mmo(t, ot=ot, csl=csl):
                for kt in range(4):
                    last = t.matmul(psT, wo_sb[:, 1024 * kt + 128 * ot:1024 * kt + 128 * (ot + 1)],
                                    att[kt][:, csl], start=(kt == 0), stop=(kt == 3))
                return last
            mt = p.op("tensor", mmo, waits=[lwo] + att_toks + res_toks[-1:] + h_prev)
            rv_ = p.op("vector", lambda v, ot=ot, csl=csl: v.tensor_add(
                xmid_tiles[ot][:, csl.start + 1:csl.stop + 1],
                xmid_tiles[ot][:, csl.start + 1:csl.stop + 1], psT), waits=[mt])
            res_toks.append(rv_)
    bufs["wo_free"] = res_toks[-1:]
    return res_toks


def emit_l1f_ffn(p, bufs, xmid_tiles, dep, final_add=None, tag="df"):
    lg, lb = bufs["l1_lng_col"], bufs["l1_lnb_col"]
    b1c, b2c = bufs["l1_b1_col"], bufs["l1_b2_col"]
    lnb = [p.sbuf(f"{tag}_ln{g}", [128, 512], BF16) for g in range(NT)]
    w1s = [p.sbuf(f"{tag}_w1_{i}", [128, NT * 128], BF16) for i in range(4)]
    w2s = [p.sbuf(f"{tag}_w2_{i}", [128, 22 * 128], BF16) for i in range(2)]
    abuf = p.sbuf(f"{tag}_a", [128, 512], F32)
    gbuf = p.sbuf(f"{tag}_g", [128, 512], F32)
    prodb = p.sbuf(f"{tag}_pr", [128, 22 * 512], BF16)
    xo = p.sbuf(f"{tag}_xo", [128, NT * 512], F32) if final_add else None
    w1r = bufs["l1_w1_dram"].rearrange("(kt k) o -> k kt o", k=128)
    w2r = bufs["l1_w2_dram"].rearrange("(kt k) o -> k kt o", k=128)
    ps = bufs["ps_mm0"]
    ps2 = bufs["ps_mm1"]
    mmh = {}
    x_toks = list(dep)
    xsh = [xmid_tiles[g][:, (1 if g < 4 else 0):(2049 if g < 4 else 2048)] for g in range(NT)]
    for tt in range(4):
        tsl = slice(512 * tt, 512 * (tt + 1))
        ln_toks = emit_ln_ttile(p, bufs, xsh, tsl, lnb, slice(0, 512), x_toks[-NT:] or x_toks,
                                gain_col=[lg[:, g:g + 1] for g in range(NT)],
                                bias_col=[lb[:, g:g + 1] for g in range(NT)],
                                tag=f"{tag}_ln{tt}")
        prod_toks = []
        for ot in range(22):
            wa, wg = w1s[ot % 2], w1s[2 + ot % 2]
            la = p.op("gpsimd", lambda s, ot=ot, wa=wa: s.dma_start(
                out=wa[:, :].rearrange("k (kt o) -> k kt o", o=128),
                in_=w1r[:, :, 128 * ot:128 * (ot + 1)]), waits=[mmh.get(("a", ot % 2))], dma=True)
            lg_ = p.op("gpsimd", lambda s, ot=ot, wg=wg: s.dma_start(
                out=wg[:, :].rearrange("k (kt o) -> k kt o", o=128),
                in_=w1r[:, :, 2816 + 128 * ot:2816 + 128 * (ot + 1)]), waits=[mmh.get(("g", ot % 2))], dma=True)
            def mma(t, wa=wa):
                for kt in range(NT):
                    last = t.matmul(ps, wa[:, 128 * kt:128 * (kt + 1)], lnb[kt][:, :],
                                    start=(kt == 0), stop=(kt == NT - 1))
                return last
            def mmg(t, wg=wg):
                for kt in range(NT):
                    last = t.matmul(ps2, wg[:, 128 * kt:128 * (kt + 1)], lnb[kt][:, :],
                                    start=(kt == 0), stop=(kt == NT - 1))
                return last
            ma = p.op("tensor", mma, waits=[la] + ln_toks + prod_toks[-1:])
            mmh[("a", ot % 2)] = ma
            ea = p.op("vector", lambda v, ot=ot: v.tensor_scalar_add(abuf[:, :], ps, b1c[:, ot:ot + 1]),
                      waits=[ma])
            mg = p.op("tensor", mmg, waits=[lg_] + ln_toks + [ea])
            mmh[("g", ot % 2)] = mg
            eg = p.op("vector", lambda v, ot=ot: v.tensor_scalar_add(gbuf[:, :], ps2, b1c[:, 22 + ot:23 + ot]),
                      waits=[mg])
            gg = p.op("scalar", lambda s: s.activation(gbuf[:, :], gbuf[:, :], AF.Gelu), waits=[eg])
            pr = p.op("vector", lambda v, ot=ot: v.tensor_mul(prodb[:, 512 * ot:512 * (ot + 1)],
                                                              abuf[:, :], gbuf[:, :]), waits=[gg, ea])
            prod_toks.append(pr)
        res_toks = []
        if final_add:
            xl = [p.op("sync", lambda s, g=g, tsl=tsl: s.dma_start(
                out=xo[:, 512 * g:512 * (g + 1)],
                in_=final_add["x_outer_dram"][128 * g:128 * (g + 1), tsl]),
                waits=x_toks[-NT:], dma=True) for g in range(NT)]
        for ot in range(NT):
            wb = w2s[ot % 2]
            lt = p.op("gpsimd", lambda s, ot=ot, wb=wb: s.dma_start(
                out=wb[:, :].rearrange("k (kt o) -> k kt o", o=128),
                in_=w2r[:, :, 128 * ot:128 * (ot + 1)]), waits=[mmh.get(("w2", ot % 2))], dma=True)
            def mm2(t, wb=wb):
                for kt in range(22):
                    last = t.matmul(ps, wb[:, 128 * kt:128 * (kt + 1)], prodb[:, 512 * kt:512 * (kt + 1)],
                                    start=(kt == 0), stop=(kt == 21))
                return last
            mt = p.op("tensor", mm2, waits=[lt] + prod_toks + res_toks[-1:])
            mmh[("w2", ot % 2)] = mt
            rv_ = p.op("vector", lambda v, ot=ot, tsl=tsl: v.scalar_tensor_tensor(
                xmid_tiles[ot][:, tsl.start + 1:tsl.stop + 1], ps, b2c[:, ot:ot + 1],
                xmid_tiles[ot][:, tsl.start + 1:tsl.stop + 1], ALU.add, ALU.add), waits=[mt])
            if final_add:
                rv_ = p.op("vector", lambda v, ot=ot, tsl=tsl: v.tensor_add(
                    xo[:, 512 * ot:512 * (ot + 1)], xo[:, 512 * ot:512 * (ot + 1)],
                    xmid_tiles[ot][:, tsl.start + 1:tsl.stop + 1]), waits=[rv_, xl[ot]])
                rv_ = p.op("sync", lambda s, ot=ot, tsl=tsl: s.dma_start(
                    out=final_add["x_out_dram"][128 * ot:128 * (ot + 1), tsl],
                    in_=xo[:, 512 * ot:512 * (ot + 1)]), waits=[rv_], dma=True)
            res_toks.append(rv_)
        x_toks = res_toks
    return x_toks



def emit_token_ln_row(p, bufs, hrow, hdep, outcol_writer, tag):
    hscratch = p.sbuf(f"{tag}_hs", [1, 8], F32)
    hsq = p.sbuf(f"{tag}_hsq", [1, D], F32)
    hlnrow = p.sbuf(f"{tag}_hln", [1, D], BF16)
    hdram = p.nc.dram_tensor(p.name(f"{tag}_hd"), [1, D], BF16).ap()
    a1 = p.op("vector", lambda v: v.reduce_sum(hscratch[:, 0:1], hrow, axis=AX.X), waits=hdep)
    a2 = p.op("scalar", lambda s: s.activation(hsq[:, :], hrow, AF.Square), waits=hdep)
    a3 = p.op("vector", lambda v: v.reduce_sum(hscratch[:, 1:2], hsq[:, :], axis=AX.X), waits=[a2])
    a4 = p.op("vector", lambda v: v.tensor_scalar_mul(hscratch[:, 0:1], hscratch[:, 0:1], 1.0 / 1024.0), waits=[a1])
    a5 = p.op("vector", lambda v: v.tensor_scalar_mul(hscratch[:, 1:2], hscratch[:, 1:2], 1.0 / 1024.0), waits=[a3])
    a6 = p.op("vector", lambda v: v.tensor_mul(hscratch[:, 2:3], hscratch[:, 0:1], hscratch[:, 0:1]), waits=[a4])
    a7 = p.op("vector", lambda v: v.tensor_sub(hscratch[:, 1:2], hscratch[:, 1:2], hscratch[:, 2:3]), waits=[a5, a6])
    a8 = p.op("vector", lambda v: v.tensor_scalar_add(hscratch[:, 1:2], hscratch[:, 1:2], 1e-5), waits=[a7])
    a9 = p.op("scalar", lambda s: s.activation(hscratch[:, 1:2], hscratch[:, 1:2], AF.Ln), waits=[a8])
    a10 = p.op("scalar", lambda s: s.activation(hscratch[:, 1:2], hscratch[:, 1:2], AF.Exp, scale=-0.5), waits=[a9])
    a11 = p.op("vector", lambda v: v.tensor_scalar(hlnrow[:, :], hrow, hscratch[:, 0:1], hscratch[:, 1:2],
                                                   ALU.subtract, ALU.mult), waits=[a10] + hdep)
    a12 = p.op("sync", lambda s: s.dma_start(out=hdram, in_=hlnrow[:, :]), waits=[a11], dma=True)
    toks = [p.op("gpsimd", lambda s, g=g: outcol_writer(s, g, hdram), waits=[a12], dma=True)
            for g in range(NT)]
    return toks


def _l1_layer_consts(p, nc, bufs):
    selh = nc.dram_tensor("selh2", [128, 32], F32, kind="ExternalInput").ap()
    permT = nc.dram_tensor("permT", [128, 128], F32, kind="ExternalInput").ap()
    cssq = nc.dram_tensor("cssq", [128, T], F32, kind="ExternalInput").ap()
    sssq = nc.dram_tensor("sssq", [128, T], F32, kind="ExternalInput").ap()
    csik = nc.dram_tensor("csik", [128, T], F32, kind="ExternalInput").ap()
    ssik = nc.dram_tensor("ssik", [128, T], F32, kind="ExternalInput").ap()
    qs2 = nc.dram_tensor("qs2", [128, 1], F32, kind="ExternalInput").ap()
    ks2 = nc.dram_tensor("ks2", [128, 1], F32, kind="ExternalInput").ap()
    id_in = nc.dram_tensor("ident", [128, 128], F32, kind="ExternalInput").ap()
    s_selh = p.sbuf("selh", [128, 32], F32)
    s_perm = p.sbuf("permT", [128, 128], BF16)
    s_qs2 = p.sbuf("qs2c", [128, 1], F32)
    s_ks2 = p.sbuf("ks2c", [128, 1], F32)
    s_id = p.sbuf("identf", [128, 128], F32)
    t0 = p.op("sync", lambda s: s.dma_start(out=s_id[:, :], in_=id_in), dma=True)
    t1 = p.op("sync", lambda s: s.dma_start(out=s_selh[:, :], in_=selh), dma=True)
    t2 = p.op("gpsimd", lambda g: g.dma_start(out=s_perm[:, :], in_=permT), dma=True)
    t7 = p.op("sync", lambda s: s.dma_start(out=s_qs2[:, :], in_=qs2), dma=True)
    t8 = p.op("sync", lambda s: s.dma_start(out=s_ks2[:, :], in_=ks2), dma=True)
    bufs.update(selh2_sb=s_selh[:, :], permT_sb=s_perm[:, :],
                cssq_dram=cssq, sssq_dram=sssq, csik_dram=csik, ssik_dram=ssik,
                qs2_col=s_qs2[:, :], ks2_col=s_ks2[:, :])
    bufs["selh_ready"] = [t1]
    bufs["perm_ready"] = [t2]
    bufs["l1cols_ready"] = [t7, t8]
    bufs["ident_sb"] = s_id[:, :]
    bufs["ident_ready"] = [t0]


def build_stage_d_qkv(with_ln4):
    nc = bass.Bass("TRN2", target_bir_lowering=False)
    p = Prog(nc)
    bufs = {}
    make_psum_pool(p, bufs)
    make_ln_shared(p, bufs)
    setup_common(p, nc, bufs)
    _l1_layer_consts(p, nc, bufs)
    bufs["l1_wqkv_dram"] = nc.dram_tensor("wqkv", [D, 1536], F32, kind="ExternalInput").ap()
    q_out = nc.dram_tensor("q_out", [512, T], BF16, kind="ExternalOutput").ap()
    k_out = nc.dram_tensor("k_out", [512, T], BF16, kind="ExternalOutput").ap()
    v_out = nc.dram_tensor("v_out", [128, 16 * 512], BF16, kind="ExternalOutput").ap()

    xin = p.sbuf("xin", [128, NT * (T + 1)], BF16)
    xin_tiles = [xin[:, g * (T + 1):(g + 1) * (T + 1)] for g in range(NT)]
    if with_ln4:
        x_outer = nc.dram_tensor("x_outer", [D, T], F32, kind="ExternalInput").ap()
        halo_row = nc.dram_tensor("halo_row", [1, D], F32, kind="ExternalInput").ap()
        ln4_out = nc.dram_tensor("ln4_out", [D, T], F32, kind="ExternalOutput").ap()
        xo = p.sbuf("xo_s", [128, NT * 512], F32)
        xo_t = [xo[:, g * 512:(g + 1) * 512] for g in range(NT)]
        l4 = p.sbuf("ln4f", [128, NT * 512], F32)
        l4_t = [l4[:, g * 512:(g + 1) * 512] for g in range(NT)]
        dep = []
        for tt in range(4):
            xl = [p.op("sync", lambda s, g=g, tt=tt: s.dma_start(
                out=xo_t[g], in_=x_outer[128 * g:128 * (g + 1), 512 * tt:512 * (tt + 1)]),
                waits=dep[-3 * NT:], dma=True) for g in range(NT)]
            lt = emit_ln_ttile(p, bufs, xo_t, slice(0, 512), l4_t,
                               slice(0, 512), xl, tag=f"ln4_{tt}")
            cp = [p.op("vector", lambda v, g=g, tt=tt: v.tensor_copy(
                xin_tiles[g][:, 512 * tt + 1:512 * (tt + 1) + 1], l4_t[g]), waits=[lt[g]])
                for g in range(NT)]
            dm = [p.op("sync", lambda s, g=g, tt=tt: s.dma_start(
                out=ln4_out[128 * g:128 * (g + 1), 512 * tt:512 * (tt + 1)], in_=l4_t[g]),
                waits=[cp[g]], dma=True) for g in range(NT)]
            dep += cp + dm
        hr = p.sbuf("hrow", [1, D], F32)
        hl = p.op("sync", lambda s: s.dma_start(out=hr[:, :], in_=halo_row), dma=True)
        def wrcol(s, g, hdram):
            return s.dma_start(out=xin_tiles[g][:, 0:1], in_=hdram[0:1, 128 * g:128 * (g + 1)])
        dep0 = emit_token_ln_row(p, bufs, hr[:, :], [hl], wrcol, tag="h4")
        dep = dep + dep0
    else:
        xin_in = nc.dram_tensor("xin", [D, T + 1], F32, kind="ExternalInput").ap()
        dep = [p.op("gpsimd", lambda s, g=g: s.dma_start(out=xin_tiles[g], in_=xin_in[128 * g:128 * (g + 1), :]), dma=True)
               for g in range(NT)]

    out = {"vtm": p.sbuf("vtm", [128, 16 * 512], BF16)[:, :]}
    toks = emit_l1f_qkv(p, bufs, xin_tiles, dep, out)
    st = []
    for g in range(4):
        st.append(p.op("sync", lambda s, g=g: s.dma_start(out=q_out[128 * g:128 * (g + 1), :], in_=out["qrot"][g]), waits=toks, dma=True))
        st.append(p.op("sync", lambda s, g=g: s.dma_start(out=k_out[128 * g:128 * (g + 1), :], in_=out["krot"][g]), waits=toks, dma=True))
    st.append(p.op("sync", lambda s: s.dma_start(out=v_out, in_=out["vtm"]), waits=toks, dma=True))
    p.op("gpsimd", lambda g: g.engine_nop(), waits=st)
    p.finalize()
    return nc


def build_stage_d_attn():
    nc = bass.Bass("TRN2", target_bir_lowering=False)
    p = Prog(nc)
    bufs = {}
    make_psum_pool_wide(p, bufs)
    setup_common(p, nc, bufs)
    m0 = nc.dram_tensor("m0", [128, 128], F32, kind="ExternalInput").ap()
    m4 = nc.dram_tensor("m4", [128, 128], F32, kind="ExternalInput").ap()
    browi = nc.dram_tensor("biasrow", [1, 2560], F32, kind="ExternalInput").ap()
    edgeti = nc.dram_tensor("edgeT", [128, 4], F32, kind="ExternalInput").ap()
    identi = nc.dram_tensor("ident", [128, 128], F32, kind="ExternalInput").ap()
    s_id2 = p.sbuf("identf", [128, 128], F32)
    ti = p.op("sync", lambda s: s.dma_start(out=s_id2[:, :], in_=identi), dma=True)
    bufs["ident_sb"] = s_id2[:, :]
    bufs["ident_ready"] = [ti]
    bufs["l1_wo_dram"] = nc.dram_tensor("wo", [512, D], F32, kind="ExternalInput").ap()
    q_in = nc.dram_tensor("q_in", [512, T], BF16, kind="ExternalInput").ap()
    k_in = nc.dram_tensor("k_in", [512, T], BF16, kind="ExternalInput").ap()
    kh_in = nc.dram_tensor("kh_in", [512, 512], BF16, kind="ExternalInput").ap()
    v_in = nc.dram_tensor("v_in", [128, 16 * 512], BF16, kind="ExternalInput").ap()
    vh_in = nc.dram_tensor("vh_in", [128, 4 * 512], BF16, kind="ExternalInput").ap()
    xi_in = nc.dram_tensor("x_inner", [D, T], F32, kind="ExternalInput").ap()
    xm_out = nc.dram_tensor("x_mid", [D, T], F32, kind="ExternalOutput").ap()

    s_m0 = p.sbuf("m0", [128, 128], F32)
    s_m4 = p.sbuf("m4", [128, 128], F32)
    s_br = p.sbuf("brow", [1, 2560], BF16)
    s_et = p.sbuf("edgeT", [128, 4], F32)
    w1 = p.op("sync", lambda s: s.dma_start(out=s_m0[:, :], in_=m0), dma=True)
    w2 = p.op("sync", lambda s: s.dma_start(out=s_m4[:, :], in_=m4), dma=True)
    w3 = p.op("gpsimd", lambda g: g.dma_start(out=s_br[:, :], in_=browi), dma=True)
    w4 = p.op("sync", lambda s: s.dma_start(out=s_et[:, :], in_=edgeti), dma=True)
    bufs["m0_sb"], bufs["m4_sb"] = s_m0[:, :], s_m4[:, :]
    bufs["biasrow_sb"], bufs["edgeT_sb"] = s_br[:, :], s_et[:, :]
    bufs["mask_ready"] = [w1, w2, w3, w4]

    qrot = [p.sbuf(f"qr{g}", [128, T], BF16) for g in range(4)]
    krotF = [p.sbuf(f"krF{g}", [128, 512 + T], BF16) for g in range(4)]
    vtmF = p.sbuf("vtmF", [128, 20 * 512], BF16)
    xm = p.sbuf("xm", [128, NT * (T + 1)], F32)
    xm_tiles = [xm[:, g * (T + 1):(g + 1) * (T + 1)] for g in range(NT)]
    dep = []
    for g in range(4):
        dep.append(p.op("sync", lambda s, g=g: s.dma_start(out=qrot[g][:, :], in_=q_in[128 * g:128 * (g + 1), :]), dma=True))
        dep.append(p.op("sync", lambda s, g=g: s.dma_start(out=krotF[g][:, 0:512], in_=kh_in[128 * g:128 * (g + 1), :]), dma=True))
        dep.append(p.op("sync", lambda s, g=g: s.dma_start(out=krotF[g][:, 512:512 + T], in_=k_in[128 * g:128 * (g + 1), :]), dma=True))
    dep.append(p.op("sync", lambda s: s.dma_start(out=vtmF[:, 0:4 * 512], in_=vh_in), dma=True))
    dep.append(p.op("sync", lambda s: s.dma_start(out=vtmF[:, 4 * 512:], in_=v_in), dma=True))
    for g in range(NT):
        dep.append(p.op("sync", lambda s, g=g: s.dma_start(out=xm_tiles[g][:, 1:T + 1], in_=xi_in[128 * g:128 * (g + 1), :]), dma=True))

    ain = {"qrot": [t[:, :] for t in qrot], "krotF": [t[:, :] for t in krotF], "vtmF": vtmF[:, :]}
    toks = emit_l1f_attn(p, bufs, xm_tiles, dep, ain)
    st = [p.op("sync", lambda s, g=g: s.dma_start(out=xm_out[128 * g:128 * (g + 1), :], in_=xm_tiles[g][:, 1:T + 1]),
               waits=toks, dma=True) for g in range(NT)]
    p.op("gpsimd", lambda g: g.engine_nop(), waits=st)
    p.finalize()
    return nc


def build_stage_d_ffn(final):
    nc = bass.Bass("TRN2", target_bir_lowering=False)
    p = Prog(nc)
    bufs = {}
    make_psum_pool(p, bufs)
    make_ln_shared(p, bufs)
    setup_common(p, nc, bufs)
    bufs["l1_w1_dram"] = nc.dram_tensor("w1p", [D, 5632], F32, kind="ExternalInput").ap()
    bufs["l1_w2_dram"] = nc.dram_tensor("w2p", [2816, D], F32, kind="ExternalInput").ap()
    lncols = nc.dram_tensor("lncols", [128, 16], F32, kind="ExternalInput").ap()
    bcols = nc.dram_tensor("bcols", [128, 52], F32, kind="ExternalInput").ap()
    xm_in = nc.dram_tensor("x_mid", [D, T], F32, kind="ExternalInput").ap()
    halo_col = nc.dram_tensor("halo_col", [D, 1], F32, kind="ExternalInput").ap()
    s_lnc = p.sbuf("lncols", [128, 16], F32)
    s_bc = p.sbuf("bcols", [128, 52], F32)
    c1 = p.op("sync", lambda s: s.dma_start(out=s_lnc[:, :], in_=lncols), dma=True)
    c2 = p.op("sync", lambda s: s.dma_start(out=s_bc[:, :], in_=bcols), dma=True)
    bufs["l1_lng_col"] = s_lnc[:, 0:8]
    bufs["l1_lnb_col"] = s_lnc[:, 8:16]
    bufs["l1_b1_col"] = s_bc[:, 0:44]
    bufs["l1_b2_col"] = s_bc[:, 44:52]

    xm = p.sbuf("xm", [128, NT * (T + 1)], F32)
    xm_tiles = [xm[:, g * (T + 1):(g + 1) * (T + 1)] for g in range(NT)]
    dep = [c1, c2]
    for g in range(NT):
        dep.append(p.op("sync", lambda s, g=g: s.dma_start(out=xm_tiles[g][:, 1:T + 1], in_=xm_in[128 * g:128 * (g + 1), :]), dma=True))
        dep.append(p.op("sync", lambda s, g=g: s.dma_start(out=xm_tiles[g][:, 0:1], in_=halo_col[128 * g:128 * (g + 1), :]), dma=True))

    fa = None
    if final:
        x_outer = nc.dram_tensor("x_outer", [D, T], F32, kind="ExternalInput").ap()
        x_out = nc.dram_tensor("x_out", [D, T], F32, kind="ExternalOutput").ap()
        fa = {"x_outer_dram": x_outer, "x_out_dram": x_out}
    else:
        x_out = nc.dram_tensor("x_next", [D, T], F32, kind="ExternalOutput").ap()

    toks = emit_l1f_ffn(p, bufs, xm_tiles, dep, final_add=fa)
    if not final:
        st = [p.op("sync", lambda s, g=g: s.dma_start(out=x_out[128 * g:128 * (g + 1), :], in_=xm_tiles[g][:, 1:T + 1]),
                   waits=toks, dma=True) for g in range(NT)]
        p.op("gpsimd", lambda g: g.engine_nop(), waits=st)
    else:
        p.op("gpsimd", lambda g: g.engine_nop(), waits=toks)
    p.finalize()
    return nc


def l1_host_consts():
    Dh, W = 64, 512
    half = np.arange(0, Dh, 2, dtype=np.float64)
    inv_freq = 1.0 / 10000.0 ** (half / Dh)
    base = (half + 0.4 * Dh) / (1.4 * Dh)

    def fields(tpos):
        freqs = tpos[:, None] * inv_freq[None, :]
        power = (tpos - 16384 // 2) / float(W // 2)
        scale = base[None, :] ** power[:, None]
        cos = np.cos(freqs); sin = np.sin(freqs)
        cs = np.concatenate([cos * scale, cos * scale], axis=1)
        ss = np.concatenate([sin * scale, sin * scale], axis=1)
        ci = np.concatenate([cos / scale, cos / scale], axis=1)
        si = np.concatenate([sin / scale, sin / scale], axis=1)
        f = lambda a: np.ascontiguousarray(np.tile(a.T, (2, 1)).astype(np.float32))
        return f(cs), f(ss), f(ci), f(si)

    selh = np.zeros((128, 32), np.float32)
    for g in range(4):
        for r in range(128):
            selh[r, 8 * g + 2 * g + (1 if r >= 64 else 0)] = 1.0
    P64 = np.zeros((64, 64), np.float32)
    for r in range(32):
        P64[r, r + 32] = -1.0
    for r in range(32, 64):
        P64[r, r - 32] = 1.0
    permT = np.zeros((128, 128), np.float32)
    permT[0:64, 0:64] = P64.T
    permT[64:128, 64:128] = P64.T
    m0 = np.where(np.arange(128)[None, :] >= np.arange(128)[:, None], 0.0, -3e38).astype(np.float32)
    m4 = np.where(np.arange(128)[None, :] <= np.arange(128)[:, None], 0.0, -3e38).astype(np.float32)
    return fields, selh, permT, m0, m4




_PROG_CACHE = {}


def _get_prog(name, builder, *args):
    key = (name,) + args
    if key not in _PROG_CACHE:
        _PROG_CACHE[key] = builder(*args)
    return _PROG_CACHE[key]


def kernel(**inputs):
    NC = NCORES
    f32c = lambda a: np.ascontiguousarray(a, dtype=np.float32)
    x0 = f32c(inputs["L0_attn_x"][0])
    ident = np.eye(128, dtype=np.float32)
    ones_row = np.ones((1, 128), np.float32)
    ones_col = np.ones((128, 1), np.float32)
    base = {"ones_row": ones_row, "ones_col": ones_col}

    wqkv0 = f32c(inputs["L0_attn_W_qkv"])
    wout0 = f32c(inputs["L0_attn_W_out"])
    proj = f32c(inputs["L0_attn_proj"])
    pmT = np.ascontiguousarray(proj.transpose(0, 2, 1).reshape(1024, 128))
    pmP = np.zeros((128, 2048), np.float32)
    for pr in range(8):
        pmP[0:64, 256 * pr:256 * pr + 128] = pmT[128 * pr:128 * pr + 64]
        pmP[64:128, 256 * pr + 128:256 * pr + 256] = pmT[128 * pr + 64:128 * pr + 128]
    identA, seld = perf_host_consts()
    pbase = dict(base, pmT=pmT, pmTpair=pmP, seld=seld, ident=ident)

    nc1 = _get_prog("a1", build_stage_a1, ("qkv", "sq", "diag", "v", "kstab", "qstab"))
    maps = [dict(pbase, x_in=np.ascontiguousarray(x0[2048 * c:2048 * (c + 1)].T), wqkv=wqkv0)
            for c in range(NC)]
    r1 = run_bass_kernel_spmd(nc1, maps, core_ids=list(range(NC))).results

    nc2 = _get_prog("a2", build_stage_a2)
    skall = np.stack([np.asarray(r1[c]["sk_out"])[:, 0] for c in range(NC)], axis=1).astype(np.float32)
    maps = [dict(pbase, k_in=np.asarray(r1[c]["k_out"]), v_in=np.asarray(r1[c]["v_out"]),
                 dk_in=np.asarray(r1[c]["dk_out"]), skall=skall) for c in range(NC)]
    r2 = run_bass_kernel_spmd(nc2, maps, core_ids=list(range(NC))).results

    nc3 = _get_prog("a3", build_stage_a3)
    kvall = np.ascontiguousarray(np.concatenate([np.asarray(r2[c]["kv_out"]) for c in range(NC)], axis=0))
    maps = [dict(pbase, x_in=np.ascontiguousarray(x0[2048 * c:2048 * (c + 1)].T),
                 q_in=np.asarray(r1[c]["q_out"]), dq_in=np.asarray(r1[c]["dq_out"]),
                 sq_in=np.asarray(r1[c]["sq_out"]), kvall=kvall, wout=wout0) for c in range(NC)]
    r3 = run_bass_kernel_spmd(nc3, maps, core_ids=list(range(NC))).results
    x1 = np.concatenate([np.asarray(r["x_out"]).T for r in r3], axis=0)

    k0 = np.ascontiguousarray(inputs["L0_ffn_k0"]).view(np.uint8).reshape(2048, 4096)
    k1w = np.ascontiguousarray(inputs["L0_ffn_k1"]).view(np.uint8).reshape(4096, 4096)
    k2w = np.ascontiguousarray(inputs["L0_ffn_k2"]).view(np.uint8).reshape(4096, 1024)
    ncb = _get_prog("b", build_stage_b)
    maps = [dict(base, x_in=np.ascontiguousarray(x1[2048 * c:2048 * (c + 1)].T),
                 k0a=k0[:1024], k0b=k0[1024:], k1=k1w, k2=k2w,
                 s0=np.asarray(inputs["L0_ffn_s0"], np.float32).reshape(1, 1),
                 s1=np.asarray(inputs["L0_ffn_s1"], np.float32).reshape(1, 1),
                 s2=np.asarray(inputs["L0_ffn_s2"], np.float32).reshape(1, 1)) for c in range(NC)]
    rb = run_bass_kernel_spmd(ncb, maps, core_ids=list(range(NC))).results
    x2 = np.concatenate([np.asarray(r["x_out"]).T for r in rb], axis=0)

    def colpack(v):
        return np.ascontiguousarray(np.asarray(v, np.float32).reshape(8, 128).T)
    rcols = np.concatenate([
        colpack(inputs["L1_attn_time_mix_key"].ravel()),
        colpack(inputs["L1_attn_time_mix_value"].ravel()),
        colpack(inputs["L1_attn_time_mix_receptance"].ravel()),
        colpack(inputs["L1_attn_bo"].ravel()),
        colpack(inputs["L1_attn_time_first"].ravel()),
        colpack(inputs["L1_attn_time_decay"].ravel()),
    ], axis=1)
    ncc = _get_prog("c", build_stage_c)
    maps = []
    for c in range(NC):
        sl = slice(2048 * c, 2048 * (c + 1))
        halo = np.zeros((1024, 257), np.float32) if c == 0 else np.ascontiguousarray(x2[2048 * c - 257:2048 * c].T)
        cm = np.zeros((128, 1), np.float32) if c == 0 else np.ones((128, 1), np.float32)
        maps.append(dict(base, x_in=np.ascontiguousarray(x2[sl].T), halo_x=halo, cmask=cm,
                         wkT=np.ascontiguousarray(f32c(inputs["L1_attn_Wk"]).T),
                         wvT=np.ascontiguousarray(f32c(inputs["L1_attn_Wv"]).T),
                         wrT=np.ascontiguousarray(f32c(inputs["L1_attn_Wr"]).T),
                         woT=np.ascontiguousarray(f32c(inputs["L1_attn_Wo"]).T),
                         rwkv_cols=rcols))
    rc = run_bass_kernel_spmd(ncc, maps, core_ids=list(range(NC))).results
    x3 = np.concatenate([np.asarray(r["x_out"]).T for r in rc], axis=0)

    fields, selh, permT, m0, m4 = l1_host_consts()
    dbase = dict(base, selh2=selh, permT=permT, ident=ident)
    rope_per_core = []
    for c in range(NC):
        tq = np.arange(2048 * c, 2048 * (c + 1)).astype(np.float64)
        cssq, sssq, csik, ssik = fields(tq)
        rope_per_core.append({"cssq": cssq, "sssq": sssq, "csik": csik, "ssik": ssik})
    browc, edgec = [], []
    for c in range(NC):
        br = np.zeros((1, 2560), np.float32)
        et = np.zeros((128, 4), np.float32)
        if c == 0:
            for qt in range(4):
                ncut = 512 - 128 * qt
                br[0, 640 * qt:640 * qt + ncut] = -3e38
            for j in range(4):
                et[:512 - 128 * j, j] = -3e38
        browc.append(br)
        edgec.append(et)

    def layer_consts(l):
        qs2 = np.concatenate([f32c(inputs["L1_ffn_q_scale"][l])] * 2).reshape(128, 1)
        ks2 = np.concatenate([f32c(inputs["L1_ffn_k_scale"][l])] * 2).reshape(128, 1)
        w1 = f32c(inputs["L1_ffn_w1"][l]); b1 = f32c(inputs["L1_ffn_b1"][l])
        w2 = f32c(inputs["L1_ffn_w2"][l]); b2 = f32c(inputs["L1_ffn_b2"][l])
        w1p = np.zeros((1024, 5632), np.float32)
        w1p[:, :2730] = w1[:, :2730]; w1p[:, 2816:2816 + 2730] = w1[:, 2730:]
        w2p = np.zeros((2816, 1024), np.float32)
        w2p[:2730] = w2
        b1p = np.zeros(5632, np.float32)
        b1p[:2730] = b1[:2730]; b1p[2816:2816 + 2730] = b1[2730:]
        bcols = np.zeros((128, 52), np.float32)
        bcols[:, 0:44] = b1p.reshape(44, 128).T
        bcols[:, 44:52] = b2.reshape(8, 128).T
        lncols = np.zeros((128, 16), np.float32)
        lncols[:, 0:8] = f32c(inputs["L1_ffn_ln_g"][l]).reshape(8, 128).T
        lncols[:, 8:16] = f32c(inputs["L1_ffn_ln_b"][l]).reshape(8, 128).T
        return dict(qs2=np.ascontiguousarray(qs2), ks2=np.ascontiguousarray(ks2),
                    wqkv=f32c(inputs["L1_ffn_wqkv"][l]), wo=f32c(inputs["L1_ffn_wo"][l]),
                    w1p=w1p, w2p=w2p, bcols=bcols, lncols=lncols)

    x_outer = x3
    xin = None
    for l in range(2):
        lc = layer_consts(l)
        final = (l == 1)
        if l == 0:
            ncq = _get_prog("dqkv1", build_stage_d_qkv, True)
            maps = []
            for c in range(NC):
                sl = slice(2048 * c, 2048 * (c + 1))
                halo = x_outer[2048 * c - 1] if c else np.zeros(1024, np.float32)
                maps.append(dict(dbase, **rope_per_core[c],
                                 x_outer=np.ascontiguousarray(x_outer[sl].T),
                                 halo_row=np.ascontiguousarray(halo.reshape(1, 1024)),
                                 wqkv=lc["wqkv"], qs2=lc["qs2"], ks2=lc["ks2"]))
        else:
            ncq = _get_prog("dqkv0", build_stage_d_qkv, False)
            maps = []
            for c in range(NC):
                sl = slice(2048 * c, 2048 * (c + 1))
                halo = xin[2048 * c - 1] if c else np.zeros(1024, np.float32)
                xi = np.concatenate([halo[:, None], xin[sl].T], axis=1)
                maps.append(dict(dbase, **rope_per_core[c], xin=np.ascontiguousarray(xi),
                                 wqkv=lc["wqkv"], qs2=lc["qs2"], ks2=lc["ks2"]))
        ra = run_bass_kernel_spmd(ncq, maps, core_ids=list(range(NC))).results
        if l == 0:
            x_inner = np.concatenate([np.asarray(r["ln4_out"]).T for r in ra], axis=0)
        else:
            x_inner = xin
        ncat = _get_prog("dattn", build_stage_d_attn)
        maps = []
        for c in range(NC):
            sl = slice(2048 * c, 2048 * (c + 1))
            kh = np.asarray(ra[c - 1]["k_out"])[:, 1536:2048] if c else np.zeros((512, 512), ml_dtypes.bfloat16)
            vh = np.asarray(ra[c - 1]["v_out"])[:, 12 * 512:] if c else np.zeros((128, 2048), ml_dtypes.bfloat16)
            maps.append(dict(base, m0=m0, m4=m4, ident=ident, biasrow=browc[c], edgeT=edgec[c],
                             wo=lc["wo"], q_in=np.asarray(ra[c]["q_out"]), k_in=np.asarray(ra[c]["k_out"]),
                             kh_in=np.ascontiguousarray(kh), v_in=np.asarray(ra[c]["v_out"]),
                             vh_in=np.ascontiguousarray(vh),
                             x_inner=np.ascontiguousarray(x_inner[sl].T)))
        rb2 = run_bass_kernel_spmd(ncat, maps, core_ids=list(range(NC))).results
        x_mid = np.concatenate([np.asarray(r["x_mid"]).T for r in rb2], axis=0)
        ncf = _get_prog("dffn", build_stage_d_ffn, final)
        maps = []
        for c in range(NC):
            sl = slice(2048 * c, 2048 * (c + 1))
            halo = x_mid[2048 * c - 1] if c else np.zeros(1024, np.float32)
            mm = dict(base, w1p=lc["w1p"], w2p=lc["w2p"], bcols=lc["bcols"], lncols=lc["lncols"],
                      x_mid=np.ascontiguousarray(x_mid[sl].T),
                      halo_col=np.ascontiguousarray(halo.reshape(1024, 1)))
            if final:
                mm["x_outer"] = np.ascontiguousarray(x_outer[sl].T)
            maps.append(mm)
        rc2 = run_bass_kernel_spmd(ncf, maps, core_ids=list(range(NC))).results
        key = "x_out" if final else "x_next"
        xin = np.concatenate([np.asarray(r[key]).T for r in rc2], axis=0)
    return xin.reshape(1, 16384, 1024).astype(np.float32)

